# version 19
# baseline (speedup 1.0000x reference)
import contextlib
import numpy as np
import ml_dtypes

import concourse.bass as bass
import concourse.mybir as mybir
from concourse.bass_utils import run_bass_kernel_spmd

F32 = mybir.dt.float32
BF16 = mybir.dt.bfloat16
I32 = mybir.dt.int32
AF = mybir.ActivationFunctionType
ALU = mybir.AluOpType
AX = mybir.AxisListType

NCORES = 8
D = 1024
NG = D // 128
TOWN = 2048
LB = 768
TPRE = LB + TOWN
IFF = 2816
ICM = 4096
BIG = float(np.float32(2.0 ** 100))
MID = [0.25, 0.75, 1.25, 1.75, 2.5, 3.5, 5.0]
NEG = -1e38


ENGINES = ("sync", "vector", "scalar", "tensor", "gpsimd")


class Step:
    __slots__ = ("engine", "fn", "name", "dep_idx", "need_dma", "idx", "inc",
                 "self_dep", "n_dmas", "dma_slot", "dma_prior", "dma_after")

    def __init__(self, engine, fn, name, dep_idx=None, need_dma=None, inc=1,
                 self_dep=True, n_dmas=0):
        self.engine = engine
        self.fn = fn
        self.name = name
        self.dep_idx = dep_idx
        self.need_dma = need_dma or []
        self.idx = None
        self.inc = inc
        self.self_dep = self_dep
        self.n_dmas = n_dmas


N_DSEM = {"sync": 6, "gpsimd": 4}


class Plan:
    def __init__(self):
        self.steps = []
        self.dma_tags = {}
        self.dma_count = {"sync": 0, "gpsimd": 0}
        self.dma_rr = {"sync": 0, "gpsimd": 0}
        self.dma_slot_count = {e: [0] * N_DSEM[e] for e in N_DSEM}
        self.last_store = None
        self.gser = [0]

    def step(self, engine, name, fn, dep_idx=None, need_dma=None, inc=1,
             self_dep=True, n_dmas=0):
        s = Step(engine, fn, name, dep_idx, need_dma, inc, self_dep, n_dmas)
        s.idx = len(self.steps)
        self.steps.append(s)
        return s

    def dma(self, engine, name, fn, n_dmas, dep_idx=None, tag=None, need_dma=None):
        s = self.step(engine, name, fn, dep_idx=dep_idx, need_dma=need_dma, inc=0,
                      n_dmas=n_dmas)
        self.dma_count[engine] += n_dmas
        slot = self.dma_rr[engine]
        self.dma_rr[engine] = (slot + 1) % N_DSEM[engine]
        s.dma_slot = slot
        s.dma_prior = self.dma_slot_count[engine][slot]
        self.dma_slot_count[engine][slot] += n_dmas
        s.dma_after = self.dma_slot_count[engine][slot]
        if tag:
            self.dma_tags[tag] = s
            if engine == "gpsimd":
                self.last_store = tag
        return s


class Ctx:

    def __init__(self, nc, sems):
        self.nc = nc
        self.sems = sems
        self.buf = {}
        self.dram = {}
        self.addr_of = {}
        self.cur_eng = None
        self.drain_count = {e: 0 for e in ENGINES}

    def dma_sem(self):
        return self.sems[f"d_{self.cur_eng_name}_{self.cur_dma_slot}"]

    def seq(self, eng, inst):
        e = self.cur_eng_name
        self.drain_count[e] += 1
        sem = self.sems["dr_" + e]
        inst.then_inc(sem, 1)
        eng.wait_ge(sem, self.drain_count[e])
        return inst


def emit_program(nc, plan, ctx, deps=None, capture=None):
    n = len(plan.steps)
    cum = {e: [0] * (n + 1) for e in ENGINES}
    for i, s in enumerate(plan.steps):
        for e in ENGINES:
            cum[e][i + 1] = cum[e][i] + (s.inc if s.engine == e else 0)

    sem_of = {"sync": "psync", "vector": "pv", "scalar": "ps",
              "tensor": "pt", "gpsimd": "pg"}

    slot_before = []
    running = {}
    for s in plan.steps:
        slot_before.append(dict(running))
        if s.inc == 0:
            running[f"d_{s.engine}_{s.dma_slot}"] = s.dma_after * 16

    class _Rec:
        __slots__ = ("_e", "_sink", "_ann")

        def __init__(self, e, sink, ann=None):
            self._e = e
            self._sink = sink
            self._ann = ann

        def __getattr__(self, nm):
            a = getattr(self._e, nm)
            if callable(a):
                sink = self._sink
                annv = self._ann

                def wrap(*args, **kw):
                    r = a(*args, **kw)
                    if annv is not None:
                        try:
                            r.annotate(annv)
                        except Exception:
                            pass
                    if sink is not None:
                        try:
                            mi = r.ins
                            for ap in mi.ins:
                                if hasattr(ap, "memref"):
                                    sink[0].add(ap.memref)
                            for ap in mi.outs:
                                if hasattr(ap, "memref"):
                                    sink[1].add(ap.memref)
                        except AttributeError:
                            pass
                    return r
                return wrap
            return a

    def engine_body(ename):
        def body(eng):
            ctx.cur_eng_name = ename
            last_wait = {}
            def wait_sem(semname, want):
                if want > last_wait.get(semname, 0):
                    eng.wait_ge(ctx.sems[semname], want)
                    last_wait[semname] = want

            for s in plan.steps:
                if s.engine != ename:
                    continue
                if s.inc == 0:
                    ctx.cur_dma_slot = s.dma_slot
                    wait_sem(f"d_{ename}_{s.dma_slot}", s.dma_prior * 16)
                if deps is not None:
                    for semname, want in deps.get(s.idx, {}).items():
                        wait_sem(semname, want)
                else:
                    di = s.dep_idx if s.dep_idx is not None else s.idx
                    for e in ENGINES:
                        if e == ename:
                            continue
                        wait_sem(sem_of[e], cum[e][di])
                    if s.self_dep:
                        wait_sem(sem_of[ename], cum[ename][s.idx])
                    if di == s.idx:
                        for semname, tot in slot_before[s.idx].items():
                            wait_sem(semname, tot)
                if deps is None:
                    for tg in s.need_dma:
                        st2 = plan.dma_tags[tg]
                        wait_sem(f"d_{st2.engine}_{st2.dma_slot}", st2.dma_after * 16)
                if capture is not None:
                    sink = (set(), set())
                    last = s.fn(_Rec(eng, sink), ctx)
                    capture[s.idx] = sink
                else:
                    last = s.fn(_Rec(eng, None, s.name), ctx)
                assert last is not None, f"step {s.name} returned no instruction"
                if s.inc:
                    last.then_inc(ctx.sems[sem_of[ename]], s.inc)
        return body

    with nc.Block() as blk:
        blk.sync(engine_body("sync"))
        blk.vector(engine_body("vector"))
        blk.scalar(engine_body("scalar"))
        blk.tensor(engine_body("tensor"))
        blk.gpsimd(engine_body("gpsimd"))


def compute_deps(plan, capture, addr_of=None):
    addr_of = addr_of or {}
    import bisect
    bounds = {}
    for sp, lo, hi in addr_of.values():
        bounds.setdefault(sp, set()).update((lo, hi))
    bounds = {sp: sorted(v) for sp, v in bounds.items()}
    keycache = {}

    def keys_of(nm):
        r = keycache.get(nm)
        if r is None:
            a = addr_of.get(nm)
            if a is None:
                r = (("d", nm),)
            else:
                sp, lo, hi = a
                bl = bounds[sp]
                i0 = bisect.bisect_left(bl, lo)
                i1 = bisect.bisect_left(bl, hi)
                r = tuple((sp, i) for i in range(i0, i1))
            keycache[nm] = r
        return r
    n = len(plan.steps)
    cum = {e: [0] * (n + 1) for e in ENGINES}
    for i, s in enumerate(plan.steps):
        for e in ENGINES:
            cum[e][i + 1] = cum[e][i] + (s.inc if s.engine == e else 0)
    sem_of = {"sync": "psync", "vector": "pv", "scalar": "ps",
              "tensor": "pt", "gpsimd": "pg"}

    last_writer = {}
    readers = {}
    deps = {}
    for s in plan.steps:
        i = s.idx
        reads0, writes0 = capture[i]
        reads = set()
        for nm in reads0:
            reads.update(keys_of(nm))
        writes = set()
        for nm in writes0:
            writes.update(keys_of(nm))
        dep_steps = set()
        for nm in reads:
            if nm in last_writer:
                dep_steps.add(last_writer[nm])
        for nm in writes:
            if nm in last_writer:
                dep_steps.add(last_writer[nm])
            for r in readers.get(nm, ()):
                dep_steps.add(r)
        dep_steps.discard(i)
        dv = {}
        for j in dep_steps:
            sj = plan.steps[j]
            if sj.inc == 0:
                key = f"d_{sj.engine}_{sj.dma_slot}"
                want = sj.dma_after * 16
            else:
                key = sem_of[sj.engine]
                want = cum[sj.engine][j + 1]
            if dv.get(key, 0) < want:
                dv[key] = want
        deps[i] = dv
        for nm in writes:
            last_writer[nm] = i
            readers[nm] = []
        for nm in reads:
            if nm not in writes:
                readers.setdefault(nm, []).append(i)
    return deps



def emit_fm_quant(plan, name, srcs, his, los, col0, width, npart_grp,
                  scratch_sets, psum, consts, dcol0=None, defer=False):
    if dcol0 is None:
        dcol0 = col0
    if isinstance(scratch_sets, dict):
        scratch_sets = [scratch_sets]
    if isinstance(psum, dict):
        psum_sets = [psum]
    else:
        psum_sets = psum
    gser = plan.gser
    stage_list = []

    def seg_steps(s0, sw):
        scratch = scd = scratch_sets[gser[0] % len(scratch_sets)]
        psum = psum_sets[gser[0] % len(psum_sets)]
        nsub = sw // 128
        nb = 8 * npart_grp

        def scale_sub(sub):
            for g in range(npart_grp):
                def tr_fn(eng, ctx, g=g, sub=sub, s0=s0):
                    return eng.transpose(
                        ctx.buf[psum["tr"]][:, 0:128],
                        srcs[g][:, col0 + s0 + sub * 128: col0 + s0 + (sub + 1) * 128],
                        ctx.buf[consts["idn"]][:, :])
                plan.step("tensor", f"{name}.tr", tr_fn)

                def red_fn(eng, ctx, g=g, sub=sub):
                    ap = ctx.buf[psum["tr"]][:, 0:128].rearrange("p (b k) -> p b k", k=16)
                    return eng.tensor_reduce(
                        ctx.buf[scratch["bm"]][:, 8 * g: 8 * g + 8], ap,
                        AX.X, ALU.max, apply_absolute_value=True)
                plan.step("vector", f"{name}.bm", red_fn)

            def rs_fn(eng, ctx):
                ctx.seq(eng, eng.tensor_scalar(ctx.buf[scratch["ss"]][:, 0:nb],
                                  ctx.buf[scratch["bm"]][:, 0:nb],
                                  1.0 / 6.0, 1e-12, ALU.mult, ALU.max))
                return eng.reciprocal(ctx.buf[scratch["rs"]][:, 0:nb],
                                      ctx.buf[scratch["ss"]][:, 0:nb])
            plan.step("vector", f"{name}.rs", rs_fn)

            def rsT_fn(eng, ctx, sub=sub):
                return eng.transpose(
                    ctx.buf[psum["rsT"]][0:nb, 0:128],
                    ctx.buf[scratch["rs"]][:, 0:nb],
                    ctx.buf[consts["idn"]][:, :])
            plan.step("tensor", f"{name}.rsT", rsT_fn)

            def rsc_fn(eng, ctx, sub=sub):
                return eng.tensor_copy(
                    ctx.buf[scratch["rsp"]][0:nb, sub * 128:(sub + 1) * 128],
                    ctx.buf[psum["rsT"]][0:nb, 0:128])
            plan.step("vector", f"{name}.rsc", rsc_fn)

        for sub in range(nsub):
            stage_list.append(lambda sub=sub: scale_sub(sub))

        def stage_exp(g, scratch, psum):
            def exp_fn(eng, ctx, g=g, scd=scd, psum=psum):
                return eng.matmul(ctx.buf[psum["exp"]][:, 0:sw],
                                  ctx.buf[consts["e64"]][0:nb, 128 * g:128 * (g + 1)],
                                  ctx.buf[scd["rsp"]][0:nb, 0:sw],
                                  start=True, stop=True)
            plan.step("tensor", f"{name}.exp{g}", exp_fn)

        def stage_abs(g, scratch, psum):
            src_ap = lambda ctx, g=g: srcs[g][:, col0 + s0: col0 + s0 + sw]
            def abs_fn(eng, ctx, scratch=scratch, src_ap=src_ap):
                return eng.activation(ctx.buf[scratch["xabs"]][:, 0:sw], src_ap(ctx), AF.Abs)
            plan.step("scalar", f"{name}.abs{g}", abs_fn)

        def stage_y(g, scratch, psum):
            def y_fn(eng, ctx, scratch=scratch, psum=psum):
                return eng.tensor_tensor(ctx.buf[scratch["xabs"]][:, 0:sw],
                                         ctx.buf[scratch["xabs"]][:, 0:sw],
                                         ctx.buf[psum["exp"]][:, 0:sw], ALU.mult)
            plan.step("vector", f"{name}.y{g}", y_fn)

        def stage_sfld(g, scratch, psum):
            def sf_fn(eng, ctx, psum=psum):
                return eng.reciprocal(ctx.buf[psum["exp2"]][:, 0:sw],
                                      ctx.buf[psum["exp"]][:, 0:sw])
            plan.step("vector", f"{name}.sf{g}", sf_fn)

        def mkstep(j, dst, g, scratch):
            def f(eng, ctx, scratch=scratch):
                return eng.activation(ctx.buf[scratch[dst]][:, 0:sw],
                                      ctx.buf[scratch["xabs"]][:, 0:sw], AF.Sigmoid,
                                      bias=ctx.buf[consts["stepb"]][:, j:j + 1],
                                      scale=BIG)
            return plan.step("scalar", f"{name}.s{j}_{g}", f)

        def stage_s12(g, scratch, psum):
            mkstep(0, "c1", g, scratch)
            mkstep(1, "c2", g, scratch)

        def stage_gpa1(g, scratch, psum):
            def gpa1(eng, ctx, scratch=scratch):
                return eng.tensor_add(ctx.buf[scratch["accA"]][:, 0:sw],
                                      ctx.buf[scratch["c1"]][:, 0:sw],
                                      ctx.buf[scratch["c2"]][:, 0:sw])
            plan.step("gpsimd", f"{name}.gA1_{g}", gpa1)

        def stage_s34(g, scratch, psum):
            mkstep(2, "c1", g, scratch)
            mkstep(3, "c2", g, scratch)

        def stage_gpa2(g, scratch, psum):
            def gpa2(eng, ctx, scratch=scratch):
                return eng.tensor_add(ctx.buf[scratch["c2"]][:, 0:sw],
                                      ctx.buf[scratch["c1"]][:, 0:sw],
                                      ctx.buf[scratch["c2"]][:, 0:sw])
            plan.step("gpsimd", f"{name}.gA2_{g}", gpa2)

        def stage_s567(g, scratch, psum):
            mkstep(4, "c1", g, scratch)
            mkstep(6, "c7", g, scratch)
            mkstep(5, "xabs", g, scratch)

        def stage_comb(g, scratch, psum):
            def comb_fn(eng, ctx, scratch=scratch):
                ctx.seq(eng, eng.tensor_add(ctx.buf[scratch["accA"]][:, 0:sw],
                                            ctx.buf[scratch["accA"]][:, 0:sw],
                                            ctx.buf[scratch["c2"]][:, 0:sw]))
                ctx.seq(eng, eng.tensor_add(ctx.buf[scratch["c1"]][:, 0:sw],
                                            ctx.buf[scratch["c1"]][:, 0:sw],
                                            ctx.buf[scratch["xabs"]][:, 0:sw]))
                ctx.seq(eng, eng.scalar_tensor_tensor(ctx.buf[scratch["c2"]][:, 0:sw],
                                         ctx.buf[scratch["c7"]][:, 0:sw], 2.0,
                                         ctx.buf[scratch["c1"]][:, 0:sw],
                                         ALU.mult, ALU.add))
                return eng.scalar_tensor_tensor(ctx.buf[scratch["qabs"]][:, 0:sw],
                                                ctx.buf[scratch["accA"]][:, 0:sw], 0.5,
                                                ctx.buf[scratch["c2"]][:, 0:sw],
                                                ALU.mult, ALU.add)
            plan.step("vector", f"{name}.comb{g}", comb_fn)

        def stage_sgn(g, scratch, psum):
            src_ap = lambda ctx, g=g: srcs[g][:, col0 + s0: col0 + s0 + sw]
            def sgn_fn(eng, ctx, scratch=scratch, src_ap=src_ap):
                return eng.activation(ctx.buf[scratch["sg"]][:, 0:sw], src_ap(ctx), AF.Sign)
            plan.step("scalar", f"{name}.sgn{g}", sgn_fn)

        def stage_q(g, scratch, psum):
            def q_fn(eng, ctx, g=g, scratch=scratch, psum=psum):
                ctx.seq(eng, eng.tensor_mul(ctx.buf[scratch["qabs"]][:, 0:sw],
                               ctx.buf[scratch["qabs"]][:, 0:sw],
                               ctx.buf[scratch["sg"]][:, 0:sw]))
                ctx.seq(eng, eng.tensor_tensor(ctx.buf[scratch["xabs"]][:, 0:sw],
                                  ctx.buf[scratch["qabs"]][:, 0:sw],
                                  ctx.buf[psum["exp2"]][:, 0:sw], ALU.mult))
                ctx.seq(eng, eng.tensor_copy(his[g][:, dcol0 + s0: dcol0 + s0 + sw],
                                ctx.buf[scratch["xabs"]][:, 0:sw]))
                return eng.tensor_tensor(los[g][:, dcol0 + s0: dcol0 + s0 + sw],
                                         ctx.buf[scratch["xabs"]][:, 0:sw],
                                         his[g][:, dcol0 + s0: dcol0 + s0 + sw],
                                         ALU.subtract)
            plan.step("vector", f"{name}.q{g}", q_fn)

        stages = [stage_exp, stage_abs, stage_sgn, stage_y, stage_sfld,
                  stage_s12, stage_gpa1, stage_s34, stage_gpa2, stage_s567,
                  stage_comb, stage_q]
        for g0 in range(0, npart_grp, 3):
            gs = [g for g in (g0, g0 + 1, g0 + 2) if g < npart_grp]
            assign = {}
            for g in gs:
                assign[g] = (scratch_sets[gser[0] % len(scratch_sets)],
                             psum_sets[gser[0] % len(psum_sets)])
                gser[0] += 1
            for stg in stages:
                def emit_stage(stg=stg, gs=gs, assign=assign):
                    for g in gs:
                        stg(g, assign[g][0], assign[g][1])
                stage_list.append(emit_stage)

    assert width <= 512
    seg_steps(0, width)
    if defer:
        return stage_list
    for f in stage_list:
        f()
    return None



def _np_fp4(x):
    GRID = np.array([0.0, 0.5, 1.0, 1.5, 2.0, 3.0, 4.0, 6.0], np.float32)
    sh = x.shape
    xb = x.reshape(sh[:-1] + (sh[-1] // 16, 16))
    sc = np.maximum(np.abs(xb).max(-1, keepdims=True) / 6.0, np.float32(1e-12)).astype(np.float32)
    xn = (xb / sc).astype(np.float32)
    idx = np.searchsorted(MID, np.abs(xn))
    q = GRID[idx] * np.sign(xn) * sc
    return q.reshape(sh).astype(np.float32)


def build_quant_test(wT):
    K, M = wT.shape
    ng = K // 128
    nc = bass.Bass("TRN2", target_bir_lowering=False)
    win = nc.dram_tensor("w", [K, M], F32, kind="ExternalInput").ap()
    idn_in = nc.dram_tensor("d_idn", [128, 128], F32, kind="ExternalInput").ap()
    e8_in = nc.dram_tensor("d_e64", [64, 1024], F32, kind="ExternalInput").ap()
    sb_in = nc.dram_tensor("d_stepb", [128, 7], F32, kind="ExternalInput").ap()
    qout = nc.dram_tensor("qo", [K, M], F32, kind="ExternalOutput").ap()

    plan = Plan()
    with contextlib.ExitStack() as st:
        sems = {}
        semnames = ["psync", "pv", "ps", "pt", "pg",
                    "dr_sync", "dr_vector", "dr_scalar", "dr_tensor", "dr_gpsimd"]
        semnames += [f"d_sync_{i}" for i in range(N_DSEM["sync"])]
        semnames += [f"d_gpsimd_{i}" for i in range(N_DSEM["gpsimd"])]
        for sname in semnames:
            sems[sname] = st.enter_context(nc.semaphore(sname))
        ctx = Ctx(nc, sems)

        def sb(nm, shape, dt=F32):
            ctx.buf[nm] = st.enter_context(nc.sbuf_tensor(nm, shape, dt))
            return nm

        def ps(nm, shape):
            ctx.buf[nm] = st.enter_context(nc.psum_tensor(nm, shape, F32))
            return nm

        wt = [sb(f"wt{g}", [128, M]) for g in range(ng)]
        hi = [sb(f"hi{g}", [128, M], BF16) for g in range(ng)]
        lo = [sb(f"lo{g}", [128, M], BF16) for g in range(ng)]
        wt_h = [ctx.buf[f"wt{g}"] for g in range(ng)]
        hi_h = [ctx.buf[f"hi{g}"] for g in range(ng)]
        lo_h = [ctx.buf[f"lo{g}"] for g in range(ng)]
        qf = [sb(f"qf{g}", [128, M]) for g in range(ng)]
        qsc2 = []
        for si in range(3):
            d = {}
            key = f"xabs.{si}"
            d["xabs"] = key; sb(key, [128, 512])
            for nm in ["c1", "c2", "accA", "c7", "qabs", "sg"]:
                key = f"{nm}.{si}"
                d[nm] = key; sb(key, [128, 512], BF16)
            for nm, shp in [("bm", [128, 64]), ("rs", [128, 64]),
                            ("ss", [128, 64]), ("rsp", [64, 512])]:
                key = f"{nm}.{si}"
                d[nm] = key; sb(key, shp)
            qsc2.append(d)
        sb("idn", [128, 128])
        sb("e64", [64, 1024])
        sb("stepb", [128, 7])
        for pnm in ["p_tr", "p_rsT", "p_exp", "p_exp2", "p_a", "p_b", "p_c", "p_d"]:
            ps(pnm, [128, 512])

        def ld(eng, ctx):
            for g in range(ng):
                eng.dma_start(out=ctx.buf[f"wt{g}"][:, :], in_=win[g * 128:(g + 1) * 128, :]).then_inc(ctx.dma_sem(), 16)
            eng.dma_start(out=ctx.buf["idn"][:, :], in_=idn_in).then_inc(ctx.dma_sem(), 16)
            eng.dma_start(out=ctx.buf["e64"][:, :], in_=e8_in).then_inc(ctx.dma_sem(), 16)
            i = eng.dma_start(out=ctx.buf["stepb"][:, :], in_=sb_in)
            i.then_inc(ctx.dma_sem(), 16)
            return i
        plan.dma("sync", "load", ld, ng + 3, tag="in")

        def nopv(eng, ctx):
            return eng.tensor_copy(ctx.buf["xabs.0"][:, 0:1], ctx.buf["wt0"][:, 0:1])
        plan.step("vector", "gate", nopv, need_dma=["in"])

        psets = [{"tr": "p_tr", "rsT": "p_rsT", "exp": "p_exp", "exp2": "p_exp2"},
                 {"tr": "p_tr", "rsT": "p_rsT", "exp": "p_a", "exp2": "p_b"},
                 {"tr": "p_tr", "rsT": "p_rsT", "exp": "p_c", "exp2": "p_d"}]
        consts = {"idn": "idn", "e64": "e64", "stepb": "stepb"}
        for c0 in range(0, M, 512):
            emit_fm_quant(plan, f"q{c0}", wt_h, hi_h, lo_h, c0, min(512, M - c0), ng,
                          qsc2, psets, consts)

        for g in range(ng):
            def deq(eng, ctx, g=g):
                return eng.tensor_add(ctx.buf[f"qf{g}"][:, :], ctx.buf[f"hi{g}"][:, :], ctx.buf[f"lo{g}"][:, :])
            plan.step("vector", f"deq{g}", deq)

        def store(eng, ctx):
            for g in range(ng):
                i = eng.dma_start(out=qout[g * 128:(g + 1) * 128, :], in_=ctx.buf[f"qf{g}"][:, :])
                i.then_inc(ctx.dma_sem(), 16)
            return i
        plan.dma("gpsimd", "store", store, ng, tag="out")

        def fin(eng, ctx):
            for e in N_DSEM:
                for k in range(N_DSEM[e]):
                    tot = plan.dma_slot_count[e][k] * 16
                    if tot:
                        eng.wait_ge(ctx.sems[f"d_{e}_{k}"], tot)
            return eng.tensor_copy(ctx.buf["xabs.0"][:, 0:1], ctx.buf["xabs.0"][:, 0:1])
        plan.step("vector", "fin", fin)

        emit_program(nc, plan, ctx)
    return nc


def host_consts():
    idn = np.eye(128, dtype=np.float32)
    e8 = np.zeros((64, 1024), np.float32)
    for g in range(8):
        for c in range(128):
            e8[8 * g + c // 16, 128 * g + c] = 1.0
    stepb = np.tile(np.float32([-BIG * np.float32(t) for t in MID]), (128, 1)).astype(np.float32)
    return idn, e8, stepb


def _selftest_quant():
    rng = np.random.default_rng(0)
    M = 512
    w = (rng.standard_normal((M, 1024)) * 0.02).astype(np.float32)
    wT = np.ascontiguousarray(w.T)
    nc = build_quant_test(wT)
    idn, e8, stepb = host_consts()
    res = run_bass_kernel_spmd(nc, [{"w": wT, "d_idn": idn, "d_e64": e8, "d_stepb": stepb}], core_ids=[0])
    got = res.results[0]["qo"]
    ref = _np_fp4(w).T
    d = np.abs(got - ref)
    rel = np.abs(got - ref).max() / np.abs(ref).max()
    nbad = (d > 1e-5 * np.maximum(np.abs(ref), 1e-3)).sum()
    print(f"quant selftest: max_abs={d.max():.3e} rel={rel:.3e} nbad={nbad} / {ref.size}")
    bad = np.argwhere(d > 1e-3)
    for i, j in bad[:5]:
        print("  at", i, j, "got", got[i, j], "ref", ref[i, j], "w", wT[i, j])



FFN1_TILES = [(0, 256)] + [(256 + 512 * i, 512) for i in range(5)]
ATTN_TILES = [(0, 256)] + [(256 + 512 * i, 512) for i in range(5)]
CM_TILES = [(256 + 512 * i, 512) for i in range(5)]
FFN2_TILES = [(768 + 512 * i, 512) for i in range(4)]
NM_FF = IFF // 128
NOG_CM = ICM // 128


def build_full(phases=("wprep", "ffn1", "attn", "cm", "ffn2"), dump=None,
               deps=None, capture=None):
    nc = bass.Bass("TRN2", target_bir_lowering=False)
    dr = {}

    def din(name, shape, dt=F32):
        dr[name] = nc.dram_tensor("i_" + name, shape, dt, kind="ExternalInput").ap()
        return dr[name]

    def dint(name, shape, dt=F32):
        dr[name] = nc.dram_tensor("t_" + name, shape, dt, kind="Internal").ap()
        return dr[name]

    x_in = din("x", [D, TPRE])
    kmask_in = din("kmask", [128, TPRE])
    for w, sh in [("wk", [D, D]), ("wv", [D, D]), ("wr", [D, D]), ("wo", [D, D]),
                  ("wck", [D, ICM]), ("wcv", [ICM, D]), ("wcr", [D, D]),
                  ("wg_t", [D, IFF]), ("wu_t", [D, IFF]), ("wd_t", [IFF, D])]:
        din(w, sh)
    for p in ["ln0g", "ln0b", "ln1g", "ln1b", "tmk", "tmv", "tmr", "cmk", "cmr",
              "td", "tf"]:
        din(p, [128, NG])
    din("idn", [128, 128])
    din("e64", [64, 1024])
    din("stepb", [128, 7])
    din("onescol", [128, 1])
    din("browones", [1, 128])
    din("epsc", [128, 1])
    din("h0m", [128, 1])
    out = nc.dram_tensor("out", [D, TOWN], F32, kind="ExternalOutput").ap()

    sA = dint("sA", [D, TPRE])
    sB = dint("sB", [D, TPRE])
    spills = {}
    for wname, n0, M in [("wk", D // 128, D), ("wv", D // 128, D), ("wr", D // 128, D),
                         ("wo", D // 128, D), ("wck", D // 128, ICM),
                         ("wcv", ICM // 128, D), ("wcr", D // 128, D),
                         ("qg", D // 128, IFF), ("qu", D // 128, IFF),
                         ("qd", IFF // 128, D)]:
        spills[wname] = (
            nc.dram_tensor("sp_" + wname + "_h", [n0, 128, M], BF16, kind="Internal").ap(),
            nc.dram_tensor("sp_" + wname + "_l", [n0, 128, M], BF16, kind="Internal").ap(),
        )

    plan = Plan()
    tagn = [0]

    def tag():
        tagn[0] += 1
        return f"g{tagn[0]}"

    with contextlib.ExitStack() as st:
        sems = {}
        semnames = ["psync", "pv", "ps", "pt", "pg",
                    "dr_sync", "dr_vector", "dr_scalar", "dr_tensor", "dr_gpsimd"]
        semnames += [f"d_sync_{i}" for i in range(N_DSEM["sync"])]
        semnames += [f"d_gpsimd_{i}" for i in range(N_DSEM["gpsimd"])]
        for sname in semnames:
            sems[sname] = st.enter_context(nc.semaphore(sname))
        ctx = Ctx(nc, sems)

        nser = [0]

        def sb(stk, nm, shape, dt=F32):
            nser[0] += 1
            h = stk.enter_context(nc.sbuf_tensor(f"{nm}__{nser[0]}", shape, dt))
            ctx.buf[nm] = h
            ml = nc.lookup_mloc(h)
            npart = max(1, shape[0])
            ctx.addr_of[h.name] = ("sb", ml.addr, ml.addr + ml.size() // npart)
            return h

        def ps(stk, nm, shape=None):
            h = stk.enter_context(nc.psum_tensor(nm, shape or [128, 512], F32))
            ctx.buf[nm] = h
            ml = nc.lookup_mloc(h)
            sh = shape or [128, 512]
            ctx.addr_of[h.name] = ("ps", ml.addr, ml.addr + ml.size() // max(1, sh[0]))
            return h

        P = st
        idn = sb(P, "idn", [128, 128]); e64 = sb(P, "e64", [64, 1024])
        stepb = sb(P, "stepb", [128, 7]); onescol = sb(P, "onescol", [128, 1])
        brow = sb(P, "browones", [1, 128]); epsc = sb(P, "epsc", [128, 1])
        h0m = sb(P, "h0m", [128, 1])
        par = {p: sb(P, "p_" + p, [128, NG]) for p in
               ["ln0g", "ln0b", "ln1g", "ln1b", "tmk", "tmv", "tmr", "cmk", "cmr",
                "td", "tf"]}
        lam = sb(P, "lam", [128, NG]); eu = sb(P, "eu", [128, NG])
        st_a = sb(P, "st_a", [128, NG]); st_b = sb(P, "st_b", [128, NG])
        s_prev = sb(P, "s_prev", [128, NG]); s_cur = sb(P, "s_cur", [128, NG])
        ns_cur = sb(P, "ns_cur", [128, NG]); resc = sb(P, "resc", [128, NG])
        _rw = {nm: sb(P, "row_" + nm, [1, 512]) for nm in
               ["m", "A", "al", "t0", "t1", "sd"]}
        rows = {"m": _rw["m"], "A": _rw["A"], "al": _rw["al"], "sd": _rw["sd"],
                "sum": _rw["t0"], "sumsq": _rw["t1"], "msq": _rw["t1"],
                "var": _rw["t1"], "t2": _rw["t0"]}
        qsc2 = []
        for si in range(3):
            d = {}
            key = f"xabs.{si}"
            d["xabs"] = key; sb(P, key, [128, 512])
            for nm in ["c1", "c2", "accA", "c7", "qabs", "sg"]:
                key = f"{nm}.{si}"
                d[nm] = key; sb(P, key, [128, 512], BF16)
            for nm, shp in [("bm", [128, 64]), ("rs", [128, 64]),
                            ("ss", [128, 64]), ("rsp", [64, 512])]:
                key = f"{nm}.{si}"
                d[nm] = key; sb(P, key, shp)
            qsc2.append(d)
        qsc = qsc2[0]
        sqs = sb(P, "sqs", [128, 512])
        consts = {"idn": "idn", "e64": "e64", "stepb": "stepb"}
        p_tr = ps(P, "p_tr"); p_rsT = ps(P, "p_rsT")
        p_exp = ps(P, "p_exp"); p_exp2 = ps(P, "p_exp2")
        p_a = ps(P, "p_a"); p_b = ps(P, "p_b")
        p_c = ps(P, "p_c"); p_d = ps(P, "p_d")
        qps = {"tr": "p_tr", "rsT": "p_rsT", "exp": "p_exp", "exp2": "p_exp2"}
        ogbank = [p_tr, p_rsT, p_exp, p_exp2, p_a, p_b, p_c, p_d]

        def ld_consts(eng, ctx):
            for nm, dst in [("idn", idn), ("e64", e64), ("stepb", stepb),
                            ("onescol", onescol), ("browones", brow), ("epsc", epsc),
                            ("h0m", h0m)]:
                eng.dma_start(out=dst[:, :], in_=dr[nm]).then_inc(ctx.dma_sem(), 16)
            last = None
            for p in par:
                last = eng.dma_start(out=par[p][:, :], in_=dr[p])
                last.then_inc(ctx.dma_sem(), 16)
            return last
        plan.dma("sync", "ldconsts", ld_consts, 7 + len(par), tag="consts")

        def derive(eng, ctx):
            ctx.seq(eng, eng.activation(lam[:, :], par["td"][:, :], AF.Exp))
            ctx.seq(eng, eng.activation(lam[:, :], lam[:, :], AF.Exp, scale=-1.0))
            return eng.activation(eu[:, :], par["tf"][:, :], AF.Exp)
        plan.step("scalar", "derive", derive, need_dma=["consts"])

        def init_state(eng, ctx):
            eng.memset(st_a[:, :], 0.0)
            eng.memset(st_b[:, :], 0.0)
            return eng.memset(s_prev[:, :], -1e30)
        plan.step("vector", "initstate", init_state)

        split_jobs = []
        if "wprep" in phases:
            def mk_split_job(wname, kg, cc0, cw):
                hi_d, lo_d = spills[wname]

                def emit(pidx):
                    t_in = tag()
                    def ldw(eng, ctx, wname=wname, kg=kg, cc0=cc0, cw=cw, pidx=pidx):
                        i = eng.dma_start(out=ctx.buf[f"wsf{pidx}"][:, 0:cw],
                                          in_=dr[wname][kg * 128:(kg + 1) * 128, cc0:cc0 + cw])
                        i.then_inc(ctx.dma_sem(), 16)
                        return i
                    plan.dma("gpsimd", f"ld_{wname}{kg}_{cc0}", ldw, 1, tag=t_in)

                    def split(eng, ctx, cw=cw, pidx=pidx):
                        ctx.seq(eng, eng.tensor_copy(ctx.buf[f"wsh{pidx}"][:, 0:cw],
                                                     ctx.buf[f"wsf{pidx}"][:, 0:cw]))
                        return eng.tensor_tensor(ctx.buf[f"wsl{pidx}"][:, 0:cw],
                                                 ctx.buf[f"wsf{pidx}"][:, 0:cw],
                                                 ctx.buf[f"wsh{pidx}"][:, 0:cw],
                                                 ALU.subtract)
                    plan.step("gpsimd", f"sp_{wname}{kg}_{cc0}", split, need_dma=[t_in])

                    def stw(eng, ctx, hi_d=hi_d, lo_d=lo_d, kg=kg, cc0=cc0, cw=cw, pidx=pidx):
                        eng.dma_start(out=hi_d[kg, :, cc0:cc0 + cw],
                                      in_=ctx.buf[f"wsh{pidx}"][:, 0:cw]).then_inc(ctx.dma_sem(), 16)
                        i = eng.dma_start(out=lo_d[kg, :, cc0:cc0 + cw],
                                          in_=ctx.buf[f"wsl{pidx}"][:, 0:cw])
                        i.then_inc(ctx.dma_sem(), 16)
                        return i
                    plan.dma("gpsimd", f"st_{wname}{kg}_{cc0}", stw, 2)
                return emit

            for wname, K, M in [("wk", D, D), ("wv", D, D), ("wr", D, D),
                                ("wo", D, D), ("wck", D, ICM),
                                ("wcv", ICM, D), ("wcr", D, D)]:
                for kg in range(K // 128):
                    for cc0 in range(0, M, 1024):
                        split_jobs.append(mk_split_job(wname, kg, cc0, min(1024, M - cc0)))

            with contextlib.ExitStack() as W:
                qsrc = [[sb(W, f"qw{c}_{g}", [128, 512]) for g in range(8)]
                        for c in range(2)]
                qhi = [[sb(W, f"qwh{c}_{g}", [128, 512], BF16) for g in range(8)]
                       for c in range(2)]
                qlo = [[sb(W, f"qwl{c}_{g}", [128, 512], BF16) for g in range(8)]
                       for c in range(2)]
                chn = [0]
                prev_store = [plan.last_store]

                def quant_one(wname, spname, K, M):
                    nslab = (K // 128 + 7) // 8
                    hi_d, lo_d = spills[spname]
                    full_slab = (nslab == 1)
                    for sl in range(nslab):
                        g0 = sl * 8
                        ngs = min(8, K // 128 - g0)
                        for c0 in range(0, M, 512):
                            cw = min(512, M - c0)
                            cc = chn[0] % 2
                            chn[0] += 1
                            t_in = tag()
                            def ldw(eng, ctx, wname=wname, g0=g0, ngs=ngs, c0=c0,
                                    cw=cw, cc=cc):
                                last = None
                                for gg in range(ngs):
                                    last = eng.dma_start(
                                        out=qsrc[cc][gg][:, 0:cw],
                                        in_=dr[wname][(g0 + gg) * 128:(g0 + gg + 1) * 128,
                                                      c0:c0 + cw])
                                    last.then_inc(ctx.dma_sem(), 16)
                                return last
                            plan.dma("sync", f"ldq_{wname}{sl}_{c0}", ldw, ngs,
                                     tag=t_in)
                            emit_fm_quant(plan, f"wq_{wname}{sl}_{c0}",
                                          qsrc[cc][:ngs], qhi[cc][:ngs], qlo[cc][:ngs],
                                          0, cw, ngs, qsc2,
                                          [qps, {"tr": "p_tr", "rsT": "p_rsT",
                                                 "exp": "p_a", "exp2": "p_b"},
                                           {"tr": "p_tr", "rsT": "p_rsT",
                                            "exp": "p_c", "exp2": "p_d"}],
                                          consts, dcol0=0)
                            t_out = tag()
                            def stw(eng, ctx, hi_d=hi_d, lo_d=lo_d, g0=g0,
                                    ngs=ngs, c0=c0, cw=cw, cc=cc):
                                last = None
                                for gg in range(ngs):
                                    eng.dma_start(out=hi_d[g0 + gg, :, c0:c0 + cw],
                                                  in_=qhi[cc][gg][:, 0:cw]).then_inc(ctx.dma_sem(), 16)
                                    last = eng.dma_start(out=lo_d[g0 + gg, :, c0:c0 + cw],
                                                         in_=qlo[cc][gg][:, 0:cw])
                                    last.then_inc(ctx.dma_sem(), 16)
                                return last
                            plan.dma("gpsimd", f"stq_{wname}{sl}_{c0}", stw,
                                     2 * ngs, tag=t_out)
                            prev_store[0] = t_out

                quant_one("wg_t", "qg", D, IFF)
                quant_one("wu_t", "qu", D, IFF)
                quant_one("wd_t", "qd", IFF, D)

        def ln_stats(name, xt, W, rows_extra=False, need=None,
                     bk=("p_a", "p_b", "p_c", "p_d", "p_exp")):
            def mm_sum(eng, ctx):
                last = None
                for g in range(NG):
                    last = eng.matmul(ctx.buf[bk[0]][0:1, 0:W], onescol[:, 0:1],
                                      xt[g][:, 0:W], start=(g == 0), stop=(g == NG - 1))
                return last
            plan.step("tensor", f"{name}.sum", mm_sum, need_dma=need or [])
            for g in range(NG):
                def sq_fn(eng, ctx, g=g):
                    return eng.activation(sqs[:, 0:W], xt[g][:, 0:W], AF.Square)
                plan.step("scalar", f"{name}.sq{g}", sq_fn)
                def mm_sq(eng, ctx, g=g):
                    return eng.matmul(ctx.buf["p_b"][0:1, 0:W], onescol[:, 0:1],
                                      sqs[:, 0:W], start=(g == 0),
                                      stop=(g == NG - 1))
                plan.step("tensor", f"{name}.msq{g}", mm_sq)

            def row_math(eng, ctx):
                r = rows
                ctx.seq(eng, eng.tensor_scalar_mul(r["m"][0:1, 0:W], ctx.buf[bk[0]][0:1, 0:W], 1.0 / D))
                ctx.seq(eng, eng.tensor_scalar_mul(r["msq"][0:1, 0:W], ctx.buf[bk[1]][0:1, 0:W], 1.0 / D))
                ctx.seq(eng, eng.tensor_mul(r["sd"][0:1, 0:W], r["m"][0:1, 0:W], r["m"][0:1, 0:W]))
                return eng.tensor_sub(r["var"][0:1, 0:W], r["msq"][0:1, 0:W], r["sd"][0:1, 0:W])
            plan.step("vector", f"{name}.rows", row_math)

            def row_sd(eng, ctx):
                return eng.activation(rows["sd"][0:1, 0:W], rows["var"][0:1, 0:W],
                                      AF.Sqrt, bias=epsc[0:1, 0:1])
            plan.step("scalar", f"{name}.sd", row_sd)

            def row_al(eng, ctx):
                return eng.reciprocal(rows["al"][0:1, 0:W], rows["sd"][0:1, 0:W])
            plan.step("vector", f"{name}.al", row_al)

            if rows_extra:
                def row_t2(eng, ctx):
                    ctx.seq(eng, eng.tensor_mul(rows["t2"][0:1, 0:W], rows["al"][0:1, 0:W], rows["al"][0:1, 0:W]))
                    return eng.tensor_mul(rows["t2"][0:1, 0:W], rows["t2"][0:1, 0:W],
                                          rows["var"][0:1, 0:W])
                plan.step("vector", f"{name}.t2", row_t2)
                def row_sd2(eng, ctx):
                    return eng.activation(rows["sd"][0:1, 0:W], rows["t2"][0:1, 0:W],
                                          AF.Sqrt, bias=epsc[0:1, 0:1])
                plan.step("scalar", f"{name}.sd2", row_sd2)
                def row_A(eng, ctx):
                    ctx.seq(eng, eng.reciprocal(rows["A"][0:1, 0:W], rows["sd"][0:1, 0:W]))
                    return eng.tensor_mul(rows["A"][0:1, 0:W], rows["A"][0:1, 0:W],
                                          rows["al"][0:1, 0:W])
                plan.step("vector", f"{name}.A", row_A)
            else:
                def row_A(eng, ctx):
                    return eng.tensor_copy(rows["A"][0:1, 0:W], rows["al"][0:1, 0:W])
                plan.step("vector", f"{name}.A", row_A)

            def bcast(eng, ctx):
                ctx.seq(eng, eng.matmul(ctx.buf[bk[2]][:, 0:W], brow[0:1, :],
                                        rows["m"][0:1, 0:W], start=True, stop=True))
                if rows_extra:
                    ctx.seq(eng, eng.matmul(ctx.buf[bk[4]][:, 0:W], brow[0:1, :],
                                            rows["al"][0:1, 0:W], start=True, stop=True))
                return eng.matmul(ctx.buf[bk[3]][:, 0:W], brow[0:1, :],
                                  rows["A"][0:1, 0:W], start=True, stop=True)
            plan.step("tensor", f"{name}.bc", bcast)

        def ln_apply(name, xt, dstf, W, gamma=None, beta=None, dst_off=0,
                     ybuf=None, bk=("p_a", "p_b", "p_c", "p_d", "p_exp")):
            for g in range(NG):
                def ap_fn(eng, ctx, g=g):
                    ctx.seq(eng, eng.tensor_tensor(sqs[:, 0:W], xt[g][:, 0:W],
                                                   ctx.buf[bk[2]][:, 0:W], ALU.subtract))
                    if ybuf is not None:
                        ctx.seq(eng, eng.tensor_tensor(ybuf[g][:, 0:W], sqs[:, 0:W],
                                                       ctx.buf[bk[4]][:, 0:W], ALU.mult))
                    if gamma is None:
                        return eng.tensor_tensor(dstf[g][:, dst_off:dst_off + W],
                                                 sqs[:, 0:W],
                                                 ctx.buf[bk[3]][:, 0:W], ALU.mult)
                    ctx.seq(eng, eng.tensor_tensor(sqs[:, 0:W],
                                                   sqs[:, 0:W],
                                                   ctx.buf[bk[3]][:, 0:W], ALU.mult))
                    return eng.tensor_scalar(dstf[g][:, dst_off:dst_off + W],
                                             sqs[:, 0:W],
                                             gamma[:, g:g + 1], beta[:, g:g + 1],
                                             ALU.mult, ALU.add)
                plan.step("vector", f"{name}.ap{g}", ap_fn)

        def ffn_phase(name, tiles, src_stream, dst_stream, out_coloff=None,
                      side_jobs=None):
            with contextlib.ExitStack() as S:
                if side_jobs:
                    for pidx in range(2):
                        sb(S, f"wsf{pidx}", [128, 1024])
                        sb(S, f"wsh{pidx}", [128, 1024], BF16)
                        sb(S, f"wsl{pidx}", [128, 1024], BF16)
                xt = [sb(S, f"{name}_x{g}", [128, 512]) for g in range(NG)]
                xc = [sb(S, f"{name}_c{g}", [128, 512]) for g in range(NG)]
                xqh = [sb(S, f"{name}_qh{g}", [128, 512], BF16) for g in range(NG)]
                xql = [sb(S, f"{name}_ql{g}", [128, 512], BF16) for g in range(NG)]
                hqh = [sb(S, f"{name}_hh{m}", [128, 512], BF16) for m in range(NM_FF)]
                hql = [sb(S, f"{name}_hl{m}", [128, 512], BF16) for m in range(NM_FF)]
                wga = [sb(S, f"wga{i}", [128, 1024], BF16) for i in range(2)]
                wgb = [sb(S, f"wgb{i}", [128, 1024], BF16) for i in range(2)]
                wua = [sb(S, f"wua{i}", [128, 1024], BF16) for i in range(2)]
                wub = [sb(S, f"wub{i}", [128, 1024], BF16) for i in range(2)]
                wda = [sb(S, f"wda{i}", [128, 2816], BF16) for i in range(2)]
                wdb = [sb(S, f"wdb{i}", [128, 2816], BF16) for i in range(2)]
                x2t = [sb(S, f"x2t{i}", [128, 512]) for i in range(2)]
                sgsr = [sb(S, f"sgsr{i}", [128, 512]) for i in range(2)]
                hscr = sgsr
                prev_x2store = [plan.last_store]
                jobs_left = list(side_jobs or [])
                per_tile = (len(jobs_left) + len(tiles) - 1) // max(1, len(tiles))
                jpidx = [0]

                for (c0, W) in tiles:
                    for _ in range(min(per_tile, len(jobs_left))):
                        jobs_left.pop(0)(jpidx[0] % 2)
                        jpidx[0] += 1
                    t_x = tag()
                    def ld_x(eng, ctx, c0=c0, W=W):
                        last = None
                        for g in range(NG):
                            last = eng.dma_start(out=xt[g][:, 0:W],
                                                 in_=src_stream[g * 128:(g + 1) * 128, c0:c0 + W])
                            last.then_inc(ctx.dma_sem(), 16)
                        return last
                    plan.dma("sync", f"{name}.ldx{c0}", ld_x, NG, tag=t_x,
                             need_dma=[prev_x2store[0]] if prev_x2store[0] else [])

                    ln_stats(f"{name}.ln{c0}", xt, W, rows_extra=False, need=[t_x])
                    ln_apply(f"{name}.lna{c0}", xt, xc, W)
                    emit_fm_quant(plan, f"{name}.xq{c0}", xc, xqh, xql, 0, W, NG,
                                  qsc2, [qps, {"tr": "p_tr", "rsT": "p_rsT",
                                               "exp": "p_a", "exp2": "p_b"},
                                         {"tr": "p_tr", "rsT": "p_rsT",
                                          "exp": "p_c", "exp2": "p_d"}],
                                  consts, dcol0=0)

                    def emit_mm_gu(m):
                        p = m % 2
                        gbank, ubank = ("p_a", "p_b") if p == 0 else ("p_c", "p_d")
                        t_w = tag()
                        def ld_w(eng, ctx, m=m, p=p):
                            last = None
                            for nmw, dsts in (("qg", (wga[p], wgb[p])), ("qu", (wua[p], wub[p]))):
                                hi_d, lo_d = spills[nmw]
                                src_h = hi_d[:, :, m * 128:(m + 1) * 128].transpose([1, 0, 2])
                                src_l = lo_d[:, :, m * 128:(m + 1) * 128].transpose([1, 0, 2])
                                eng.dma_start(out=dsts[0][:, :].rearrange("p (k c) -> p k c", c=128),
                                              in_=src_h).then_inc(ctx.dma_sem(), 16)
                                last = eng.dma_start(out=dsts[1][:, :].rearrange("p (k c) -> p k c", c=128),
                                                     in_=src_l)
                                last.then_inc(ctx.dma_sem(), 16)
                            return last
                        plan.dma("sync", f"{name}.ldw{c0}_{m}", ld_w, 4, tag=t_w)

                        def mm_gu(eng, ctx, W=W, p=p, gbank=gbank, ubank=ubank):
                            last = None
                            for (pbank, wa, wb) in ((gbank, wga[p], wgb[p]),
                                                    (ubank, wua[p], wub[p])):
                                for kg in range(NG):
                                    sl = slice(kg * 128, (kg + 1) * 128)
                                    eng.matmul(ctx.buf[pbank][:, 0:W], wa[:, sl], xqh[kg][:, 0:W],
                                               start=(kg == 0), stop=False)
                                    eng.matmul(ctx.buf[pbank][:, 0:W], wa[:, sl], xql[kg][:, 0:W],
                                               start=False, stop=False)
                                    last = eng.matmul(ctx.buf[pbank][:, 0:W], wb[:, sl], xqh[kg][:, 0:W],
                                                      start=False, stop=(kg == NG - 1))
                            return last
                        plan.step("tensor", f"{name}.gu{c0}_{m}", mm_gu, need_dma=[t_w])

                    def emit_post(m):
                        p = m % 2
                        ps3 = m % 3
                        gbank, ubank = ("p_a", "p_b") if p == 0 else ("p_c", "p_d")
                        def silu_fn(eng, ctx, W=W, p=p, gbank=gbank):
                            return eng.activation(sgsr[p][:, 0:W], ctx.buf[gbank][:, 0:W], AF.Sigmoid)
                        plan.step("scalar", f"{name}.silu{c0}_{m}", silu_fn)

                        def h_fn(eng, ctx, W=W, p=p, gbank=gbank, ubank=ubank):
                            ctx.seq(eng, eng.tensor_tensor(sgsr[p][:, 0:W], sgsr[p][:, 0:W],
                                                           ctx.buf[gbank][:, 0:W], ALU.mult))
                            return eng.tensor_tensor(sgsr[p][:, 0:W], sgsr[p][:, 0:W],
                                                     ctx.buf[ubank][:, 0:W], ALU.mult)
                        plan.step("vector", f"{name}.h{c0}_{m}", h_fn)

                        return emit_fm_quant(plan, f"{name}.hq{c0}_{m}", [hscr[p]],
                                             [hqh[m]], [hql[m]], 0, W, 1, qsc2[ps3],
                                             qps, consts, dcol0=0, defer=True)

                    pending = []
                    for m in range(NM_FF):
                        emit_mm_gu(m)
                        for qch in pending:
                            k = 8
                            for f in qch[:k]:
                                f()
                            del qch[:k]
                        pending = [q for q in pending if q]
                        if m > 0:
                            pending.append(emit_post(m - 1))
                    for qch in pending:
                        for f in qch:
                            f()
                    for f in emit_post(NM_FF - 1):
                        f()

                    for og in range(NG):
                        p = og % 2
                        dbank = "p_a" if p == 0 else "p_b"
                        t_w = tag()
                        def ld_wd(eng, ctx, og=og, p=p):
                            hi_d, lo_d = spills["qd"]
                            src_h = hi_d[:, :, og * 128:(og + 1) * 128].transpose([1, 0, 2])
                            src_l = lo_d[:, :, og * 128:(og + 1) * 128].transpose([1, 0, 2])
                            eng.dma_start(out=wda[p][:, :].rearrange("p (k c) -> p k c", c=128),
                                          in_=src_h).then_inc(ctx.dma_sem(), 16)
                            last = eng.dma_start(out=wdb[p][:, :].rearrange("p (k c) -> p k c", c=128),
                                                 in_=src_l)
                            last.then_inc(ctx.dma_sem(), 16)
                            return last
                        plan.dma("sync", f"{name}.ldwd{c0}_{og}", ld_wd, 2, tag=t_w)

                        def mm_dn(eng, ctx, W=W, p=p, dbank=dbank):
                            last = None
                            for m in range(NM_FF):
                                sl = slice(m * 128, (m + 1) * 128)
                                eng.matmul(ctx.buf[dbank][:, 0:W], wda[p][:, sl], hqh[m][:, 0:W],
                                           start=(m == 0), stop=False)
                                eng.matmul(ctx.buf[dbank][:, 0:W], wda[p][:, sl], hql[m][:, 0:W],
                                           start=False, stop=False)
                                last = eng.matmul(ctx.buf[dbank][:, 0:W], wdb[p][:, sl], hqh[m][:, 0:W],
                                                  start=False, stop=(m == NM_FF - 1))
                            return last
                        plan.step("tensor", f"{name}.dn{c0}_{og}", mm_dn, need_dma=[t_w])

                        def resid(eng, ctx, og=og, W=W, p=p, dbank=dbank):
                            return eng.scalar_tensor_tensor(x2t[p][:, 0:W], ctx.buf[dbank][:, 0:W],
                                                            0.5, xt[og][:, 0:W],
                                                            ALU.mult, ALU.add)
                        plan.step("vector", f"{name}.res{c0}_{og}", resid)

                        t_st = tag()
                        def st_x2(eng, ctx, og=og, c0=c0, W=W, p=p):
                            if out_coloff is not None:
                                dst = out[og * 128:(og + 1) * 128, c0 - out_coloff:c0 - out_coloff + W]
                            else:
                                dst = dst_stream[og * 128:(og + 1) * 128, c0:c0 + W]
                            i = eng.dma_start(out=dst, in_=x2t[p][:, 0:W])
                            i.then_inc(ctx.dma_sem(), 16)
                            return i
                        plan.dma("gpsimd", f"{name}.stx{c0}_{og}", st_x2, 1, tag=t_st)
                        prev_x2store[0] = t_st

        if "ffn1" in phases:
            ffn_phase("f1", FFN1_TILES, x_in, sA, side_jobs=split_jobs)
        elif split_jobs:
            with contextlib.ExitStack() as W:
                for pidx in range(2):
                    sb(W, f"wsf{pidx}", [128, 1024])
                    sb(W, f"wsh{pidx}", [128, 1024], BF16)
                    sb(W, f"wsl{pidx}", [128, 1024], BF16)
                for j, job in enumerate(split_jobs):
                    job(j % 2)

        def attn_phase():
            with contextlib.ExitStack() as S:
                x2t = [sb(S, f"at_x{g}", [128, 512]) for g in range(NG)]
                yt = x2t
                xab = [sb(S, f"at_a{g}", [128, 513]) for g in range(NG)]
                ks = [sb(S, f"at_k{g}", [128, 512]) for g in range(NG)]
                vs = [sb(S, f"at_v{g}", [128, 512]) for g in range(NG)]
                rs_ = [sb(S, f"at_r{g}", [128, 512]) for g in range(NG)]
                sab = [sb(S, f"at_sa{i}", [128, 513]) for i in range(2)]
                sbb = [sb(S, f"at_sb{i}", [128, 513]) for i in range(2)]
                dsc = [sb(S, f"dsc{i}", [128, 512]) for i in range(2)]
                sc1 = [sb(S, f"sc1_{i}", [128, 512]) for i in range(2)]
                sc2 = [sb(S, f"sc2_{i}", [128, 512]) for i in range(2)]
                rwh = [sb(S, f"at_rh{g}", [128, 512], BF16) for g in range(NG)]
                rwl = [sb(S, f"at_rl{g}", [128, 512], BF16) for g in range(NG)]
                kmt = sb(S, "kmt", [128, 512])
                wsa = [sb(S, f"wsa{i}", [128, 1024], BF16) for i in range(2)]
                wsb_ = [sb(S, f"wsb{i}", [128, 1024], BF16) for i in range(2)]
                mxf2 = [sb(S, f"mxf{i}", [128, 512]) for i in range(2)]
                mxh2 = [sb(S, f"mxh{i}", [128, 512], BF16) for i in range(2)]
                mxl2 = [sb(S, f"mxl{i}", [128, 512], BF16) for i in range(2)]

                def zero_halo(eng, ctx):
                    last = None
                    for g in range(NG):
                        last = eng.memset(xab[g][:, 0:1], 0.0)
                    return last
                plan.step("vector", "at.zh", zero_halo, need_dma=[plan.last_store] if plan.last_store else [])
                prev_x3store = [plan.last_store]
                prev_W = [None]

                for ti, (c0, W) in enumerate(ATTN_TILES):
                    t_x = tag()
                    def ld_x(eng, ctx, c0=c0, W=W):
                        for g in range(NG):
                            eng.dma_start(out=x2t[g][:, 0:W],
                                          in_=sA[g * 128:(g + 1) * 128, c0:c0 + W]).then_inc(ctx.dma_sem(), 16)
                        i = eng.dma_start(out=kmt[:, 0:W], in_=kmask_in[:, c0:c0 + W])
                        i.then_inc(ctx.dma_sem(), 16)
                        return i
                    plan.dma("sync", f"at.ldx{c0}", ld_x, NG + 1, tag=t_x,
                             need_dma=[prev_x3store[0]] if prev_x3store[0] else [])

                    if ti > 0:
                        def halo(eng, ctx, pw=prev_W[0], c0=c0):
                            last = None
                            for g in range(NG):
                                last = eng.tensor_copy(xab[g][:, 0:1], xab[g][:, pw:pw + 1])
                                if c0 == LB:
                                    ctx.seq(eng, last)
                                    last = eng.tensor_scalar_mul(xab[g][:, 0:1], xab[g][:, 0:1],
                                                                 ctx.buf["h0m"][:, 0:1])
                            return last
                        plan.step("vector", f"at.halo{c0}", halo)
                    prev_W[0] = W

                    ln_stats(f"at.ln{c0}", x2t, W, rows_extra=True, need=[t_x])
                    ln_apply(f"at.lna{c0}", x2t, xab, W, gamma=par["ln0g"],
                             beta=par["ln0b"], dst_off=1, ybuf=yt)

                    mats = [("wk", "tmk", ks, "k"), ("wv", "tmv", vs, "v")]
                    if ti > 0:
                        mats.append(("wr", "tmr", rs_, "r"))
                    for (wname, tmname, dsts, mt) in mats:
                        for kg in range(NG):
                            p = kg % 2
                            t_w = tag()
                            def ld_w(eng, ctx, wname=wname, kg=kg, p=p):
                                hi_d, lo_d = spills[wname]
                                eng.dma_start(out=wsa[p][:, :], in_=hi_d[kg, :, :]).then_inc(ctx.dma_sem(), 16)
                                i = eng.dma_start(out=wsb_[p][:, :], in_=lo_d[kg, :, :])
                                i.then_inc(ctx.dma_sem(), 16)
                                return i
                            plan.dma("sync", f"at.lw{mt}{c0}_{kg}", ld_w, 2, tag=t_w)

                            def mix(eng, ctx, kg=kg, tmname=tmname, W=W, p=p):
                                ctx.seq(eng, eng.tensor_tensor(dsc[p][:, 0:W],
                                                               xab[kg][:, 1:W + 1],
                                                               xab[kg][:, 0:W], ALU.subtract))
                                ctx.seq(eng, eng.scalar_tensor_tensor(mxf2[p][:, 0:W], dsc[p][:, 0:W],
                                                                      par[tmname][:, kg:kg + 1],
                                                                      xab[kg][:, 0:W],
                                                                      ALU.mult, ALU.add))
                                ctx.seq(eng, eng.tensor_copy(mxh2[p][:, 0:W], mxf2[p][:, 0:W]))
                                return eng.tensor_tensor(mxl2[p][:, 0:W], mxf2[p][:, 0:W],
                                                         mxh2[p][:, 0:W], ALU.subtract)
                            plan.step("vector", f"at.mx{mt}{c0}_{kg}", mix)

                            def mm(eng, ctx, kg=kg, W=W, p=p):
                                last = None
                                for og in range(NG):
                                    sl = slice(og * 128, (og + 1) * 128)
                                    eng.matmul(ogbank[og][:, 0:W], wsa[p][:, sl], mxh2[p][:, 0:W],
                                               start=(kg == 0), stop=False)
                                    eng.matmul(ogbank[og][:, 0:W], wsa[p][:, sl], mxl2[p][:, 0:W],
                                               start=False, stop=False)
                                    last = eng.matmul(ogbank[og][:, 0:W], wsb_[p][:, sl], mxh2[p][:, 0:W],
                                                      start=False, stop=(kg == NG - 1))
                                return last
                            plan.step("tensor", f"at.mm{mt}{c0}_{kg}", mm, need_dma=[t_w])

                        if mt == "k":
                            for og in range(NG):
                                def ev(eng, ctx, og=og, W=W):
                                    return eng.tensor_tensor(ks[og][:, 0:W],
                                                             ogbank[og][:, 0:W],
                                                             kmt[:, 0:W], ALU.add)
                                plan.step("vector", f"at.evk{c0}_{og}", ev)
                        elif mt == "v":
                            for og in range(NG):
                                def ev(eng, ctx, og=og, W=W):
                                    return eng.tensor_copy(vs[og][:, 0:W], ogbank[og][:, 0:W])
                                plan.step("vector", f"at.evv{c0}_{og}", ev)
                        else:
                            for og in range(NG):
                                def ev(eng, ctx, og=og, W=W):
                                    return eng.activation(rs_[og][:, 0:W], ogbank[og][:, 0:W],
                                                          AF.Sigmoid)
                                plan.step("scalar", f"at.evr{c0}_{og}", ev)

                    def smax(eng, ctx, W=W):
                        last = None
                        for g in range(NG):
                            last = eng.tensor_reduce(s_cur[:, g:g + 1],
                                                     ks[g][:, 0:W], AX.X, ALU.max)
                        return last
                    plan.step("vector", f"at.smax{c0}", smax)

                    def sresc(eng, ctx):
                        ctx.seq(eng, eng.tensor_scalar_max(s_cur[:, :], s_cur[:, :], -60.0))
                        ctx.seq(eng, eng.tensor_sub(resc[:, :], s_prev[:, :], s_cur[:, :]))
                        return eng.tensor_scalar_mul(ns_cur[:, :], s_cur[:, :], -1.0)
                    plan.step("vector", f"at.sresc{c0}", sresc)

                    def sexp(eng, ctx):
                        return eng.activation(resc[:, :], resc[:, :], AF.Exp)
                    plan.step("scalar", f"at.sexp{c0}", sexp)

                    def stfix(eng, ctx):
                        ctx.seq(eng, eng.tensor_mul(st_a[:, :], st_a[:, :], resc[:, :]))
                        ctx.seq(eng, eng.tensor_mul(st_b[:, :], st_b[:, :], resc[:, :]))
                        return eng.tensor_copy(s_prev[:, :], s_cur[:, :])
                    plan.step("vector", f"at.stfix{c0}", stfix)

                    for g in range(NG):
                        def kap(eng, ctx, g=g, W=W):
                            return eng.activation(ks[g][:, 0:W], ks[g][:, 0:W], AF.Exp,
                                                  bias=ns_cur[:, g:g + 1])
                        plan.step("scalar", f"at.kap{c0}_{g}", kap)
                        def kv(eng, ctx, g=g, W=W):
                            return eng.tensor_mul(vs[g][:, 0:W], ks[g][:, 0:W], vs[g][:, 0:W])
                        plan.step("vector", f"at.kv{c0}_{g}", kv)

                    for g in range(NG):
                        p = g % 2
                        def scan(eng, ctx, g=g, W=W, p=p):
                            lam_bc = lam[:, g:g + 1].to_broadcast([128, W])
                            ctx.seq(eng, eng.tensor_copy(sab[p][:, 0:1], st_a[:, g:g + 1]))
                            ctx.seq(eng, eng.tensor_copy(sbb[p][:, 0:1], st_b[:, g:g + 1]))
                            ctx.seq(eng, eng.tensor_tensor_scan(sab[p][:, 1:W + 1], lam_bc,
                                                                vs[g][:, 0:W], st_a[:, g:g + 1],
                                                                ALU.mult, ALU.add))
                            ctx.seq(eng, eng.tensor_tensor_scan(sbb[p][:, 1:W + 1], lam_bc,
                                                                ks[g][:, 0:W], st_b[:, g:g + 1],
                                                                ALU.mult, ALU.add))
                            ctx.seq(eng, eng.tensor_copy(st_a[:, g:g + 1], sab[p][:, W:W + 1]))
                            return eng.tensor_copy(st_b[:, g:g + 1], sbb[p][:, W:W + 1])
                        plan.step("vector", f"at.scan{c0}_{g}", scan)

                        if ti == 0:
                            continue

                        def wkv(eng, ctx, g=g, W=W, p=p):
                            s1 = sc1[p][:, 0:W]
                            s2 = sc2[p][:, 0:W]
                            ctx.seq(eng, eng.tensor_scalar_mul(s1, ks[g][:, 0:W],
                                                               eu[:, g:g + 1]))
                            ctx.seq(eng, eng.tensor_add(s1, s1, sbb[p][:, 0:W]))
                            ctx.seq(eng, eng.tensor_scalar_max(s1, s1, 1e-30))
                            ctx.seq(eng, eng.reciprocal(s2, s1))
                            ctx.seq(eng, eng.tensor_scalar_mul(s1, vs[g][:, 0:W],
                                                               eu[:, g:g + 1]))
                            ctx.seq(eng, eng.tensor_add(s1, s1, sab[p][:, 0:W]))
                            ctx.seq(eng, eng.tensor_mul(s1, s1, s2))
                            ctx.seq(eng, eng.tensor_mul(s1, s1, rs_[g][:, 0:W]))
                            ctx.seq(eng, eng.tensor_copy(rwh[g][:, 0:W], s1))
                            return eng.tensor_tensor(rwl[g][:, 0:W], s1,
                                                     rwh[g][:, 0:W], ALU.subtract)
                        plan.step("vector", f"at.wkv{c0}_{g}", wkv)

                    if ti == 0:
                        continue

                    for kg in range(NG):
                        p = kg % 2
                        t_w = tag()
                        def ld_w(eng, ctx, kg=kg, p=p):
                            hi_d, lo_d = spills["wo"]
                            eng.dma_start(out=wsa[p][:, :], in_=hi_d[kg, :, :]).then_inc(ctx.dma_sem(), 16)
                            i = eng.dma_start(out=wsb_[p][:, :], in_=lo_d[kg, :, :])
                            i.then_inc(ctx.dma_sem(), 16)
                            return i
                        plan.dma("sync", f"at.lwo{c0}_{kg}", ld_w, 2, tag=t_w)

                        def mm(eng, ctx, kg=kg, W=W, p=p):
                            last = None
                            for og in range(NG):
                                sl = slice(og * 128, (og + 1) * 128)
                                eng.matmul(ogbank[og][:, 0:W], wsa[p][:, sl], rwh[kg][:, 0:W],
                                           start=(kg == 0), stop=False)
                                eng.matmul(ogbank[og][:, 0:W], wsa[p][:, sl], rwl[kg][:, 0:W],
                                           start=False, stop=False)
                                last = eng.matmul(ogbank[og][:, 0:W], wsb_[p][:, sl], rwh[kg][:, 0:W],
                                                  start=False, stop=(kg == NG - 1))
                            return last
                        plan.step("tensor", f"at.mmo{c0}_{kg}", mm, need_dma=[t_w])

                    for og in range(NG):
                        def x3(eng, ctx, og=og, W=W):
                            return eng.tensor_add(yt[og][:, 0:W], ogbank[og][:, 0:W],
                                                  yt[og][:, 0:W])
                        plan.step("vector", f"at.x3{c0}_{og}", x3)

                        t_st = tag()
                        def st_x3(eng, ctx, og=og, c0=c0, W=W):
                            i = eng.dma_start(out=sB[og * 128:(og + 1) * 128, c0:c0 + W],
                                              in_=yt[og][:, 0:W])
                            i.then_inc(ctx.dma_sem(), 16)
                            return i
                        plan.dma("gpsimd", f"at.stx3{c0}_{og}", st_x3, 1, tag=t_st)
                        prev_x3store[0] = t_st

        if "attn" in phases:
            attn_phase()

        def cm_phase():
            with contextlib.ExitStack() as S:
                x3t = [sb(S, f"cm_x{g}", [128, 512]) for g in range(NG)]
                x2one = [sb(S, f"cm_x2o{i}", [128, 512]) for i in range(2)]
                xcb = [sb(S, f"cm_c{g}", [128, 513]) for g in range(NG)]
                rrs = [sb(S, f"cm_rr{g}", [128, 512]) for g in range(NG)]
                rkh = [sb(S, f"cm_kh{o}", [128, 512], BF16) for o in range(NOG_CM)]
                rkl = [sb(S, f"cm_kl{o}", [128, 512], BF16) for o in range(NOG_CM)]
                dsc = [sb(S, f"dsc{i}", [128, 512]) for i in range(2)]
                mxf2 = [sb(S, f"mxf{i}", [128, 512]) for i in range(2)]
                mxh2 = [sb(S, f"mxh{i}", [128, 512], BF16) for i in range(2)]
                mxl2 = [sb(S, f"mxl{i}", [128, 512], BF16) for i in range(2)]
                sc1 = [sb(S, f"sc1_{i}", [128, 512]) for i in range(2)]
                wsa = [sb(S, f"wsa{i}", [128, 1024], BF16) for i in range(2)]
                wsb_ = [sb(S, f"wsb{i}", [128, 1024], BF16) for i in range(2)]

                def zero_halo(eng, ctx):
                    last = None
                    for g in range(NG):
                        last = eng.memset(xcb[g][:, 0:1], 0.0)
                    return last
                plan.step("vector", "cm.zh", zero_halo, need_dma=[plan.last_store] if plan.last_store else [])
                prev_store = [plan.last_store]
                prev_W = [None]

                for ti, (c0, W) in enumerate(CM_TILES):
                    t_x = tag()
                    def ld_x(eng, ctx, c0=c0, W=W):
                        last = None
                        for g in range(NG):
                            last = eng.dma_start(out=x3t[g][:, 0:W],
                                                 in_=sB[g * 128:(g + 1) * 128, c0:c0 + W])
                            last.then_inc(ctx.dma_sem(), 16)
                        return last
                    plan.dma("sync", f"cm.ldx{c0}", ld_x, NG,
                             tag=t_x, need_dma=[prev_store[0]] if prev_store[0] else [])

                    if ti > 0:
                        def halo(eng, ctx, pw=prev_W[0], c0=c0):
                            last = None
                            for g in range(NG):
                                last = eng.tensor_copy(xcb[g][:, 0:1], xcb[g][:, pw:pw + 1])
                                if c0 == LB:
                                    ctx.seq(eng, last)
                                    last = eng.tensor_scalar_mul(xcb[g][:, 0:1], xcb[g][:, 0:1],
                                                                 ctx.buf["h0m"][:, 0:1])
                            return last
                        plan.step("vector", f"cm.halo{c0}", halo)
                    prev_W[0] = W

                    ln_stats(f"cm.ln{c0}", x3t, W, rows_extra=False, need=[t_x])
                    ln_apply(f"cm.lna{c0}", x3t, xcb, W, gamma=par["ln1g"],
                             beta=par["ln1b"], dst_off=1)

                    if ti == 0:
                        continue

                    for kg in range(NG):
                        p = kg % 2
                        t_w = tag()
                        def ld_w(eng, ctx, kg=kg, p=p):
                            hi_d, lo_d = spills["wcr"]
                            eng.dma_start(out=wsa[p][:, 0:D], in_=hi_d[kg, :, :]).then_inc(ctx.dma_sem(), 16)
                            i = eng.dma_start(out=wsb_[p][:, 0:D], in_=lo_d[kg, :, :])
                            i.then_inc(ctx.dma_sem(), 16)
                            return i
                        plan.dma("sync", f"cm.lwr{c0}_{kg}", ld_w, 2, tag=t_w)

                        def mix(eng, ctx, kg=kg, W=W, p=p):
                            ctx.seq(eng, eng.tensor_tensor(dsc[p][:, 0:W], xcb[kg][:, 1:W + 1],
                                                           xcb[kg][:, 0:W], ALU.subtract))
                            ctx.seq(eng, eng.scalar_tensor_tensor(mxf2[p][:, 0:W], dsc[p][:, 0:W],
                                                                  par["cmr"][:, kg:kg + 1],
                                                                  xcb[kg][:, 0:W],
                                                                  ALU.mult, ALU.add))
                            ctx.seq(eng, eng.tensor_copy(mxh2[p][:, 0:W], mxf2[p][:, 0:W]))
                            return eng.tensor_tensor(mxl2[p][:, 0:W], mxf2[p][:, 0:W],
                                                     mxh2[p][:, 0:W], ALU.subtract)
                        plan.step("vector", f"cm.mxr{c0}_{kg}", mix)

                        def mm(eng, ctx, kg=kg, W=W, p=p):
                            last = None
                            for og in range(NG):
                                sl = slice(og * 128, (og + 1) * 128)
                                eng.matmul(ogbank[og][:, 0:W], wsa[p][:, sl], mxh2[p][:, 0:W],
                                           start=(kg == 0), stop=False)
                                eng.matmul(ogbank[og][:, 0:W], wsa[p][:, sl], mxl2[p][:, 0:W],
                                           start=False, stop=False)
                                last = eng.matmul(ogbank[og][:, 0:W], wsb_[p][:, sl], mxh2[p][:, 0:W],
                                                  start=False, stop=(kg == NG - 1))
                            return last
                        plan.step("tensor", f"cm.mmr{c0}_{kg}", mm, need_dma=[t_w])
                    for og in range(NG):
                        def ev(eng, ctx, og=og, W=W):
                            return eng.activation(rrs[og][:, 0:W], ogbank[og][:, 0:W],
                                                  AF.Sigmoid)
                        plan.step("scalar", f"cm.evr{c0}_{og}", ev)

                    for kg in range(NG):
                        p = kg % 2
                        def mix(eng, ctx, kg=kg, W=W, p=p):
                            ctx.seq(eng, eng.tensor_tensor(dsc[p][:, 0:W], xcb[kg][:, 1:W + 1],
                                                           xcb[kg][:, 0:W], ALU.subtract))
                            ctx.seq(eng, eng.scalar_tensor_tensor(mxf2[p][:, 0:W], dsc[p][:, 0:W],
                                                                  par["cmk"][:, kg:kg + 1],
                                                                  xcb[kg][:, 0:W],
                                                                  ALU.mult, ALU.add))
                            ctx.seq(eng, eng.tensor_copy(rkh[NOG_CM - 8 + kg][:, 0:W], mxf2[p][:, 0:W]))
                            return eng.tensor_tensor(rkl[NOG_CM - 8 + kg][:, 0:W], mxf2[p][:, 0:W],
                                                     rkh[NOG_CM - 8 + kg][:, 0:W], ALU.subtract)
                        plan.step("vector", f"cm.mxk{c0}_{kg}", mix)

                    for rnd in range(4):
                        for kg in range(NG):
                            p = kg % 2
                            t_w = tag()
                            def ld_w(eng, ctx, kg=kg, rnd=rnd, p=p):
                                hi_d, lo_d = spills["wck"]
                                eng.dma_start(out=wsa[p][:, 0:1024],
                                              in_=hi_d[kg, :, rnd * 1024:(rnd + 1) * 1024]).then_inc(ctx.dma_sem(), 16)
                                i = eng.dma_start(out=wsb_[p][:, 0:1024],
                                                  in_=lo_d[kg, :, rnd * 1024:(rnd + 1) * 1024])
                                i.then_inc(ctx.dma_sem(), 16)
                                return i
                            plan.dma("sync", f"cm.lwk{c0}_{rnd}_{kg}", ld_w, 2, tag=t_w)

                            def mm(eng, ctx, kg=kg, W=W, p=p):
                                last = None
                                mh = rkh[NOG_CM - 8 + kg]
                                ml = rkl[NOG_CM - 8 + kg]
                                for oo in range(8):
                                    sl = slice(oo * 128, (oo + 1) * 128)
                                    eng.matmul(ogbank[oo][:, 0:W], wsa[p][:, sl], mh[:, 0:W],
                                               start=(kg == 0), stop=False)
                                    eng.matmul(ogbank[oo][:, 0:W], wsa[p][:, sl], ml[:, 0:W],
                                               start=False, stop=False)
                                    last = eng.matmul(ogbank[oo][:, 0:W], wsb_[p][:, sl], mh[:, 0:W],
                                                      start=False, stop=(kg == NG - 1))
                                return last
                            plan.step("tensor", f"cm.mmk{c0}_{rnd}_{kg}", mm, need_dma=[t_w])

                        for oo in range(8):
                            og32 = rnd * 8 + oo
                            p = oo % 2
                            def rk2(eng, ctx, oo=oo, og32=og32, W=W, p=p):
                                return eng.activation(sc1[p][:, 0:W], ogbank[oo][:, 0:W], AF.Relu)
                            plan.step("scalar", f"cm.rk{c0}_{og32}", rk2)

                            def sq_split(eng, ctx, og32=og32, W=W, p=p):
                                ctx.seq(eng, eng.tensor_mul(sc1[p][:, 0:W], sc1[p][:, 0:W], sc1[p][:, 0:W]))
                                ctx.seq(eng, eng.tensor_copy(rkh[og32][:, 0:W], sc1[p][:, 0:W]))
                                return eng.tensor_tensor(rkl[og32][:, 0:W], sc1[p][:, 0:W],
                                                         rkh[og32][:, 0:W], ALU.subtract)
                            plan.step("vector", f"cm.rs{c0}_{og32}", sq_split)

                    for kg32 in range(NOG_CM):
                        p = kg32 % 2
                        t_w = tag()
                        def ld_w(eng, ctx, kg32=kg32, p=p):
                            hi_d, lo_d = spills["wcv"]
                            eng.dma_start(out=wsa[p][:, 0:D], in_=hi_d[kg32, :, :]).then_inc(ctx.dma_sem(), 16)
                            i = eng.dma_start(out=wsb_[p][:, 0:D], in_=lo_d[kg32, :, :])
                            i.then_inc(ctx.dma_sem(), 16)
                            return i
                        plan.dma("sync", f"cm.lwv{c0}_{kg32}", ld_w, 2, tag=t_w)

                        def mm(eng, ctx, kg32=kg32, W=W, p=p):
                            last = None
                            for og in range(NG):
                                sl = slice(og * 128, (og + 1) * 128)
                                eng.matmul(ogbank[og][:, 0:W], wsa[p][:, sl], rkh[kg32][:, 0:W],
                                           start=(kg32 == 0), stop=False)
                                eng.matmul(ogbank[og][:, 0:W], wsa[p][:, sl], rkl[kg32][:, 0:W],
                                           start=False, stop=False)
                                last = eng.matmul(ogbank[og][:, 0:W], wsb_[p][:, sl], rkh[kg32][:, 0:W],
                                                  start=False, stop=(kg32 == NOG_CM - 1))
                            return last
                        plan.step("tensor", f"cm.mmv{c0}_{kg32}", mm, need_dma=[t_w])

                    for og in range(NG):
                        t_x2 = tag()
                        def ldx2(eng, ctx, og=og, c0=c0, W=W):
                            i = eng.dma_start(out=x2one[og % 2][:, 0:W],
                                              in_=sA[og * 128:(og + 1) * 128, c0:c0 + W])
                            i.then_inc(ctx.dma_sem(), 16)
                            return i
                        plan.dma("sync", f"cm.ldx2{c0}_{og}", ldx2, 1, tag=t_x2)

                        def x5(eng, ctx, og=og, W=W):
                            ctx.seq(eng, eng.tensor_mul(sc1[og % 2][:, 0:W], ogbank[og][:, 0:W],
                                                        rrs[og][:, 0:W]))
                            ctx.seq(eng, eng.tensor_add(x3t[og][:, 0:W], sc1[og % 2][:, 0:W],
                                                        x3t[og][:, 0:W]))
                            return eng.tensor_add(x3t[og][:, 0:W], x3t[og][:, 0:W],
                                                  x2one[og % 2][:, 0:W])
                        plan.step("vector", f"cm.x5{c0}_{og}", x5, need_dma=[t_x2])

                        t_st = tag()
                        def st5(eng, ctx, og=og, c0=c0, W=W):
                            i = eng.dma_start(out=sA[og * 128:(og + 1) * 128, c0:c0 + W],
                                              in_=x3t[og][:, 0:W])
                            i.then_inc(ctx.dma_sem(), 16)
                            return i
                        plan.dma("gpsimd", f"cm.st5{c0}_{og}", st5, 1, tag=t_st)
                        prev_store[0] = t_st

        if "cm" in phases:
            cm_phase()

        if "ffn2" in phases:
            ffn_phase("f2", FFN2_TILES, sA, None, out_coloff=LB)

        if dump is not None:
            src, cols = dump
            srcmap = {"sA": sA, "sB": sB}[src]
            with contextlib.ExitStack() as S:
                dsb = sb(S, "dumpb", [128, 512])
                prevs = [plan.last_store]
                for g in range(NG):
                    for c0 in range(cols[0], cols[1], 512):
                        cw = min(512, cols[1] - c0)
                        t_l = tag()
                        def dl(eng, ctx, g=g, c0=c0, cw=cw):
                            i = eng.dma_start(out=dsb[:, 0:cw],
                                              in_=srcmap[g * 128:(g + 1) * 128, c0:c0 + cw])
                            i.then_inc(ctx.dma_sem(), 16)
                            return i
                        plan.dma("sync", f"dmp_l{g}_{c0}", dl, 1, tag=t_l,
                                 need_dma=[prevs[0]] if prevs[0] else [])
                        t_s = tag()
                        def ds(eng, ctx, g=g, c0=c0, cw=cw):
                            i = eng.dma_start(out=out[g * 128:(g + 1) * 128,
                                                      c0 - cols[0]:c0 - cols[0] + cw],
                                              in_=dsb[:, 0:cw])
                            i.then_inc(ctx.dma_sem(), 16)
                            return i
                        plan.dma("gpsimd", f"dmp_s{g}_{c0}", ds, 1, tag=t_s,
                                 need_dma=[t_l])
                        prevs[0] = t_s

        def fin(eng, ctx):
            for e in N_DSEM:
                for k in range(N_DSEM[e]):
                    tot = plan.dma_slot_count[e][k] * 16
                    if tot:
                        eng.wait_ge(ctx.sems[f"d_{e}_{k}"], tot)
            return eng.memset(ctx.buf["lam"][0:1, 0:1], 0.0)
        plan.step("vector", "fin", fin)

        emit_program(nc, plan, ctx, deps=deps, capture=capture)
    return nc, plan, ctx


def build_auto(phases=("wprep", "ffn1", "attn", "cm", "ffn2"), dump=None):
    cap = {}
    _nc1, plan1, ctx1 = build_full(phases, dump, capture=cap)
    deps = compute_deps(plan1, cap, ctx1.addr_of)
    nc, _p, _c = build_full(phases, dump, deps=deps)
    return nc



def host_inputs(inputs, core):
    x = np.asarray(inputs["L0_attn_x"]).reshape(16384, D)
    t0 = core * TOWN
    lo = t0 - LB
    if lo < 0:
        sl = np.zeros((TPRE, D), np.float32)
        sl[LB:] = x[0:TOWN]
    else:
        sl = x[lo:t0 + TOWN]
    xfm = np.ascontiguousarray(sl.T)

    km = np.zeros((128, TPRE), np.float32)
    if core == 0:
        km[:, 0:LB] = NEG

    idn, e64c, stepb = host_consts()
    fm = lambda v: np.ascontiguousarray(np.asarray(v).reshape(NG, 128).T)
    m = {
        "i_x": xfm, "i_kmask": km,
        "i_wk": np.asarray(inputs["L0_attn_Wk"]),
        "i_wv": np.asarray(inputs["L0_attn_Wv"]),
        "i_wr": np.asarray(inputs["L0_attn_Wr"]),
        "i_wo": np.asarray(inputs["L0_attn_Wo"]),
        "i_wck": np.asarray(inputs["L0_attn_Wck"]),
        "i_wcv": np.asarray(inputs["L0_attn_Wcv"]),
        "i_wcr": np.asarray(inputs["L0_attn_Wcr"]),
        "i_wg_t": np.ascontiguousarray(np.asarray(inputs["L0_ffn_w_gate"]).T),
        "i_wu_t": np.ascontiguousarray(np.asarray(inputs["L0_ffn_w_up"]).T),
        "i_wd_t": np.ascontiguousarray(np.asarray(inputs["L0_ffn_w_down"]).T),
        "i_ln0g": fm(inputs["L0_attn_ln0_g"]), "i_ln0b": fm(inputs["L0_attn_ln0_b"]),
        "i_ln1g": fm(inputs["L0_attn_ln1_g"]), "i_ln1b": fm(inputs["L0_attn_ln1_b"]),
        "i_tmk": fm(inputs["L0_attn_tm_k"]), "i_tmv": fm(inputs["L0_attn_tm_v"]),
        "i_tmr": fm(inputs["L0_attn_tm_r"]),
        "i_cmk": fm(inputs["L0_attn_cm_k"]), "i_cmr": fm(inputs["L0_attn_cm_r"]),
        "i_td": fm(inputs["L0_attn_time_decay"]), "i_tf": fm(inputs["L0_attn_time_first"]),
        "i_idn": idn, "i_e64": e64c, "i_stepb": stepb,
        "i_onescol": np.ones((128, 1), np.float32),
        "i_browones": np.ones((1, 128), np.float32),
        "i_epsc": np.full((128, 1), 1e-5, np.float32),
        "i_h0m": np.full((128, 1), 0.0 if core == 0 else 1.0, np.float32),
    }
    return m


def kernel(**inputs):
    nc = build_auto()
    in_maps = [host_inputs(inputs, c) for c in range(NCORES)]
    res = run_bass_kernel_spmd(nc, in_maps, core_ids=list(range(NCORES)))
    out = np.empty((16384, D), np.float32)
    for c in range(NCORES):
        out[c * TOWN:(c + 1) * TOWN] = res.results[c]["out"].T
    return out.reshape(1, 16384, D)



if __name__ == "__main__":
    _selftest_quant()


# version 20
# speedup vs baseline: 1.0222x; 1.0222x over previous
import contextlib
import numpy as np
import ml_dtypes

import concourse.bass as bass
import concourse.mybir as mybir
from concourse.bass_utils import run_bass_kernel_spmd

F32 = mybir.dt.float32
BF16 = mybir.dt.bfloat16
I32 = mybir.dt.int32
AF = mybir.ActivationFunctionType
ALU = mybir.AluOpType
AX = mybir.AxisListType

NCORES = 8
D = 1024
NG = D // 128
TOWN = 2048
LB = 768
TPRE = LB + TOWN
IFF = 2816
ICM = 4096
BIG = float(np.float32(2.0 ** 100))
MID = [0.25, 0.75, 1.25, 1.75, 2.5, 3.5, 5.0]
NEG = -1e38


ENGINES = ("sync", "vector", "scalar", "tensor", "gpsimd")


class Step:
    __slots__ = ("engine", "fn", "name", "dep_idx", "need_dma", "idx", "inc",
                 "self_dep", "n_dmas", "dma_slot", "dma_prior", "dma_after")

    def __init__(self, engine, fn, name, dep_idx=None, need_dma=None, inc=1,
                 self_dep=True, n_dmas=0):
        self.engine = engine
        self.fn = fn
        self.name = name
        self.dep_idx = dep_idx
        self.need_dma = need_dma or []
        self.idx = None
        self.inc = inc
        self.self_dep = self_dep
        self.n_dmas = n_dmas


N_DSEM = {"sync": 6, "gpsimd": 4}


class Plan:
    def __init__(self):
        self.steps = []
        self.dma_tags = {}
        self.dma_count = {"sync": 0, "gpsimd": 0}
        self.dma_rr = {"sync": 0, "gpsimd": 0}
        self.dma_slot_count = {e: [0] * N_DSEM[e] for e in N_DSEM}
        self.last_store = None
        self.gser = [0]

    def step(self, engine, name, fn, dep_idx=None, need_dma=None, inc=1,
             self_dep=True, n_dmas=0):
        s = Step(engine, fn, name, dep_idx, need_dma, inc, self_dep, n_dmas)
        s.idx = len(self.steps)
        self.steps.append(s)
        return s

    def dma(self, engine, name, fn, n_dmas, dep_idx=None, tag=None, need_dma=None):
        s = self.step(engine, name, fn, dep_idx=dep_idx, need_dma=need_dma, inc=0,
                      n_dmas=n_dmas)
        self.dma_count[engine] += n_dmas
        slot = self.dma_rr[engine]
        self.dma_rr[engine] = (slot + 1) % N_DSEM[engine]
        s.dma_slot = slot
        s.dma_prior = self.dma_slot_count[engine][slot]
        self.dma_slot_count[engine][slot] += n_dmas
        s.dma_after = self.dma_slot_count[engine][slot]
        if tag:
            self.dma_tags[tag] = s
            if engine == "gpsimd":
                self.last_store = tag
        return s


class Ctx:

    def __init__(self, nc, sems):
        self.nc = nc
        self.sems = sems
        self.buf = {}
        self.dram = {}
        self.addr_of = {}
        self.cur_eng = None
        self.drain_count = {e: 0 for e in ENGINES}

    def dma_sem(self):
        return self.sems[f"d_{self.cur_eng_name}_{self.cur_dma_slot}"]

    def seq(self, eng, inst):
        e = self.cur_eng_name
        self.drain_count[e] += 1
        sem = self.sems["dr_" + e]
        inst.then_inc(sem, 1)
        eng.wait_ge(sem, self.drain_count[e])
        return inst


def emit_program(nc, plan, ctx, deps=None, capture=None):
    n = len(plan.steps)
    cum = {e: [0] * (n + 1) for e in ENGINES}
    for i, s in enumerate(plan.steps):
        for e in ENGINES:
            cum[e][i + 1] = cum[e][i] + (s.inc if s.engine == e else 0)

    sem_of = {"sync": "psync", "vector": "pv", "scalar": "ps",
              "tensor": "pt", "gpsimd": "pg"}

    slot_before = []
    running = {}
    for s in plan.steps:
        slot_before.append(dict(running))
        if s.inc == 0:
            running[f"d_{s.engine}_{s.dma_slot}"] = s.dma_after * 16

    class _Rec:
        __slots__ = ("_e", "_sink", "_ann")

        def __init__(self, e, sink, ann=None):
            self._e = e
            self._sink = sink
            self._ann = ann

        def __getattr__(self, nm):
            a = getattr(self._e, nm)
            if callable(a):
                sink = self._sink
                annv = self._ann

                def wrap(*args, **kw):
                    r = a(*args, **kw)
                    if annv is not None:
                        try:
                            r.annotate(annv)
                        except Exception:
                            pass
                    if sink is not None:
                        try:
                            mi = r.ins
                            for ap in mi.ins:
                                if hasattr(ap, "memref"):
                                    sink[0].add(ap.memref)
                            for ap in mi.outs:
                                if hasattr(ap, "memref"):
                                    sink[1].add(ap.memref)
                        except AttributeError:
                            pass
                    return r
                return wrap
            return a

    def engine_body(ename):
        def body(eng):
            ctx.cur_eng_name = ename
            last_wait = {}
            def wait_sem(semname, want):
                if want > last_wait.get(semname, 0):
                    eng.wait_ge(ctx.sems[semname], want)
                    last_wait[semname] = want

            for s in plan.steps:
                if s.engine != ename:
                    continue
                if s.inc == 0:
                    ctx.cur_dma_slot = s.dma_slot
                    wait_sem(f"d_{ename}_{s.dma_slot}", s.dma_prior * 16)
                if deps is not None:
                    for semname, want in deps.get(s.idx, {}).items():
                        wait_sem(semname, want)
                else:
                    di = s.dep_idx if s.dep_idx is not None else s.idx
                    for e in ENGINES:
                        if e == ename:
                            continue
                        wait_sem(sem_of[e], cum[e][di])
                    if s.self_dep:
                        wait_sem(sem_of[ename], cum[ename][s.idx])
                    if di == s.idx:
                        for semname, tot in slot_before[s.idx].items():
                            wait_sem(semname, tot)
                if deps is None:
                    for tg in s.need_dma:
                        st2 = plan.dma_tags[tg]
                        wait_sem(f"d_{st2.engine}_{st2.dma_slot}", st2.dma_after * 16)
                if capture is not None:
                    sink = (set(), set())
                    last = s.fn(_Rec(eng, sink), ctx)
                    capture[s.idx] = sink
                else:
                    last = s.fn(_Rec(eng, None, s.name), ctx)
                assert last is not None, f"step {s.name} returned no instruction"
                if s.inc:
                    last.then_inc(ctx.sems[sem_of[ename]], s.inc)
        return body

    with nc.Block() as blk:
        blk.sync(engine_body("sync"))
        blk.vector(engine_body("vector"))
        blk.scalar(engine_body("scalar"))
        blk.tensor(engine_body("tensor"))
        blk.gpsimd(engine_body("gpsimd"))


def compute_deps(plan, capture, addr_of=None):
    addr_of = addr_of or {}
    import bisect
    bounds = {}
    for sp, lo, hi in addr_of.values():
        bounds.setdefault(sp, set()).update((lo, hi))
    bounds = {sp: sorted(v) for sp, v in bounds.items()}
    keycache = {}

    def keys_of(nm):
        r = keycache.get(nm)
        if r is None:
            a = addr_of.get(nm)
            if a is None:
                r = (("d", nm),)
            else:
                sp, lo, hi = a
                bl = bounds[sp]
                i0 = bisect.bisect_left(bl, lo)
                i1 = bisect.bisect_left(bl, hi)
                r = tuple((sp, i) for i in range(i0, i1))
            keycache[nm] = r
        return r
    n = len(plan.steps)
    cum = {e: [0] * (n + 1) for e in ENGINES}
    for i, s in enumerate(plan.steps):
        for e in ENGINES:
            cum[e][i + 1] = cum[e][i] + (s.inc if s.engine == e else 0)
    sem_of = {"sync": "psync", "vector": "pv", "scalar": "ps",
              "tensor": "pt", "gpsimd": "pg"}

    last_writer = {}
    readers = {}
    deps = {}
    for s in plan.steps:
        i = s.idx
        reads0, writes0 = capture[i]
        reads = set()
        for nm in reads0:
            reads.update(keys_of(nm))
        writes = set()
        for nm in writes0:
            writes.update(keys_of(nm))
        dep_steps = set()
        for nm in reads:
            if nm in last_writer:
                dep_steps.add(last_writer[nm])
        for nm in writes:
            if nm in last_writer:
                dep_steps.add(last_writer[nm])
            for r in readers.get(nm, ()):
                dep_steps.add(r)
        dep_steps.discard(i)
        dv = {}
        for j in dep_steps:
            sj = plan.steps[j]
            if sj.inc == 0:
                key = f"d_{sj.engine}_{sj.dma_slot}"
                want = sj.dma_after * 16
            else:
                key = sem_of[sj.engine]
                want = cum[sj.engine][j + 1]
            if dv.get(key, 0) < want:
                dv[key] = want
        deps[i] = dv
        for nm in writes:
            last_writer[nm] = i
            readers[nm] = []
        for nm in reads:
            if nm not in writes:
                readers.setdefault(nm, []).append(i)
    return deps



def emit_fm_quant(plan, name, srcs, his, los, col0, width, npart_grp,
                  scratch_sets, psum, consts, dcol0=None, defer=False):
    if dcol0 is None:
        dcol0 = col0
    if isinstance(scratch_sets, dict):
        scratch_sets = [scratch_sets]
    if isinstance(psum, dict):
        psum_sets = [psum]
    else:
        psum_sets = psum
    gser = plan.gser
    stage_list = []

    def seg_steps(s0, sw):
        scratch = scratch_sets[0]
        psum = psum_sets[0]
        nsub = sw // 128
        nb = 8 * npart_grp

        def scale_sub(sub):
            for g in range(npart_grp):
                def tr_fn(eng, ctx, g=g, sub=sub, s0=s0):
                    return eng.transpose(
                        ctx.buf[psum["tr"]][:, 0:128],
                        srcs[g][:, col0 + s0 + sub * 128: col0 + s0 + (sub + 1) * 128],
                        ctx.buf[consts["idn"]][:, :])
                plan.step("tensor", f"{name}.tr", tr_fn)

                def red_fn(eng, ctx, g=g, sub=sub):
                    ap = ctx.buf[psum["tr"]][:, 0:128].rearrange("p (b k) -> p b k", k=16)
                    return eng.tensor_reduce(
                        ctx.buf[scratch["bm"]][:, 8 * g: 8 * g + 8], ap,
                        AX.X, ALU.max, apply_absolute_value=True)
                plan.step("vector", f"{name}.bm", red_fn)

            def rs_fn(eng, ctx):
                ctx.seq(eng, eng.tensor_scalar(ctx.buf[scratch["ss"]][:, 0:nb],
                                  ctx.buf[scratch["bm"]][:, 0:nb],
                                  1.0 / 6.0, 1e-12, ALU.mult, ALU.max))
                return eng.reciprocal(ctx.buf[scratch["rs"]][:, 0:nb],
                                      ctx.buf[scratch["ss"]][:, 0:nb])
            plan.step("vector", f"{name}.rs", rs_fn)

            def rsT_fn(eng, ctx, sub=sub):
                return eng.transpose(
                    ctx.buf[psum["rsT"]][0:nb, 0:128],
                    ctx.buf[scratch["rs"]][:, 0:nb],
                    ctx.buf[consts["idn"]][:, :])
            plan.step("tensor", f"{name}.rsT", rsT_fn)

            def rsc_fn(eng, ctx, sub=sub):
                return eng.tensor_copy(
                    ctx.buf[scratch["rsp"]][0:nb, sub * 128:(sub + 1) * 128],
                    ctx.buf[psum["rsT"]][0:nb, 0:128])
            plan.step("vector", f"{name}.rsc", rsc_fn)

        for sub in range(nsub):
            stage_list.append(lambda sub=sub: scale_sub(sub))

        def stage_exp(g, scratch, psum):
            def exp_fn(eng, ctx, g=g, scratch=scratch, psum=psum):
                return eng.matmul(ctx.buf[psum["exp"]][:, 0:sw],
                                  ctx.buf[consts["e64"]][0:nb, 128 * g:128 * (g + 1)],
                                  ctx.buf[scratch["rsp"]][0:nb, 0:sw],
                                  start=True, stop=True)
            plan.step("tensor", f"{name}.exp{g}", exp_fn)

        def stage_abs(g, scratch, psum):
            src_ap = lambda ctx, g=g: srcs[g][:, col0 + s0: col0 + s0 + sw]
            def abs_fn(eng, ctx, scratch=scratch, src_ap=src_ap):
                return eng.activation(ctx.buf[scratch["xabs"]][:, 0:sw], src_ap(ctx), AF.Abs)
            plan.step("scalar", f"{name}.abs{g}", abs_fn)

        def stage_y(g, scratch, psum):
            def y_fn(eng, ctx, scratch=scratch, psum=psum):
                return eng.tensor_tensor(ctx.buf[scratch["xabs"]][:, 0:sw],
                                         ctx.buf[scratch["xabs"]][:, 0:sw],
                                         ctx.buf[psum["exp"]][:, 0:sw], ALU.mult)
            plan.step("vector", f"{name}.y{g}", y_fn)

        def stage_sfld(g, scratch, psum):
            def sf_fn(eng, ctx, psum=psum):
                return eng.reciprocal(ctx.buf[psum["exp2"]][:, 0:sw],
                                      ctx.buf[psum["exp"]][:, 0:sw])
            plan.step("vector", f"{name}.sf{g}", sf_fn)

        def mkstep(j, dst, g, scratch):
            def f(eng, ctx, scratch=scratch):
                return eng.activation(ctx.buf[scratch[dst]][:, 0:sw],
                                      ctx.buf[scratch["xabs"]][:, 0:sw], AF.Sigmoid,
                                      bias=ctx.buf[consts["stepb"]][:, j:j + 1],
                                      scale=BIG)
            return plan.step("scalar", f"{name}.s{j}_{g}", f)

        def stage_s12(g, scratch, psum):
            mkstep(0, "c1", g, scratch)
            mkstep(1, "c2", g, scratch)

        def stage_gpa1(g, scratch, psum):
            def gpa1(eng, ctx, scratch=scratch):
                return eng.tensor_add(ctx.buf[scratch["accA"]][:, 0:sw],
                                      ctx.buf[scratch["c1"]][:, 0:sw],
                                      ctx.buf[scratch["c2"]][:, 0:sw])
            plan.step("gpsimd", f"{name}.gA1_{g}", gpa1)

        def stage_s34(g, scratch, psum):
            mkstep(2, "c1", g, scratch)
            mkstep(3, "c2", g, scratch)

        def stage_gpa2(g, scratch, psum):
            def gpa2(eng, ctx, scratch=scratch):
                return eng.tensor_add(ctx.buf[scratch["c2"]][:, 0:sw],
                                      ctx.buf[scratch["c1"]][:, 0:sw],
                                      ctx.buf[scratch["c2"]][:, 0:sw])
            plan.step("gpsimd", f"{name}.gA2_{g}", gpa2)

        def stage_s567(g, scratch, psum):
            mkstep(4, "c1", g, scratch)
            mkstep(6, "c7", g, scratch)
            mkstep(5, "xabs", g, scratch)

        def stage_comb(g, scratch, psum):
            def comb_fn(eng, ctx, scratch=scratch):
                ctx.seq(eng, eng.tensor_add(ctx.buf[scratch["accA"]][:, 0:sw],
                                            ctx.buf[scratch["accA"]][:, 0:sw],
                                            ctx.buf[scratch["c2"]][:, 0:sw]))
                ctx.seq(eng, eng.tensor_add(ctx.buf[scratch["c1"]][:, 0:sw],
                                            ctx.buf[scratch["c1"]][:, 0:sw],
                                            ctx.buf[scratch["xabs"]][:, 0:sw]))
                ctx.seq(eng, eng.scalar_tensor_tensor(ctx.buf[scratch["c2"]][:, 0:sw],
                                         ctx.buf[scratch["c7"]][:, 0:sw], 2.0,
                                         ctx.buf[scratch["c1"]][:, 0:sw],
                                         ALU.mult, ALU.add))
                return eng.scalar_tensor_tensor(ctx.buf[scratch["qabs"]][:, 0:sw],
                                                ctx.buf[scratch["accA"]][:, 0:sw], 0.5,
                                                ctx.buf[scratch["c2"]][:, 0:sw],
                                                ALU.mult, ALU.add)
            plan.step("vector", f"{name}.comb{g}", comb_fn)

        def stage_sgn(g, scratch, psum):
            src_ap = lambda ctx, g=g: srcs[g][:, col0 + s0: col0 + s0 + sw]
            def sgn_fn(eng, ctx, scratch=scratch, src_ap=src_ap):
                return eng.activation(ctx.buf[scratch["c1"]][:, 0:sw], src_ap(ctx), AF.Sign)
            plan.step("scalar", f"{name}.sgn{g}", sgn_fn)

        def stage_q(g, scratch, psum):
            def q_fn(eng, ctx, g=g, scratch=scratch, psum=psum):
                ctx.seq(eng, eng.tensor_mul(ctx.buf[scratch["qabs"]][:, 0:sw],
                               ctx.buf[scratch["qabs"]][:, 0:sw],
                               ctx.buf[scratch["c1"]][:, 0:sw]))
                return eng.tensor_tensor(ctx.buf[scratch["xabs"]][:, 0:sw],
                                  ctx.buf[scratch["qabs"]][:, 0:sw],
                                  ctx.buf[psum["exp2"]][:, 0:sw], ALU.mult)
            plan.step("vector", f"{name}.q{g}", q_fn)

        def stage_split(g, scratch, psum):
            def sp_fn(eng, ctx, g=g, scratch=scratch):
                ctx.seq(eng, eng.tensor_copy(his[g][:, dcol0 + s0: dcol0 + s0 + sw],
                                ctx.buf[scratch["xabs"]][:, 0:sw]))
                return eng.tensor_tensor(los[g][:, dcol0 + s0: dcol0 + s0 + sw],
                                         ctx.buf[scratch["xabs"]][:, 0:sw],
                                         his[g][:, dcol0 + s0: dcol0 + s0 + sw],
                                         ALU.subtract)
            plan.step("gpsimd", f"{name}.sp{g}", sp_fn)

        stages = [stage_exp, stage_abs, stage_y, stage_sfld, stage_s12,
                  stage_gpa1, stage_s34, stage_gpa2, stage_s567, stage_comb,
                  stage_sgn, stage_q, stage_split]
        for g0 in range(0, npart_grp, 3):
            gs = [g for g in (g0, g0 + 1, g0 + 2) if g < npart_grp]
            assign = {}
            for g in gs:
                assign[g] = (scratch_sets[gser[0] % len(scratch_sets)],
                             psum_sets[gser[0] % len(psum_sets)])
                gser[0] += 1
            for stg in stages:
                def emit_stage(stg=stg, gs=gs, assign=assign):
                    for g in gs:
                        stg(g, assign[g][0], assign[g][1])
                stage_list.append(emit_stage)

    assert width <= 512
    seg_steps(0, width)
    if defer:
        return stage_list
    for f in stage_list:
        f()
    return None



def _np_fp4(x):
    GRID = np.array([0.0, 0.5, 1.0, 1.5, 2.0, 3.0, 4.0, 6.0], np.float32)
    sh = x.shape
    xb = x.reshape(sh[:-1] + (sh[-1] // 16, 16))
    sc = np.maximum(np.abs(xb).max(-1, keepdims=True) / 6.0, np.float32(1e-12)).astype(np.float32)
    xn = (xb / sc).astype(np.float32)
    idx = np.searchsorted(MID, np.abs(xn))
    q = GRID[idx] * np.sign(xn) * sc
    return q.reshape(sh).astype(np.float32)


def build_quant_test(wT):
    K, M = wT.shape
    ng = K // 128
    nc = bass.Bass("TRN2", target_bir_lowering=False)
    win = nc.dram_tensor("w", [K, M], F32, kind="ExternalInput").ap()
    idn_in = nc.dram_tensor("d_idn", [128, 128], F32, kind="ExternalInput").ap()
    e8_in = nc.dram_tensor("d_e64", [64, 1024], F32, kind="ExternalInput").ap()
    sb_in = nc.dram_tensor("d_stepb", [128, 7], F32, kind="ExternalInput").ap()
    qout = nc.dram_tensor("qo", [K, M], F32, kind="ExternalOutput").ap()

    plan = Plan()
    with contextlib.ExitStack() as st:
        sems = {}
        semnames = ["psync", "pv", "ps", "pt", "pg",
                    "dr_sync", "dr_vector", "dr_scalar", "dr_tensor", "dr_gpsimd"]
        semnames += [f"d_sync_{i}" for i in range(N_DSEM["sync"])]
        semnames += [f"d_gpsimd_{i}" for i in range(N_DSEM["gpsimd"])]
        for sname in semnames:
            sems[sname] = st.enter_context(nc.semaphore(sname))
        ctx = Ctx(nc, sems)

        def sb(nm, shape, dt=F32):
            ctx.buf[nm] = st.enter_context(nc.sbuf_tensor(nm, shape, dt))
            return nm

        def ps(nm, shape):
            ctx.buf[nm] = st.enter_context(nc.psum_tensor(nm, shape, F32))
            return nm

        wt = [sb(f"wt{g}", [128, M]) for g in range(ng)]
        hi = [sb(f"hi{g}", [128, M], BF16) for g in range(ng)]
        lo = [sb(f"lo{g}", [128, M], BF16) for g in range(ng)]
        wt_h = [ctx.buf[f"wt{g}"] for g in range(ng)]
        hi_h = [ctx.buf[f"hi{g}"] for g in range(ng)]
        lo_h = [ctx.buf[f"lo{g}"] for g in range(ng)]
        qf = [sb(f"qf{g}", [128, M]) for g in range(ng)]
        qsc2 = []
        for si in range(3):
            d = {}
            key = f"xabs.{si}"
            d["xabs"] = key; sb(key, [128, 512])
            for nm in ["c1", "c2", "accA", "c7", "qabs"]:
                key = f"{nm}.{si}"
                d[nm] = key; sb(key, [128, 512], BF16)
            qsc2.append(d)
        for nm, shp in [("bm", [128, 64]), ("rs", [128, 64]), ("ss", [128, 64]),
                        ("rsp", [64, 512]), ("ssp", [64, 512])]:
            for d in qsc2:
                d[nm] = nm
            sb(nm, shp)
        sb("idn", [128, 128])
        sb("e64", [64, 1024])
        sb("stepb", [128, 7])
        for pnm in ["p_tr", "p_rsT", "p_exp", "p_exp2", "p_a", "p_b", "p_c", "p_d"]:
            ps(pnm, [128, 512])

        def ld(eng, ctx):
            for g in range(ng):
                eng.dma_start(out=ctx.buf[f"wt{g}"][:, :], in_=win[g * 128:(g + 1) * 128, :]).then_inc(ctx.dma_sem(), 16)
            eng.dma_start(out=ctx.buf["idn"][:, :], in_=idn_in).then_inc(ctx.dma_sem(), 16)
            eng.dma_start(out=ctx.buf["e64"][:, :], in_=e8_in).then_inc(ctx.dma_sem(), 16)
            i = eng.dma_start(out=ctx.buf["stepb"][:, :], in_=sb_in)
            i.then_inc(ctx.dma_sem(), 16)
            return i
        plan.dma("sync", "load", ld, ng + 3, tag="in")

        def nopv(eng, ctx):
            return eng.tensor_copy(ctx.buf["xabs.0"][:, 0:1], ctx.buf["wt0"][:, 0:1])
        plan.step("vector", "gate", nopv, need_dma=["in"])

        psets = [{"tr": "p_tr", "rsT": "p_rsT", "exp": "p_exp", "exp2": "p_exp2"},
                 {"tr": "p_tr", "rsT": "p_rsT", "exp": "p_a", "exp2": "p_b"},
                 {"tr": "p_tr", "rsT": "p_rsT", "exp": "p_c", "exp2": "p_d"}]
        consts = {"idn": "idn", "e64": "e64", "stepb": "stepb"}
        for c0 in range(0, M, 512):
            emit_fm_quant(plan, f"q{c0}", wt_h, hi_h, lo_h, c0, min(512, M - c0), ng,
                          qsc2, psets, consts)

        for g in range(ng):
            def deq(eng, ctx, g=g):
                return eng.tensor_add(ctx.buf[f"qf{g}"][:, :], ctx.buf[f"hi{g}"][:, :], ctx.buf[f"lo{g}"][:, :])
            plan.step("vector", f"deq{g}", deq)

        def store(eng, ctx):
            for g in range(ng):
                i = eng.dma_start(out=qout[g * 128:(g + 1) * 128, :], in_=ctx.buf[f"qf{g}"][:, :])
                i.then_inc(ctx.dma_sem(), 16)
            return i
        plan.dma("gpsimd", "store", store, ng, tag="out")

        def fin(eng, ctx):
            for e in N_DSEM:
                for k in range(N_DSEM[e]):
                    tot = plan.dma_slot_count[e][k] * 16
                    if tot:
                        eng.wait_ge(ctx.sems[f"d_{e}_{k}"], tot)
            return eng.tensor_copy(ctx.buf["xabs.0"][:, 0:1], ctx.buf["xabs.0"][:, 0:1])
        plan.step("vector", "fin", fin)

        emit_program(nc, plan, ctx)
    return nc


def host_consts():
    idn = np.eye(128, dtype=np.float32)
    e8 = np.zeros((64, 1024), np.float32)
    for g in range(8):
        for c in range(128):
            e8[8 * g + c // 16, 128 * g + c] = 1.0
    stepb = np.tile(np.float32([-BIG * np.float32(t) for t in MID]), (128, 1)).astype(np.float32)
    return idn, e8, stepb


def _selftest_quant():
    rng = np.random.default_rng(0)
    M = 512
    w = (rng.standard_normal((M, 1024)) * 0.02).astype(np.float32)
    wT = np.ascontiguousarray(w.T)
    nc = build_quant_test(wT)
    idn, e8, stepb = host_consts()
    res = run_bass_kernel_spmd(nc, [{"w": wT, "d_idn": idn, "d_e64": e8, "d_stepb": stepb}], core_ids=[0])
    got = res.results[0]["qo"]
    ref = _np_fp4(w).T
    d = np.abs(got - ref)
    rel = np.abs(got - ref).max() / np.abs(ref).max()
    nbad = (d > 1e-5 * np.maximum(np.abs(ref), 1e-3)).sum()
    print(f"quant selftest: max_abs={d.max():.3e} rel={rel:.3e} nbad={nbad} / {ref.size}")
    bad = np.argwhere(d > 1e-3)
    for i, j in bad[:5]:
        print("  at", i, j, "got", got[i, j], "ref", ref[i, j], "w", wT[i, j])



FFN1_TILES = [(0, 256)] + [(256 + 512 * i, 512) for i in range(5)]
ATTN_TILES = [(0, 256)] + [(256 + 512 * i, 512) for i in range(5)]
CM_TILES = [(256 + 512 * i, 512) for i in range(5)]
FFN2_TILES = [(768 + 512 * i, 512) for i in range(4)]
NM_FF = IFF // 128
NOG_CM = ICM // 128


def build_full(phases=("wprep", "ffn1", "attn", "cm", "ffn2"), dump=None,
               deps=None, capture=None):
    nc = bass.Bass("TRN2", target_bir_lowering=False)
    dr = {}

    def din(name, shape, dt=F32):
        dr[name] = nc.dram_tensor("i_" + name, shape, dt, kind="ExternalInput").ap()
        return dr[name]

    def dint(name, shape, dt=F32):
        dr[name] = nc.dram_tensor("t_" + name, shape, dt, kind="Internal").ap()
        return dr[name]

    x_in = din("x", [D, TPRE])
    kmask_in = din("kmask", [128, TPRE])
    for w, sh in [("wk", [D, D]), ("wv", [D, D]), ("wr", [D, D]), ("wo", [D, D]),
                  ("wck", [D, ICM]), ("wcv", [ICM, D]), ("wcr", [D, D]),
                  ("wg_t", [D, IFF]), ("wu_t", [D, IFF]), ("wd_t", [IFF, D])]:
        din(w, sh)
    for p in ["ln0g", "ln0b", "ln1g", "ln1b", "tmk", "tmv", "tmr", "cmk", "cmr",
              "td", "tf"]:
        din(p, [128, NG])
    din("idn", [128, 128])
    din("e64", [64, 1024])
    din("stepb", [128, 7])
    din("onescol", [128, 1])
    din("browones", [1, 128])
    din("epsc", [128, 1])
    din("h0m", [128, 1])
    out = nc.dram_tensor("out", [D, TOWN], F32, kind="ExternalOutput").ap()

    sA = dint("sA", [D, TPRE])
    sB = dint("sB", [D, TPRE])
    spills = {}
    for wname, n0, M in [("wk", D // 128, D), ("wv", D // 128, D), ("wr", D // 128, D),
                         ("wo", D // 128, D), ("wck", D // 128, ICM),
                         ("wcv", ICM // 128, D), ("wcr", D // 128, D),
                         ("qg", D // 128, IFF), ("qu", D // 128, IFF),
                         ("qd", IFF // 128, D)]:
        spills[wname] = (
            nc.dram_tensor("sp_" + wname + "_h", [n0, 128, M], BF16, kind="Internal").ap(),
            nc.dram_tensor("sp_" + wname + "_l", [n0, 128, M], BF16, kind="Internal").ap(),
        )

    plan = Plan()
    tagn = [0]

    def tag():
        tagn[0] += 1
        return f"g{tagn[0]}"

    with contextlib.ExitStack() as st:
        sems = {}
        semnames = ["psync", "pv", "ps", "pt", "pg",
                    "dr_sync", "dr_vector", "dr_scalar", "dr_tensor", "dr_gpsimd"]
        semnames += [f"d_sync_{i}" for i in range(N_DSEM["sync"])]
        semnames += [f"d_gpsimd_{i}" for i in range(N_DSEM["gpsimd"])]
        for sname in semnames:
            sems[sname] = st.enter_context(nc.semaphore(sname))
        ctx = Ctx(nc, sems)

        nser = [0]

        def sb(stk, nm, shape, dt=F32):
            nser[0] += 1
            h = stk.enter_context(nc.sbuf_tensor(f"{nm}__{nser[0]}", shape, dt))
            ctx.buf[nm] = h
            ml = nc.lookup_mloc(h)
            npart = max(1, shape[0])
            ctx.addr_of[h.name] = ("sb", ml.addr, ml.addr + ml.size() // npart)
            return h

        def ps(stk, nm, shape=None):
            h = stk.enter_context(nc.psum_tensor(nm, shape or [128, 512], F32))
            ctx.buf[nm] = h
            ml = nc.lookup_mloc(h)
            sh = shape or [128, 512]
            ctx.addr_of[h.name] = ("ps", ml.addr, ml.addr + ml.size() // max(1, sh[0]))
            return h

        P = st
        idn = sb(P, "idn", [128, 128]); e64 = sb(P, "e64", [64, 1024])
        stepb = sb(P, "stepb", [128, 7]); onescol = sb(P, "onescol", [128, 1])
        brow = sb(P, "browones", [1, 128]); epsc = sb(P, "epsc", [128, 1])
        h0m = sb(P, "h0m", [128, 1])
        par = {p: sb(P, "p_" + p, [128, NG]) for p in
               ["ln0g", "ln0b", "ln1g", "ln1b", "tmk", "tmv", "tmr", "cmk", "cmr",
                "td", "tf"]}
        lam = sb(P, "lam", [128, NG]); eu = sb(P, "eu", [128, NG])
        st_a = sb(P, "st_a", [128, NG]); st_b = sb(P, "st_b", [128, NG])
        s_prev = sb(P, "s_prev", [128, NG]); s_cur = sb(P, "s_cur", [128, NG])
        ns_cur = sb(P, "ns_cur", [128, NG]); resc = sb(P, "resc", [128, NG])
        _rw = {nm: sb(P, "row_" + nm, [1, 512]) for nm in
               ["m", "A", "al", "t0", "t1", "sd"]}
        rows = {"m": _rw["m"], "A": _rw["A"], "al": _rw["al"], "sd": _rw["sd"],
                "sum": _rw["t0"], "sumsq": _rw["t1"], "msq": _rw["t1"],
                "var": _rw["t1"], "t2": _rw["t0"]}
        qsc2 = []
        for si in range(3):
            d = {}
            key = f"xabs.{si}"
            d["xabs"] = key; sb(P, key, [128, 512])
            for nm in ["c1", "c2", "accA", "c7", "qabs"]:
                key = f"{nm}.{si}"
                d[nm] = key; sb(P, key, [128, 512], BF16)
            qsc2.append(d)
        for nm, shp in [("bm", [128, 64]), ("rs", [128, 64]), ("ss", [128, 64]),
                        ("rsp", [64, 512]), ("ssp", [64, 512])]:
            for d in qsc2:
                d[nm] = nm
            sb(P, nm, shp)
        qsc = qsc2[0]
        sqs = sb(P, "sqs", [128, 512])
        consts = {"idn": "idn", "e64": "e64", "stepb": "stepb"}
        p_tr = ps(P, "p_tr"); p_rsT = ps(P, "p_rsT")
        p_exp = ps(P, "p_exp"); p_exp2 = ps(P, "p_exp2")
        p_a = ps(P, "p_a"); p_b = ps(P, "p_b")
        p_c = ps(P, "p_c"); p_d = ps(P, "p_d")
        qps = {"tr": "p_tr", "rsT": "p_rsT", "exp": "p_exp", "exp2": "p_exp2"}
        ogbank = [p_tr, p_rsT, p_exp, p_exp2, p_a, p_b, p_c, p_d]

        def ld_consts(eng, ctx):
            for nm, dst in [("idn", idn), ("e64", e64), ("stepb", stepb),
                            ("onescol", onescol), ("browones", brow), ("epsc", epsc),
                            ("h0m", h0m)]:
                eng.dma_start(out=dst[:, :], in_=dr[nm]).then_inc(ctx.dma_sem(), 16)
            last = None
            for p in par:
                last = eng.dma_start(out=par[p][:, :], in_=dr[p])
                last.then_inc(ctx.dma_sem(), 16)
            return last
        plan.dma("sync", "ldconsts", ld_consts, 7 + len(par), tag="consts")

        def derive(eng, ctx):
            ctx.seq(eng, eng.activation(lam[:, :], par["td"][:, :], AF.Exp))
            ctx.seq(eng, eng.activation(lam[:, :], lam[:, :], AF.Exp, scale=-1.0))
            return eng.activation(eu[:, :], par["tf"][:, :], AF.Exp)
        plan.step("scalar", "derive", derive, need_dma=["consts"])

        def init_state(eng, ctx):
            eng.memset(st_a[:, :], 0.0)
            eng.memset(st_b[:, :], 0.0)
            return eng.memset(s_prev[:, :], -1e30)
        plan.step("vector", "initstate", init_state)

        split_jobs = []
        if "wprep" in phases:
            def mk_split_job(wname, kg, cc0, cw):
                hi_d, lo_d = spills[wname]

                def emit(pidx):
                    t_in = tag()
                    def ldw(eng, ctx, wname=wname, kg=kg, cc0=cc0, cw=cw, pidx=pidx):
                        i = eng.dma_start(out=ctx.buf[f"wsf{pidx}"][:, 0:cw],
                                          in_=dr[wname][kg * 128:(kg + 1) * 128, cc0:cc0 + cw])
                        i.then_inc(ctx.dma_sem(), 16)
                        return i
                    plan.dma("gpsimd", f"ld_{wname}{kg}_{cc0}", ldw, 1, tag=t_in)

                    def split(eng, ctx, cw=cw, pidx=pidx):
                        ctx.seq(eng, eng.tensor_copy(ctx.buf[f"wsh{pidx}"][:, 0:cw],
                                                     ctx.buf[f"wsf{pidx}"][:, 0:cw]))
                        return eng.tensor_tensor(ctx.buf[f"wsl{pidx}"][:, 0:cw],
                                                 ctx.buf[f"wsf{pidx}"][:, 0:cw],
                                                 ctx.buf[f"wsh{pidx}"][:, 0:cw],
                                                 ALU.subtract)
                    plan.step("gpsimd", f"sp_{wname}{kg}_{cc0}", split, need_dma=[t_in])

                    def stw(eng, ctx, hi_d=hi_d, lo_d=lo_d, kg=kg, cc0=cc0, cw=cw, pidx=pidx):
                        eng.dma_start(out=hi_d[kg, :, cc0:cc0 + cw],
                                      in_=ctx.buf[f"wsh{pidx}"][:, 0:cw]).then_inc(ctx.dma_sem(), 16)
                        i = eng.dma_start(out=lo_d[kg, :, cc0:cc0 + cw],
                                          in_=ctx.buf[f"wsl{pidx}"][:, 0:cw])
                        i.then_inc(ctx.dma_sem(), 16)
                        return i
                    plan.dma("gpsimd", f"st_{wname}{kg}_{cc0}", stw, 2)
                return emit

            for wname, K, M in [("wk", D, D), ("wv", D, D), ("wr", D, D),
                                ("wo", D, D), ("wck", D, ICM),
                                ("wcv", ICM, D), ("wcr", D, D)]:
                for kg in range(K // 128):
                    for cc0 in range(0, M, 1024):
                        split_jobs.append(mk_split_job(wname, kg, cc0, min(1024, M - cc0)))

            with contextlib.ExitStack() as W:
                qsrc = [[sb(W, f"qw{c}_{g}", [128, 512]) for g in range(8)]
                        for c in range(2)]
                qhi = [[sb(W, f"qwh{c}_{g}", [128, 512], BF16) for g in range(8)]
                       for c in range(2)]
                qlo = [[sb(W, f"qwl{c}_{g}", [128, 512], BF16) for g in range(8)]
                       for c in range(2)]
                chn = [0]
                prev_store = [plan.last_store]

                def quant_one(wname, spname, K, M):
                    nslab = (K // 128 + 7) // 8
                    hi_d, lo_d = spills[spname]
                    full_slab = (nslab == 1)
                    for sl in range(nslab):
                        g0 = sl * 8
                        ngs = min(8, K // 128 - g0)
                        for c0 in range(0, M, 512):
                            cw = min(512, M - c0)
                            cc = chn[0] % 2
                            chn[0] += 1
                            t_in = tag()
                            def ldw(eng, ctx, wname=wname, g0=g0, ngs=ngs, c0=c0,
                                    cw=cw, cc=cc):
                                last = None
                                for gg in range(ngs):
                                    last = eng.dma_start(
                                        out=qsrc[cc][gg][:, 0:cw],
                                        in_=dr[wname][(g0 + gg) * 128:(g0 + gg + 1) * 128,
                                                      c0:c0 + cw])
                                    last.then_inc(ctx.dma_sem(), 16)
                                return last
                            plan.dma("sync", f"ldq_{wname}{sl}_{c0}", ldw, ngs,
                                     tag=t_in)
                            emit_fm_quant(plan, f"wq_{wname}{sl}_{c0}",
                                          qsrc[cc][:ngs], qhi[cc][:ngs], qlo[cc][:ngs],
                                          0, cw, ngs, qsc2,
                                          [qps, {"tr": "p_tr", "rsT": "p_rsT",
                                                 "exp": "p_a", "exp2": "p_b"},
                                           {"tr": "p_tr", "rsT": "p_rsT",
                                            "exp": "p_c", "exp2": "p_d"}],
                                          consts, dcol0=0)
                            t_out = tag()
                            def stw(eng, ctx, hi_d=hi_d, lo_d=lo_d, g0=g0,
                                    ngs=ngs, c0=c0, cw=cw, cc=cc):
                                last = None
                                for gg in range(ngs):
                                    eng.dma_start(out=hi_d[g0 + gg, :, c0:c0 + cw],
                                                  in_=qhi[cc][gg][:, 0:cw]).then_inc(ctx.dma_sem(), 16)
                                    last = eng.dma_start(out=lo_d[g0 + gg, :, c0:c0 + cw],
                                                         in_=qlo[cc][gg][:, 0:cw])
                                    last.then_inc(ctx.dma_sem(), 16)
                                return last
                            plan.dma("gpsimd", f"stq_{wname}{sl}_{c0}", stw,
                                     2 * ngs, tag=t_out)
                            prev_store[0] = t_out

                quant_one("wg_t", "qg", D, IFF)
                quant_one("wu_t", "qu", D, IFF)
                quant_one("wd_t", "qd", IFF, D)

        def ln_stats(name, xt, W, rows_extra=False, need=None,
                     bk=("p_a", "p_b", "p_c", "p_d", "p_exp")):
            def mm_sum(eng, ctx):
                last = None
                for g in range(NG):
                    last = eng.matmul(ctx.buf[bk[0]][0:1, 0:W], onescol[:, 0:1],
                                      xt[g][:, 0:W], start=(g == 0), stop=(g == NG - 1))
                return last
            plan.step("tensor", f"{name}.sum", mm_sum, need_dma=need or [])
            for g in range(NG):
                def sq_fn(eng, ctx, g=g):
                    return eng.activation(sqs[:, 0:W], xt[g][:, 0:W], AF.Square)
                plan.step("scalar", f"{name}.sq{g}", sq_fn)
                def mm_sq(eng, ctx, g=g):
                    return eng.matmul(ctx.buf["p_b"][0:1, 0:W], onescol[:, 0:1],
                                      sqs[:, 0:W], start=(g == 0),
                                      stop=(g == NG - 1))
                plan.step("tensor", f"{name}.msq{g}", mm_sq)

            def row_math(eng, ctx):
                r = rows
                ctx.seq(eng, eng.tensor_scalar_mul(r["m"][0:1, 0:W], ctx.buf[bk[0]][0:1, 0:W], 1.0 / D))
                ctx.seq(eng, eng.tensor_scalar_mul(r["msq"][0:1, 0:W], ctx.buf[bk[1]][0:1, 0:W], 1.0 / D))
                ctx.seq(eng, eng.tensor_mul(r["sd"][0:1, 0:W], r["m"][0:1, 0:W], r["m"][0:1, 0:W]))
                return eng.tensor_sub(r["var"][0:1, 0:W], r["msq"][0:1, 0:W], r["sd"][0:1, 0:W])
            plan.step("vector", f"{name}.rows", row_math)

            def row_sd(eng, ctx):
                return eng.activation(rows["sd"][0:1, 0:W], rows["var"][0:1, 0:W],
                                      AF.Sqrt, bias=epsc[0:1, 0:1])
            plan.step("scalar", f"{name}.sd", row_sd)

            def row_al(eng, ctx):
                return eng.reciprocal(rows["al"][0:1, 0:W], rows["sd"][0:1, 0:W])
            plan.step("vector", f"{name}.al", row_al)

            if rows_extra:
                def row_t2(eng, ctx):
                    ctx.seq(eng, eng.tensor_mul(rows["t2"][0:1, 0:W], rows["al"][0:1, 0:W], rows["al"][0:1, 0:W]))
                    return eng.tensor_mul(rows["t2"][0:1, 0:W], rows["t2"][0:1, 0:W],
                                          rows["var"][0:1, 0:W])
                plan.step("vector", f"{name}.t2", row_t2)
                def row_sd2(eng, ctx):
                    return eng.activation(rows["sd"][0:1, 0:W], rows["t2"][0:1, 0:W],
                                          AF.Sqrt, bias=epsc[0:1, 0:1])
                plan.step("scalar", f"{name}.sd2", row_sd2)
                def row_A(eng, ctx):
                    ctx.seq(eng, eng.reciprocal(rows["A"][0:1, 0:W], rows["sd"][0:1, 0:W]))
                    return eng.tensor_mul(rows["A"][0:1, 0:W], rows["A"][0:1, 0:W],
                                          rows["al"][0:1, 0:W])
                plan.step("vector", f"{name}.A", row_A)
            else:
                def row_A(eng, ctx):
                    return eng.tensor_copy(rows["A"][0:1, 0:W], rows["al"][0:1, 0:W])
                plan.step("vector", f"{name}.A", row_A)

            def bcast(eng, ctx):
                ctx.seq(eng, eng.matmul(ctx.buf[bk[2]][:, 0:W], brow[0:1, :],
                                        rows["m"][0:1, 0:W], start=True, stop=True))
                if rows_extra:
                    ctx.seq(eng, eng.matmul(ctx.buf[bk[4]][:, 0:W], brow[0:1, :],
                                            rows["al"][0:1, 0:W], start=True, stop=True))
                return eng.matmul(ctx.buf[bk[3]][:, 0:W], brow[0:1, :],
                                  rows["A"][0:1, 0:W], start=True, stop=True)
            plan.step("tensor", f"{name}.bc", bcast)

        def ln_apply(name, xt, dstf, W, gamma=None, beta=None, dst_off=0,
                     ybuf=None, bk=("p_a", "p_b", "p_c", "p_d", "p_exp")):
            for g in range(NG):
                def ap_fn(eng, ctx, g=g):
                    ctx.seq(eng, eng.tensor_tensor(sqs[:, 0:W], xt[g][:, 0:W],
                                                   ctx.buf[bk[2]][:, 0:W], ALU.subtract))
                    if ybuf is not None:
                        ctx.seq(eng, eng.tensor_tensor(ybuf[g][:, 0:W], sqs[:, 0:W],
                                                       ctx.buf[bk[4]][:, 0:W], ALU.mult))
                    if gamma is None:
                        return eng.tensor_tensor(dstf[g][:, dst_off:dst_off + W],
                                                 sqs[:, 0:W],
                                                 ctx.buf[bk[3]][:, 0:W], ALU.mult)
                    ctx.seq(eng, eng.tensor_tensor(sqs[:, 0:W],
                                                   sqs[:, 0:W],
                                                   ctx.buf[bk[3]][:, 0:W], ALU.mult))
                    return eng.tensor_scalar(dstf[g][:, dst_off:dst_off + W],
                                             sqs[:, 0:W],
                                             gamma[:, g:g + 1], beta[:, g:g + 1],
                                             ALU.mult, ALU.add)
                plan.step("vector", f"{name}.ap{g}", ap_fn)

        def ffn_phase(name, tiles, src_stream, dst_stream, out_coloff=None,
                      side_jobs=None):
            with contextlib.ExitStack() as S:
                if side_jobs:
                    for pidx in range(2):
                        sb(S, f"wsf{pidx}", [128, 1024])
                        sb(S, f"wsh{pidx}", [128, 1024], BF16)
                        sb(S, f"wsl{pidx}", [128, 1024], BF16)
                xt = [sb(S, f"{name}_x{g}", [128, 512]) for g in range(NG)]
                xc = [sb(S, f"{name}_c{g}", [128, 512]) for g in range(NG)]
                xqh = [sb(S, f"{name}_qh{g}", [128, 512], BF16) for g in range(NG)]
                xql = [sb(S, f"{name}_ql{g}", [128, 512], BF16) for g in range(NG)]
                hqh = [sb(S, f"{name}_hh{m}", [128, 512], BF16) for m in range(NM_FF)]
                hql = [sb(S, f"{name}_hl{m}", [128, 512], BF16) for m in range(NM_FF)]
                wga = [sb(S, f"wga{i}", [128, 1024], BF16) for i in range(2)]
                wgb = [sb(S, f"wgb{i}", [128, 1024], BF16) for i in range(2)]
                wua = [sb(S, f"wua{i}", [128, 1024], BF16) for i in range(2)]
                wub = [sb(S, f"wub{i}", [128, 1024], BF16) for i in range(2)]
                wda = [sb(S, f"wda{i}", [128, 2816], BF16) for i in range(2)]
                wdb = [sb(S, f"wdb{i}", [128, 2816], BF16) for i in range(2)]
                x2t = [sb(S, f"x2t{i}", [128, 512]) for i in range(2)]
                sgsr = [sb(S, f"sgsr{i}", [128, 512]) for i in range(2)]
                hscr = sgsr
                prev_x2store = [plan.last_store]
                jobs_left = list(side_jobs or [])
                per_tile = (len(jobs_left) + len(tiles) - 1) // max(1, len(tiles))
                jpidx = [0]

                for (c0, W) in tiles:
                    for _ in range(min(per_tile, len(jobs_left))):
                        jobs_left.pop(0)(jpidx[0] % 2)
                        jpidx[0] += 1
                    t_x = tag()
                    def ld_x(eng, ctx, c0=c0, W=W):
                        last = None
                        for g in range(NG):
                            last = eng.dma_start(out=xt[g][:, 0:W],
                                                 in_=src_stream[g * 128:(g + 1) * 128, c0:c0 + W])
                            last.then_inc(ctx.dma_sem(), 16)
                        return last
                    plan.dma("sync", f"{name}.ldx{c0}", ld_x, NG, tag=t_x,
                             need_dma=[prev_x2store[0]] if prev_x2store[0] else [])

                    ln_stats(f"{name}.ln{c0}", xt, W, rows_extra=False, need=[t_x])
                    ln_apply(f"{name}.lna{c0}", xt, xc, W)
                    emit_fm_quant(plan, f"{name}.xq{c0}", xc, xqh, xql, 0, W, NG,
                                  qsc2, [qps, {"tr": "p_tr", "rsT": "p_rsT",
                                               "exp": "p_a", "exp2": "p_b"},
                                         {"tr": "p_tr", "rsT": "p_rsT",
                                          "exp": "p_c", "exp2": "p_d"}],
                                  consts, dcol0=0)

                    def emit_mm_gu(m):
                        p = m % 2
                        gbank, ubank = ("p_a", "p_b") if p == 0 else ("p_c", "p_d")
                        t_w = tag()
                        def ld_w(eng, ctx, m=m, p=p):
                            last = None
                            for nmw, dsts in (("qg", (wga[p], wgb[p])), ("qu", (wua[p], wub[p]))):
                                hi_d, lo_d = spills[nmw]
                                src_h = hi_d[:, :, m * 128:(m + 1) * 128].transpose([1, 0, 2])
                                src_l = lo_d[:, :, m * 128:(m + 1) * 128].transpose([1, 0, 2])
                                eng.dma_start(out=dsts[0][:, :].rearrange("p (k c) -> p k c", c=128),
                                              in_=src_h).then_inc(ctx.dma_sem(), 16)
                                last = eng.dma_start(out=dsts[1][:, :].rearrange("p (k c) -> p k c", c=128),
                                                     in_=src_l)
                                last.then_inc(ctx.dma_sem(), 16)
                            return last
                        plan.dma("sync", f"{name}.ldw{c0}_{m}", ld_w, 4, tag=t_w)

                        def mm_gu(eng, ctx, W=W, p=p, gbank=gbank, ubank=ubank):
                            last = None
                            for (pbank, wa, wb) in ((gbank, wga[p], wgb[p]),
                                                    (ubank, wua[p], wub[p])):
                                for kg in range(NG):
                                    sl = slice(kg * 128, (kg + 1) * 128)
                                    eng.matmul(ctx.buf[pbank][:, 0:W], wa[:, sl], xqh[kg][:, 0:W],
                                               start=(kg == 0), stop=False)
                                    eng.matmul(ctx.buf[pbank][:, 0:W], wa[:, sl], xql[kg][:, 0:W],
                                               start=False, stop=False)
                                    last = eng.matmul(ctx.buf[pbank][:, 0:W], wb[:, sl], xqh[kg][:, 0:W],
                                                      start=False, stop=(kg == NG - 1))
                            return last
                        plan.step("tensor", f"{name}.gu{c0}_{m}", mm_gu, need_dma=[t_w])

                    def emit_post(m):
                        p = m % 2
                        ps3 = m % 3
                        gbank, ubank = ("p_a", "p_b") if p == 0 else ("p_c", "p_d")
                        def silu_fn(eng, ctx, W=W, p=p, gbank=gbank):
                            return eng.activation(sgsr[p][:, 0:W], ctx.buf[gbank][:, 0:W], AF.Sigmoid)
                        plan.step("scalar", f"{name}.silu{c0}_{m}", silu_fn)

                        def h_fn(eng, ctx, W=W, p=p, gbank=gbank, ubank=ubank):
                            ctx.seq(eng, eng.tensor_tensor(sgsr[p][:, 0:W], sgsr[p][:, 0:W],
                                                           ctx.buf[gbank][:, 0:W], ALU.mult))
                            return eng.tensor_tensor(sgsr[p][:, 0:W], sgsr[p][:, 0:W],
                                                     ctx.buf[ubank][:, 0:W], ALU.mult)
                        plan.step("vector", f"{name}.h{c0}_{m}", h_fn)

                        return emit_fm_quant(plan, f"{name}.hq{c0}_{m}", [hscr[p]],
                                             [hqh[m]], [hql[m]], 0, W, 1, qsc2[ps3],
                                             qps, consts, dcol0=0, defer=True)

                    pending = []
                    for m in range(NM_FF):
                        emit_mm_gu(m)
                        for qch in pending:
                            k = 8
                            for f in qch[:k]:
                                f()
                            del qch[:k]
                        pending = [q for q in pending if q]
                        if m > 0:
                            pending.append(emit_post(m - 1))
                    for qch in pending:
                        for f in qch:
                            f()
                    for f in emit_post(NM_FF - 1):
                        f()

                    for og in range(NG):
                        p = og % 2
                        dbank = "p_a" if p == 0 else "p_b"
                        t_w = tag()
                        def ld_wd(eng, ctx, og=og, p=p):
                            hi_d, lo_d = spills["qd"]
                            src_h = hi_d[:, :, og * 128:(og + 1) * 128].transpose([1, 0, 2])
                            src_l = lo_d[:, :, og * 128:(og + 1) * 128].transpose([1, 0, 2])
                            eng.dma_start(out=wda[p][:, :].rearrange("p (k c) -> p k c", c=128),
                                          in_=src_h).then_inc(ctx.dma_sem(), 16)
                            last = eng.dma_start(out=wdb[p][:, :].rearrange("p (k c) -> p k c", c=128),
                                                 in_=src_l)
                            last.then_inc(ctx.dma_sem(), 16)
                            return last
                        plan.dma("sync", f"{name}.ldwd{c0}_{og}", ld_wd, 2, tag=t_w)

                        def mm_dn(eng, ctx, W=W, p=p, dbank=dbank):
                            last = None
                            for m in range(NM_FF):
                                sl = slice(m * 128, (m + 1) * 128)
                                eng.matmul(ctx.buf[dbank][:, 0:W], wda[p][:, sl], hqh[m][:, 0:W],
                                           start=(m == 0), stop=False)
                                eng.matmul(ctx.buf[dbank][:, 0:W], wda[p][:, sl], hql[m][:, 0:W],
                                           start=False, stop=False)
                                last = eng.matmul(ctx.buf[dbank][:, 0:W], wdb[p][:, sl], hqh[m][:, 0:W],
                                                  start=False, stop=(m == NM_FF - 1))
                            return last
                        plan.step("tensor", f"{name}.dn{c0}_{og}", mm_dn, need_dma=[t_w])

                        def resid(eng, ctx, og=og, W=W, p=p, dbank=dbank):
                            return eng.scalar_tensor_tensor(x2t[p][:, 0:W], ctx.buf[dbank][:, 0:W],
                                                            0.5, xt[og][:, 0:W],
                                                            ALU.mult, ALU.add)
                        plan.step("vector", f"{name}.res{c0}_{og}", resid)

                        t_st = tag()
                        def st_x2(eng, ctx, og=og, c0=c0, W=W, p=p):
                            if out_coloff is not None:
                                dst = out[og * 128:(og + 1) * 128, c0 - out_coloff:c0 - out_coloff + W]
                            else:
                                dst = dst_stream[og * 128:(og + 1) * 128, c0:c0 + W]
                            i = eng.dma_start(out=dst, in_=x2t[p][:, 0:W])
                            i.then_inc(ctx.dma_sem(), 16)
                            return i
                        plan.dma("gpsimd", f"{name}.stx{c0}_{og}", st_x2, 1, tag=t_st)
                        prev_x2store[0] = t_st

        if "ffn1" in phases:
            ffn_phase("f1", FFN1_TILES, x_in, sA, side_jobs=split_jobs)
        elif split_jobs:
            with contextlib.ExitStack() as W:
                for pidx in range(2):
                    sb(W, f"wsf{pidx}", [128, 1024])
                    sb(W, f"wsh{pidx}", [128, 1024], BF16)
                    sb(W, f"wsl{pidx}", [128, 1024], BF16)
                for j, job in enumerate(split_jobs):
                    job(j % 2)

        def attn_phase():
            with contextlib.ExitStack() as S:
                x2t = [sb(S, f"at_x{g}", [128, 512]) for g in range(NG)]
                yt = x2t
                xab = [sb(S, f"at_a{g}", [128, 513]) for g in range(NG)]
                ks = [sb(S, f"at_k{g}", [128, 512]) for g in range(NG)]
                vs = [sb(S, f"at_v{g}", [128, 512]) for g in range(NG)]
                rs_ = [sb(S, f"at_r{g}", [128, 512]) for g in range(NG)]
                sab = [sb(S, f"at_sa{i}", [128, 513]) for i in range(2)]
                sbb = [sb(S, f"at_sb{i}", [128, 513]) for i in range(2)]
                dsc = [sb(S, f"dsc{i}", [128, 512]) for i in range(2)]
                sc1 = [sb(S, f"sc1_{i}", [128, 512]) for i in range(2)]
                sc2 = [sb(S, f"sc2_{i}", [128, 512]) for i in range(2)]
                rwh = [sb(S, f"at_rh{g}", [128, 512], BF16) for g in range(NG)]
                rwl = [sb(S, f"at_rl{g}", [128, 512], BF16) for g in range(NG)]
                kmt = sb(S, "kmt", [128, 512])
                wsa = [sb(S, f"wsa{i}", [128, 1024], BF16) for i in range(2)]
                wsb_ = [sb(S, f"wsb{i}", [128, 1024], BF16) for i in range(2)]
                mxf2 = [sb(S, f"mxf{i}", [128, 512]) for i in range(2)]
                mxh2 = [sb(S, f"mxh{i}", [128, 512], BF16) for i in range(2)]
                mxl2 = [sb(S, f"mxl{i}", [128, 512], BF16) for i in range(2)]

                def zero_halo(eng, ctx):
                    last = None
                    for g in range(NG):
                        last = eng.memset(xab[g][:, 0:1], 0.0)
                    return last
                plan.step("vector", "at.zh", zero_halo, need_dma=[plan.last_store] if plan.last_store else [])
                prev_x3store = [plan.last_store]
                prev_W = [None]

                for ti, (c0, W) in enumerate(ATTN_TILES):
                    t_x = tag()
                    def ld_x(eng, ctx, c0=c0, W=W):
                        for g in range(NG):
                            eng.dma_start(out=x2t[g][:, 0:W],
                                          in_=sA[g * 128:(g + 1) * 128, c0:c0 + W]).then_inc(ctx.dma_sem(), 16)
                        i = eng.dma_start(out=kmt[:, 0:W], in_=kmask_in[:, c0:c0 + W])
                        i.then_inc(ctx.dma_sem(), 16)
                        return i
                    plan.dma("sync", f"at.ldx{c0}", ld_x, NG + 1, tag=t_x,
                             need_dma=[prev_x3store[0]] if prev_x3store[0] else [])

                    if ti > 0:
                        def halo(eng, ctx, pw=prev_W[0], c0=c0):
                            last = None
                            for g in range(NG):
                                last = eng.tensor_copy(xab[g][:, 0:1], xab[g][:, pw:pw + 1])
                                if c0 == LB:
                                    ctx.seq(eng, last)
                                    last = eng.tensor_scalar_mul(xab[g][:, 0:1], xab[g][:, 0:1],
                                                                 ctx.buf["h0m"][:, 0:1])
                            return last
                        plan.step("vector", f"at.halo{c0}", halo)
                    prev_W[0] = W

                    ln_stats(f"at.ln{c0}", x2t, W, rows_extra=True, need=[t_x])
                    ln_apply(f"at.lna{c0}", x2t, xab, W, gamma=par["ln0g"],
                             beta=par["ln0b"], dst_off=1, ybuf=yt)

                    mats = [("wk", "tmk", ks, "k"), ("wv", "tmv", vs, "v")]
                    if ti > 0:
                        mats.append(("wr", "tmr", rs_, "r"))
                    for (wname, tmname, dsts, mt) in mats:
                        for kg in range(NG):
                            p = kg % 2
                            t_w = tag()
                            def ld_w(eng, ctx, wname=wname, kg=kg, p=p):
                                hi_d, lo_d = spills[wname]
                                eng.dma_start(out=wsa[p][:, :], in_=hi_d[kg, :, :]).then_inc(ctx.dma_sem(), 16)
                                i = eng.dma_start(out=wsb_[p][:, :], in_=lo_d[kg, :, :])
                                i.then_inc(ctx.dma_sem(), 16)
                                return i
                            plan.dma("sync", f"at.lw{mt}{c0}_{kg}", ld_w, 2, tag=t_w)

                            def mix(eng, ctx, kg=kg, tmname=tmname, W=W, p=p):
                                ctx.seq(eng, eng.tensor_tensor(dsc[p][:, 0:W],
                                                               xab[kg][:, 1:W + 1],
                                                               xab[kg][:, 0:W], ALU.subtract))
                                ctx.seq(eng, eng.scalar_tensor_tensor(mxf2[p][:, 0:W], dsc[p][:, 0:W],
                                                                      par[tmname][:, kg:kg + 1],
                                                                      xab[kg][:, 0:W],
                                                                      ALU.mult, ALU.add))
                                ctx.seq(eng, eng.tensor_copy(mxh2[p][:, 0:W], mxf2[p][:, 0:W]))
                                return eng.tensor_tensor(mxl2[p][:, 0:W], mxf2[p][:, 0:W],
                                                         mxh2[p][:, 0:W], ALU.subtract)
                            plan.step("vector", f"at.mx{mt}{c0}_{kg}", mix)

                            def mm(eng, ctx, kg=kg, W=W, p=p):
                                last = None
                                for og in range(NG):
                                    sl = slice(og * 128, (og + 1) * 128)
                                    eng.matmul(ogbank[og][:, 0:W], wsa[p][:, sl], mxh2[p][:, 0:W],
                                               start=(kg == 0), stop=False)
                                    eng.matmul(ogbank[og][:, 0:W], wsa[p][:, sl], mxl2[p][:, 0:W],
                                               start=False, stop=False)
                                    last = eng.matmul(ogbank[og][:, 0:W], wsb_[p][:, sl], mxh2[p][:, 0:W],
                                                      start=False, stop=(kg == NG - 1))
                                return last
                            plan.step("tensor", f"at.mm{mt}{c0}_{kg}", mm, need_dma=[t_w])

                        if mt == "k":
                            for og in range(NG):
                                def ev(eng, ctx, og=og, W=W):
                                    return eng.tensor_tensor(ks[og][:, 0:W],
                                                             ogbank[og][:, 0:W],
                                                             kmt[:, 0:W], ALU.add)
                                plan.step("vector", f"at.evk{c0}_{og}", ev)
                        elif mt == "v":
                            for og in range(NG):
                                def ev(eng, ctx, og=og, W=W):
                                    return eng.tensor_copy(vs[og][:, 0:W], ogbank[og][:, 0:W])
                                plan.step("vector", f"at.evv{c0}_{og}", ev)
                        else:
                            for og in range(NG):
                                def ev(eng, ctx, og=og, W=W):
                                    return eng.activation(rs_[og][:, 0:W], ogbank[og][:, 0:W],
                                                          AF.Sigmoid)
                                plan.step("scalar", f"at.evr{c0}_{og}", ev)

                    def smax(eng, ctx, W=W):
                        last = None
                        for g in range(NG):
                            last = eng.tensor_reduce(s_cur[:, g:g + 1],
                                                     ks[g][:, 0:W], AX.X, ALU.max)
                        return last
                    plan.step("vector", f"at.smax{c0}", smax)

                    def sresc(eng, ctx):
                        ctx.seq(eng, eng.tensor_scalar_max(s_cur[:, :], s_cur[:, :], -60.0))
                        ctx.seq(eng, eng.tensor_sub(resc[:, :], s_prev[:, :], s_cur[:, :]))
                        return eng.tensor_scalar_mul(ns_cur[:, :], s_cur[:, :], -1.0)
                    plan.step("vector", f"at.sresc{c0}", sresc)

                    def sexp(eng, ctx):
                        return eng.activation(resc[:, :], resc[:, :], AF.Exp)
                    plan.step("scalar", f"at.sexp{c0}", sexp)

                    def stfix(eng, ctx):
                        ctx.seq(eng, eng.tensor_mul(st_a[:, :], st_a[:, :], resc[:, :]))
                        ctx.seq(eng, eng.tensor_mul(st_b[:, :], st_b[:, :], resc[:, :]))
                        return eng.tensor_copy(s_prev[:, :], s_cur[:, :])
                    plan.step("vector", f"at.stfix{c0}", stfix)

                    for g in range(NG):
                        def kap(eng, ctx, g=g, W=W):
                            return eng.activation(ks[g][:, 0:W], ks[g][:, 0:W], AF.Exp,
                                                  bias=ns_cur[:, g:g + 1])
                        plan.step("scalar", f"at.kap{c0}_{g}", kap)
                        def kv(eng, ctx, g=g, W=W):
                            return eng.tensor_mul(vs[g][:, 0:W], ks[g][:, 0:W], vs[g][:, 0:W])
                        plan.step("vector", f"at.kv{c0}_{g}", kv)

                    for g in range(NG):
                        p = g % 2
                        def scan(eng, ctx, g=g, W=W, p=p):
                            lam_bc = lam[:, g:g + 1].to_broadcast([128, W])
                            ctx.seq(eng, eng.tensor_copy(sab[p][:, 0:1], st_a[:, g:g + 1]))
                            ctx.seq(eng, eng.tensor_copy(sbb[p][:, 0:1], st_b[:, g:g + 1]))
                            ctx.seq(eng, eng.tensor_tensor_scan(sab[p][:, 1:W + 1], lam_bc,
                                                                vs[g][:, 0:W], st_a[:, g:g + 1],
                                                                ALU.mult, ALU.add))
                            ctx.seq(eng, eng.tensor_tensor_scan(sbb[p][:, 1:W + 1], lam_bc,
                                                                ks[g][:, 0:W], st_b[:, g:g + 1],
                                                                ALU.mult, ALU.add))
                            ctx.seq(eng, eng.tensor_copy(st_a[:, g:g + 1], sab[p][:, W:W + 1]))
                            return eng.tensor_copy(st_b[:, g:g + 1], sbb[p][:, W:W + 1])
                        plan.step("vector", f"at.scan{c0}_{g}", scan)

                        if ti == 0:
                            continue

                        def wkv(eng, ctx, g=g, W=W, p=p):
                            s1 = sc1[p][:, 0:W]
                            s2 = sc2[p][:, 0:W]
                            ctx.seq(eng, eng.tensor_scalar_mul(s1, ks[g][:, 0:W],
                                                               eu[:, g:g + 1]))
                            ctx.seq(eng, eng.tensor_add(s1, s1, sbb[p][:, 0:W]))
                            ctx.seq(eng, eng.tensor_scalar_max(s1, s1, 1e-30))
                            ctx.seq(eng, eng.reciprocal(s2, s1))
                            ctx.seq(eng, eng.tensor_scalar_mul(s1, vs[g][:, 0:W],
                                                               eu[:, g:g + 1]))
                            ctx.seq(eng, eng.tensor_add(s1, s1, sab[p][:, 0:W]))
                            ctx.seq(eng, eng.tensor_mul(s1, s1, s2))
                            ctx.seq(eng, eng.tensor_mul(s1, s1, rs_[g][:, 0:W]))
                            ctx.seq(eng, eng.tensor_copy(rwh[g][:, 0:W], s1))
                            return eng.tensor_tensor(rwl[g][:, 0:W], s1,
                                                     rwh[g][:, 0:W], ALU.subtract)
                        plan.step("vector", f"at.wkv{c0}_{g}", wkv)

                    if ti == 0:
                        continue

                    for kg in range(NG):
                        p = kg % 2
                        t_w = tag()
                        def ld_w(eng, ctx, kg=kg, p=p):
                            hi_d, lo_d = spills["wo"]
                            eng.dma_start(out=wsa[p][:, :], in_=hi_d[kg, :, :]).then_inc(ctx.dma_sem(), 16)
                            i = eng.dma_start(out=wsb_[p][:, :], in_=lo_d[kg, :, :])
                            i.then_inc(ctx.dma_sem(), 16)
                            return i
                        plan.dma("sync", f"at.lwo{c0}_{kg}", ld_w, 2, tag=t_w)

                        def mm(eng, ctx, kg=kg, W=W, p=p):
                            last = None
                            for og in range(NG):
                                sl = slice(og * 128, (og + 1) * 128)
                                eng.matmul(ogbank[og][:, 0:W], wsa[p][:, sl], rwh[kg][:, 0:W],
                                           start=(kg == 0), stop=False)
                                eng.matmul(ogbank[og][:, 0:W], wsa[p][:, sl], rwl[kg][:, 0:W],
                                           start=False, stop=False)
                                last = eng.matmul(ogbank[og][:, 0:W], wsb_[p][:, sl], rwh[kg][:, 0:W],
                                                  start=False, stop=(kg == NG - 1))
                            return last
                        plan.step("tensor", f"at.mmo{c0}_{kg}", mm, need_dma=[t_w])

                    for og in range(NG):
                        def x3(eng, ctx, og=og, W=W):
                            return eng.tensor_add(yt[og][:, 0:W], ogbank[og][:, 0:W],
                                                  yt[og][:, 0:W])
                        plan.step("vector", f"at.x3{c0}_{og}", x3)

                        t_st = tag()
                        def st_x3(eng, ctx, og=og, c0=c0, W=W):
                            i = eng.dma_start(out=sB[og * 128:(og + 1) * 128, c0:c0 + W],
                                              in_=yt[og][:, 0:W])
                            i.then_inc(ctx.dma_sem(), 16)
                            return i
                        plan.dma("gpsimd", f"at.stx3{c0}_{og}", st_x3, 1, tag=t_st)
                        prev_x3store[0] = t_st

        if "attn" in phases:
            attn_phase()

        def cm_phase():
            with contextlib.ExitStack() as S:
                x3t = [sb(S, f"cm_x{g}", [128, 512]) for g in range(NG)]
                x2one = [sb(S, f"cm_x2o{i}", [128, 512]) for i in range(2)]
                xcb = [sb(S, f"cm_c{g}", [128, 513]) for g in range(NG)]
                rrs = [sb(S, f"cm_rr{g}", [128, 512]) for g in range(NG)]
                rkh = [sb(S, f"cm_kh{o}", [128, 512], BF16) for o in range(NOG_CM)]
                rkl = [sb(S, f"cm_kl{o}", [128, 512], BF16) for o in range(NOG_CM)]
                dsc = [sb(S, f"dsc{i}", [128, 512]) for i in range(2)]
                mxf2 = [sb(S, f"mxf{i}", [128, 512]) for i in range(2)]
                mxh2 = [sb(S, f"mxh{i}", [128, 512], BF16) for i in range(2)]
                mxl2 = [sb(S, f"mxl{i}", [128, 512], BF16) for i in range(2)]
                sc1 = [sb(S, f"sc1_{i}", [128, 512]) for i in range(2)]
                wsa = [sb(S, f"wsa{i}", [128, 1024], BF16) for i in range(2)]
                wsb_ = [sb(S, f"wsb{i}", [128, 1024], BF16) for i in range(2)]

                def zero_halo(eng, ctx):
                    last = None
                    for g in range(NG):
                        last = eng.memset(xcb[g][:, 0:1], 0.0)
                    return last
                plan.step("vector", "cm.zh", zero_halo, need_dma=[plan.last_store] if plan.last_store else [])
                prev_store = [plan.last_store]
                prev_W = [None]

                for ti, (c0, W) in enumerate(CM_TILES):
                    t_x = tag()
                    def ld_x(eng, ctx, c0=c0, W=W):
                        last = None
                        for g in range(NG):
                            last = eng.dma_start(out=x3t[g][:, 0:W],
                                                 in_=sB[g * 128:(g + 1) * 128, c0:c0 + W])
                            last.then_inc(ctx.dma_sem(), 16)
                        return last
                    plan.dma("sync", f"cm.ldx{c0}", ld_x, NG,
                             tag=t_x, need_dma=[prev_store[0]] if prev_store[0] else [])

                    if ti > 0:
                        def halo(eng, ctx, pw=prev_W[0], c0=c0):
                            last = None
                            for g in range(NG):
                                last = eng.tensor_copy(xcb[g][:, 0:1], xcb[g][:, pw:pw + 1])
                                if c0 == LB:
                                    ctx.seq(eng, last)
                                    last = eng.tensor_scalar_mul(xcb[g][:, 0:1], xcb[g][:, 0:1],
                                                                 ctx.buf["h0m"][:, 0:1])
                            return last
                        plan.step("vector", f"cm.halo{c0}", halo)
                    prev_W[0] = W

                    ln_stats(f"cm.ln{c0}", x3t, W, rows_extra=False, need=[t_x])
                    ln_apply(f"cm.lna{c0}", x3t, xcb, W, gamma=par["ln1g"],
                             beta=par["ln1b"], dst_off=1)

                    if ti == 0:
                        continue

                    for kg in range(NG):
                        p = kg % 2
                        t_w = tag()
                        def ld_w(eng, ctx, kg=kg, p=p):
                            hi_d, lo_d = spills["wcr"]
                            eng.dma_start(out=wsa[p][:, 0:D], in_=hi_d[kg, :, :]).then_inc(ctx.dma_sem(), 16)
                            i = eng.dma_start(out=wsb_[p][:, 0:D], in_=lo_d[kg, :, :])
                            i.then_inc(ctx.dma_sem(), 16)
                            return i
                        plan.dma("sync", f"cm.lwr{c0}_{kg}", ld_w, 2, tag=t_w)

                        def mix(eng, ctx, kg=kg, W=W, p=p):
                            ctx.seq(eng, eng.tensor_tensor(dsc[p][:, 0:W], xcb[kg][:, 1:W + 1],
                                                           xcb[kg][:, 0:W], ALU.subtract))
                            ctx.seq(eng, eng.scalar_tensor_tensor(mxf2[p][:, 0:W], dsc[p][:, 0:W],
                                                                  par["cmr"][:, kg:kg + 1],
                                                                  xcb[kg][:, 0:W],
                                                                  ALU.mult, ALU.add))
                            ctx.seq(eng, eng.tensor_copy(mxh2[p][:, 0:W], mxf2[p][:, 0:W]))
                            return eng.tensor_tensor(mxl2[p][:, 0:W], mxf2[p][:, 0:W],
                                                     mxh2[p][:, 0:W], ALU.subtract)
                        plan.step("vector", f"cm.mxr{c0}_{kg}", mix)

                        def mm(eng, ctx, kg=kg, W=W, p=p):
                            last = None
                            for og in range(NG):
                                sl = slice(og * 128, (og + 1) * 128)
                                eng.matmul(ogbank[og][:, 0:W], wsa[p][:, sl], mxh2[p][:, 0:W],
                                           start=(kg == 0), stop=False)
                                eng.matmul(ogbank[og][:, 0:W], wsa[p][:, sl], mxl2[p][:, 0:W],
                                           start=False, stop=False)
                                last = eng.matmul(ogbank[og][:, 0:W], wsb_[p][:, sl], mxh2[p][:, 0:W],
                                                  start=False, stop=(kg == NG - 1))
                            return last
                        plan.step("tensor", f"cm.mmr{c0}_{kg}", mm, need_dma=[t_w])
                    for og in range(NG):
                        def ev(eng, ctx, og=og, W=W):
                            return eng.activation(rrs[og][:, 0:W], ogbank[og][:, 0:W],
                                                  AF.Sigmoid)
                        plan.step("scalar", f"cm.evr{c0}_{og}", ev)

                    for kg in range(NG):
                        p = kg % 2
                        def mix(eng, ctx, kg=kg, W=W, p=p):
                            ctx.seq(eng, eng.tensor_tensor(dsc[p][:, 0:W], xcb[kg][:, 1:W + 1],
                                                           xcb[kg][:, 0:W], ALU.subtract))
                            ctx.seq(eng, eng.scalar_tensor_tensor(mxf2[p][:, 0:W], dsc[p][:, 0:W],
                                                                  par["cmk"][:, kg:kg + 1],
                                                                  xcb[kg][:, 0:W],
                                                                  ALU.mult, ALU.add))
                            ctx.seq(eng, eng.tensor_copy(rkh[NOG_CM - 8 + kg][:, 0:W], mxf2[p][:, 0:W]))
                            return eng.tensor_tensor(rkl[NOG_CM - 8 + kg][:, 0:W], mxf2[p][:, 0:W],
                                                     rkh[NOG_CM - 8 + kg][:, 0:W], ALU.subtract)
                        plan.step("vector", f"cm.mxk{c0}_{kg}", mix)

                    for rnd in range(4):
                        for kg in range(NG):
                            p = kg % 2
                            t_w = tag()
                            def ld_w(eng, ctx, kg=kg, rnd=rnd, p=p):
                                hi_d, lo_d = spills["wck"]
                                eng.dma_start(out=wsa[p][:, 0:1024],
                                              in_=hi_d[kg, :, rnd * 1024:(rnd + 1) * 1024]).then_inc(ctx.dma_sem(), 16)
                                i = eng.dma_start(out=wsb_[p][:, 0:1024],
                                                  in_=lo_d[kg, :, rnd * 1024:(rnd + 1) * 1024])
                                i.then_inc(ctx.dma_sem(), 16)
                                return i
                            plan.dma("sync", f"cm.lwk{c0}_{rnd}_{kg}", ld_w, 2, tag=t_w)

                            def mm(eng, ctx, kg=kg, W=W, p=p):
                                last = None
                                mh = rkh[NOG_CM - 8 + kg]
                                ml = rkl[NOG_CM - 8 + kg]
                                for oo in range(8):
                                    sl = slice(oo * 128, (oo + 1) * 128)
                                    eng.matmul(ogbank[oo][:, 0:W], wsa[p][:, sl], mh[:, 0:W],
                                               start=(kg == 0), stop=False)
                                    eng.matmul(ogbank[oo][:, 0:W], wsa[p][:, sl], ml[:, 0:W],
                                               start=False, stop=False)
                                    last = eng.matmul(ogbank[oo][:, 0:W], wsb_[p][:, sl], mh[:, 0:W],
                                                      start=False, stop=(kg == NG - 1))
                                return last
                            plan.step("tensor", f"cm.mmk{c0}_{rnd}_{kg}", mm, need_dma=[t_w])

                        for oo in range(8):
                            og32 = rnd * 8 + oo
                            p = oo % 2
                            def rk2(eng, ctx, oo=oo, og32=og32, W=W, p=p):
                                return eng.activation(sc1[p][:, 0:W], ogbank[oo][:, 0:W], AF.Relu)
                            plan.step("scalar", f"cm.rk{c0}_{og32}", rk2)

                            def sq_split(eng, ctx, og32=og32, W=W, p=p):
                                ctx.seq(eng, eng.tensor_mul(sc1[p][:, 0:W], sc1[p][:, 0:W], sc1[p][:, 0:W]))
                                ctx.seq(eng, eng.tensor_copy(rkh[og32][:, 0:W], sc1[p][:, 0:W]))
                                return eng.tensor_tensor(rkl[og32][:, 0:W], sc1[p][:, 0:W],
                                                         rkh[og32][:, 0:W], ALU.subtract)
                            plan.step("vector", f"cm.rs{c0}_{og32}", sq_split)

                    for kg32 in range(NOG_CM):
                        p = kg32 % 2
                        t_w = tag()
                        def ld_w(eng, ctx, kg32=kg32, p=p):
                            hi_d, lo_d = spills["wcv"]
                            eng.dma_start(out=wsa[p][:, 0:D], in_=hi_d[kg32, :, :]).then_inc(ctx.dma_sem(), 16)
                            i = eng.dma_start(out=wsb_[p][:, 0:D], in_=lo_d[kg32, :, :])
                            i.then_inc(ctx.dma_sem(), 16)
                            return i
                        plan.dma("sync", f"cm.lwv{c0}_{kg32}", ld_w, 2, tag=t_w)

                        def mm(eng, ctx, kg32=kg32, W=W, p=p):
                            last = None
                            for og in range(NG):
                                sl = slice(og * 128, (og + 1) * 128)
                                eng.matmul(ogbank[og][:, 0:W], wsa[p][:, sl], rkh[kg32][:, 0:W],
                                           start=(kg32 == 0), stop=False)
                                eng.matmul(ogbank[og][:, 0:W], wsa[p][:, sl], rkl[kg32][:, 0:W],
                                           start=False, stop=False)
                                last = eng.matmul(ogbank[og][:, 0:W], wsb_[p][:, sl], rkh[kg32][:, 0:W],
                                                  start=False, stop=(kg32 == NOG_CM - 1))
                            return last
                        plan.step("tensor", f"cm.mmv{c0}_{kg32}", mm, need_dma=[t_w])

                    for og in range(NG):
                        t_x2 = tag()
                        def ldx2(eng, ctx, og=og, c0=c0, W=W):
                            i = eng.dma_start(out=x2one[og % 2][:, 0:W],
                                              in_=sA[og * 128:(og + 1) * 128, c0:c0 + W])
                            i.then_inc(ctx.dma_sem(), 16)
                            return i
                        plan.dma("sync", f"cm.ldx2{c0}_{og}", ldx2, 1, tag=t_x2)

                        def x5(eng, ctx, og=og, W=W):
                            ctx.seq(eng, eng.tensor_mul(sc1[og % 2][:, 0:W], ogbank[og][:, 0:W],
                                                        rrs[og][:, 0:W]))
                            ctx.seq(eng, eng.tensor_add(x3t[og][:, 0:W], sc1[og % 2][:, 0:W],
                                                        x3t[og][:, 0:W]))
                            return eng.tensor_add(x3t[og][:, 0:W], x3t[og][:, 0:W],
                                                  x2one[og % 2][:, 0:W])
                        plan.step("vector", f"cm.x5{c0}_{og}", x5, need_dma=[t_x2])

                        t_st = tag()
                        def st5(eng, ctx, og=og, c0=c0, W=W):
                            i = eng.dma_start(out=sA[og * 128:(og + 1) * 128, c0:c0 + W],
                                              in_=x3t[og][:, 0:W])
                            i.then_inc(ctx.dma_sem(), 16)
                            return i
                        plan.dma("gpsimd", f"cm.st5{c0}_{og}", st5, 1, tag=t_st)
                        prev_store[0] = t_st

        if "cm" in phases:
            cm_phase()

        if "ffn2" in phases:
            ffn_phase("f2", FFN2_TILES, sA, None, out_coloff=LB)

        if dump is not None:
            src, cols = dump
            srcmap = {"sA": sA, "sB": sB}[src]
            with contextlib.ExitStack() as S:
                dsb = sb(S, "dumpb", [128, 512])
                prevs = [plan.last_store]
                for g in range(NG):
                    for c0 in range(cols[0], cols[1], 512):
                        cw = min(512, cols[1] - c0)
                        t_l = tag()
                        def dl(eng, ctx, g=g, c0=c0, cw=cw):
                            i = eng.dma_start(out=dsb[:, 0:cw],
                                              in_=srcmap[g * 128:(g + 1) * 128, c0:c0 + cw])
                            i.then_inc(ctx.dma_sem(), 16)
                            return i
                        plan.dma("sync", f"dmp_l{g}_{c0}", dl, 1, tag=t_l,
                                 need_dma=[prevs[0]] if prevs[0] else [])
                        t_s = tag()
                        def ds(eng, ctx, g=g, c0=c0, cw=cw):
                            i = eng.dma_start(out=out[g * 128:(g + 1) * 128,
                                                      c0 - cols[0]:c0 - cols[0] + cw],
                                              in_=dsb[:, 0:cw])
                            i.then_inc(ctx.dma_sem(), 16)
                            return i
                        plan.dma("gpsimd", f"dmp_s{g}_{c0}", ds, 1, tag=t_s,
                                 need_dma=[t_l])
                        prevs[0] = t_s

        def fin(eng, ctx):
            for e in N_DSEM:
                for k in range(N_DSEM[e]):
                    tot = plan.dma_slot_count[e][k] * 16
                    if tot:
                        eng.wait_ge(ctx.sems[f"d_{e}_{k}"], tot)
            return eng.memset(ctx.buf["lam"][0:1, 0:1], 0.0)
        plan.step("vector", "fin", fin)

        emit_program(nc, plan, ctx, deps=deps, capture=capture)
    return nc, plan, ctx


def build_auto(phases=("wprep", "ffn1", "attn", "cm", "ffn2"), dump=None):
    cap = {}
    _nc1, plan1, ctx1 = build_full(phases, dump, capture=cap)
    deps = compute_deps(plan1, cap, ctx1.addr_of)
    nc, _p, _c = build_full(phases, dump, deps=deps)
    return nc



def host_inputs(inputs, core):
    x = np.asarray(inputs["L0_attn_x"]).reshape(16384, D)
    t0 = core * TOWN
    lo = t0 - LB
    if lo < 0:
        sl = np.zeros((TPRE, D), np.float32)
        sl[LB:] = x[0:TOWN]
    else:
        sl = x[lo:t0 + TOWN]
    xfm = np.ascontiguousarray(sl.T)

    km = np.zeros((128, TPRE), np.float32)
    if core == 0:
        km[:, 0:LB] = NEG

    idn, e64c, stepb = host_consts()
    fm = lambda v: np.ascontiguousarray(np.asarray(v).reshape(NG, 128).T)
    m = {
        "i_x": xfm, "i_kmask": km,
        "i_wk": np.asarray(inputs["L0_attn_Wk"]),
        "i_wv": np.asarray(inputs["L0_attn_Wv"]),
        "i_wr": np.asarray(inputs["L0_attn_Wr"]),
        "i_wo": np.asarray(inputs["L0_attn_Wo"]),
        "i_wck": np.asarray(inputs["L0_attn_Wck"]),
        "i_wcv": np.asarray(inputs["L0_attn_Wcv"]),
        "i_wcr": np.asarray(inputs["L0_attn_Wcr"]),
        "i_wg_t": np.ascontiguousarray(np.asarray(inputs["L0_ffn_w_gate"]).T),
        "i_wu_t": np.ascontiguousarray(np.asarray(inputs["L0_ffn_w_up"]).T),
        "i_wd_t": np.ascontiguousarray(np.asarray(inputs["L0_ffn_w_down"]).T),
        "i_ln0g": fm(inputs["L0_attn_ln0_g"]), "i_ln0b": fm(inputs["L0_attn_ln0_b"]),
        "i_ln1g": fm(inputs["L0_attn_ln1_g"]), "i_ln1b": fm(inputs["L0_attn_ln1_b"]),
        "i_tmk": fm(inputs["L0_attn_tm_k"]), "i_tmv": fm(inputs["L0_attn_tm_v"]),
        "i_tmr": fm(inputs["L0_attn_tm_r"]),
        "i_cmk": fm(inputs["L0_attn_cm_k"]), "i_cmr": fm(inputs["L0_attn_cm_r"]),
        "i_td": fm(inputs["L0_attn_time_decay"]), "i_tf": fm(inputs["L0_attn_time_first"]),
        "i_idn": idn, "i_e64": e64c, "i_stepb": stepb,
        "i_onescol": np.ones((128, 1), np.float32),
        "i_browones": np.ones((1, 128), np.float32),
        "i_epsc": np.full((128, 1), 1e-5, np.float32),
        "i_h0m": np.full((128, 1), 0.0 if core == 0 else 1.0, np.float32),
    }
    return m


def kernel(**inputs):
    nc = build_auto()
    in_maps = [host_inputs(inputs, c) for c in range(NCORES)]
    res = run_bass_kernel_spmd(nc, in_maps, core_ids=list(range(NCORES)))
    out = np.empty((16384, D), np.float32)
    for c in range(NCORES):
        out[c * TOWN:(c + 1) * TOWN] = res.results[c]["out"].T
    return out.reshape(1, 16384, D)



if __name__ == "__main__":
    _selftest_quant()


# version 21 (final)
# speedup vs baseline: 1.0369x; 1.0144x over previous
import contextlib
import numpy as np
import ml_dtypes

import concourse.bass as bass
import concourse.mybir as mybir
from concourse.bass_utils import run_bass_kernel_spmd

F32 = mybir.dt.float32
BF16 = mybir.dt.bfloat16
I32 = mybir.dt.int32
AF = mybir.ActivationFunctionType
ALU = mybir.AluOpType
AX = mybir.AxisListType

NCORES = 8
D = 1024
NG = D // 128
TOWN = 2048
LB = 768
TPRE = LB + TOWN
IFF = 2816
ICM = 4096
BIG = float(np.float32(2.0 ** 100))
MID = [0.25, 0.75, 1.25, 1.75, 2.5, 3.5, 5.0]
NEG = -1e38


ENGINES = ("sync", "vector", "scalar", "tensor", "gpsimd")


class Step:
    __slots__ = ("engine", "fn", "name", "dep_idx", "need_dma", "idx", "inc",
                 "self_dep", "n_dmas", "dma_slot", "dma_prior", "dma_after")

    def __init__(self, engine, fn, name, dep_idx=None, need_dma=None, inc=1,
                 self_dep=True, n_dmas=0):
        self.engine = engine
        self.fn = fn
        self.name = name
        self.dep_idx = dep_idx
        self.need_dma = need_dma or []
        self.idx = None
        self.inc = inc
        self.self_dep = self_dep
        self.n_dmas = n_dmas


N_DSEM = {"sync": 6, "gpsimd": 4}


class Plan:
    def __init__(self):
        self.steps = []
        self.dma_tags = {}
        self.dma_count = {"sync": 0, "gpsimd": 0}
        self.dma_rr = {"sync": 0, "gpsimd": 0}
        self.dma_slot_count = {e: [0] * N_DSEM[e] for e in N_DSEM}
        self.last_store = None
        self.gser = [0]

    def step(self, engine, name, fn, dep_idx=None, need_dma=None, inc=1,
             self_dep=True, n_dmas=0):
        s = Step(engine, fn, name, dep_idx, need_dma, inc, self_dep, n_dmas)
        s.idx = len(self.steps)
        self.steps.append(s)
        return s

    def dma(self, engine, name, fn, n_dmas, dep_idx=None, tag=None, need_dma=None):
        s = self.step(engine, name, fn, dep_idx=dep_idx, need_dma=need_dma, inc=0,
                      n_dmas=n_dmas)
        self.dma_count[engine] += n_dmas
        slot = self.dma_rr[engine]
        self.dma_rr[engine] = (slot + 1) % N_DSEM[engine]
        s.dma_slot = slot
        s.dma_prior = self.dma_slot_count[engine][slot]
        self.dma_slot_count[engine][slot] += n_dmas
        s.dma_after = self.dma_slot_count[engine][slot]
        if tag:
            self.dma_tags[tag] = s
            if engine == "gpsimd":
                self.last_store = tag
        return s


class Ctx:

    def __init__(self, nc, sems):
        self.nc = nc
        self.sems = sems
        self.buf = {}
        self.dram = {}
        self.addr_of = {}
        self.cur_eng = None
        self.drain_count = {e: 0 for e in ENGINES}

    def dma_sem(self):
        return self.sems[f"d_{self.cur_eng_name}_{self.cur_dma_slot}"]

    def seq(self, eng, inst):
        e = self.cur_eng_name
        self.drain_count[e] += 1
        sem = self.sems["dr_" + e]
        inst.then_inc(sem, 1)
        eng.wait_ge(sem, self.drain_count[e])
        return inst


def emit_program(nc, plan, ctx, deps=None, capture=None):
    n = len(plan.steps)
    cum = {e: [0] * (n + 1) for e in ENGINES}
    for i, s in enumerate(plan.steps):
        for e in ENGINES:
            cum[e][i + 1] = cum[e][i] + (s.inc if s.engine == e else 0)

    sem_of = {"sync": "psync", "vector": "pv", "scalar": "ps",
              "tensor": "pt", "gpsimd": "pg"}

    slot_before = []
    running = {}
    for s in plan.steps:
        slot_before.append(dict(running))
        if s.inc == 0:
            running[f"d_{s.engine}_{s.dma_slot}"] = s.dma_after * 16

    class _Rec:
        __slots__ = ("_e", "_sink", "_ann")

        def __init__(self, e, sink, ann=None):
            self._e = e
            self._sink = sink
            self._ann = ann

        def __getattr__(self, nm):
            a = getattr(self._e, nm)
            if callable(a):
                sink = self._sink
                annv = self._ann

                def wrap(*args, **kw):
                    r = a(*args, **kw)
                    if annv is not None:
                        try:
                            r.annotate(annv)
                        except Exception:
                            pass
                    if sink is not None:
                        try:
                            mi = r.ins
                            for ap in mi.ins:
                                if hasattr(ap, "memref"):
                                    sink[0].add(ap.memref)
                            for ap in mi.outs:
                                if hasattr(ap, "memref"):
                                    sink[1].add(ap.memref)
                        except AttributeError:
                            pass
                    return r
                return wrap
            return a

    def engine_body(ename):
        def body(eng):
            ctx.cur_eng_name = ename
            last_wait = {}
            def wait_sem(semname, want):
                if want > last_wait.get(semname, 0):
                    eng.wait_ge(ctx.sems[semname], want)
                    last_wait[semname] = want

            for s in plan.steps:
                if s.engine != ename:
                    continue
                if s.inc == 0:
                    ctx.cur_dma_slot = s.dma_slot
                    wait_sem(f"d_{ename}_{s.dma_slot}", s.dma_prior * 16)
                if deps is not None:
                    for semname, want in deps.get(s.idx, {}).items():
                        wait_sem(semname, want)
                else:
                    di = s.dep_idx if s.dep_idx is not None else s.idx
                    for e in ENGINES:
                        if e == ename:
                            continue
                        wait_sem(sem_of[e], cum[e][di])
                    if s.self_dep:
                        wait_sem(sem_of[ename], cum[ename][s.idx])
                    if di == s.idx:
                        for semname, tot in slot_before[s.idx].items():
                            wait_sem(semname, tot)
                if deps is None:
                    for tg in s.need_dma:
                        st2 = plan.dma_tags[tg]
                        wait_sem(f"d_{st2.engine}_{st2.dma_slot}", st2.dma_after * 16)
                if capture is not None:
                    sink = (set(), set())
                    last = s.fn(_Rec(eng, sink), ctx)
                    capture[s.idx] = sink
                else:
                    last = s.fn(_Rec(eng, None, s.name), ctx)
                assert last is not None, f"step {s.name} returned no instruction"
                if s.inc:
                    last.then_inc(ctx.sems[sem_of[ename]], s.inc)
        return body

    with nc.Block() as blk:
        blk.sync(engine_body("sync"))
        blk.vector(engine_body("vector"))
        blk.scalar(engine_body("scalar"))
        blk.tensor(engine_body("tensor"))
        blk.gpsimd(engine_body("gpsimd"))


def compute_deps(plan, capture, addr_of=None):
    addr_of = addr_of or {}
    import bisect
    bounds = {}
    for sp, lo, hi in addr_of.values():
        bounds.setdefault(sp, set()).update((lo, hi))
    bounds = {sp: sorted(v) for sp, v in bounds.items()}
    keycache = {}

    def keys_of(nm):
        r = keycache.get(nm)
        if r is None:
            a = addr_of.get(nm)
            if a is None:
                r = (("d", nm),)
            else:
                sp, lo, hi = a
                bl = bounds[sp]
                i0 = bisect.bisect_left(bl, lo)
                i1 = bisect.bisect_left(bl, hi)
                r = tuple((sp, i) for i in range(i0, i1))
            keycache[nm] = r
        return r
    n = len(plan.steps)
    cum = {e: [0] * (n + 1) for e in ENGINES}
    for i, s in enumerate(plan.steps):
        for e in ENGINES:
            cum[e][i + 1] = cum[e][i] + (s.inc if s.engine == e else 0)
    sem_of = {"sync": "psync", "vector": "pv", "scalar": "ps",
              "tensor": "pt", "gpsimd": "pg"}

    last_writer = {}
    readers = {}
    deps = {}
    for s in plan.steps:
        i = s.idx
        reads0, writes0 = capture[i]
        reads = set()
        for nm in reads0:
            reads.update(keys_of(nm))
        writes = set()
        for nm in writes0:
            writes.update(keys_of(nm))
        dep_steps = set()
        for nm in reads:
            if nm in last_writer:
                dep_steps.add(last_writer[nm])
        for nm in writes:
            if nm in last_writer:
                dep_steps.add(last_writer[nm])
            for r in readers.get(nm, ()):
                dep_steps.add(r)
        dep_steps.discard(i)
        dv = {}
        for j in dep_steps:
            sj = plan.steps[j]
            if sj.inc == 0:
                key = f"d_{sj.engine}_{sj.dma_slot}"
                want = sj.dma_after * 16
            else:
                key = sem_of[sj.engine]
                want = cum[sj.engine][j + 1]
            if dv.get(key, 0) < want:
                dv[key] = want
        deps[i] = dv
        for nm in writes:
            last_writer[nm] = i
            readers[nm] = []
        for nm in reads:
            if nm not in writes:
                readers.setdefault(nm, []).append(i)
    return deps



def emit_fm_quant(plan, name, srcs, his, los, col0, width, npart_grp,
                  scratch_sets, psum, consts, dcol0=None, defer=False):
    if dcol0 is None:
        dcol0 = col0
    if isinstance(scratch_sets, dict):
        scratch_sets = [scratch_sets]
    if isinstance(psum, dict):
        psum_sets = [psum]
    else:
        psum_sets = psum
    gser = plan.gser
    stage_list = []

    def seg_steps(s0, sw):
        scratch = scratch_sets[0]
        psum = psum_sets[0]
        nsub = sw // 128
        nb = 8 * npart_grp

        def scale_sub(sub):
            for g in range(npart_grp):
                def tr_fn(eng, ctx, g=g, sub=sub, s0=s0):
                    return eng.transpose(
                        ctx.buf[psum["tr"]][:, 0:128],
                        srcs[g][:, col0 + s0 + sub * 128: col0 + s0 + (sub + 1) * 128],
                        ctx.buf[consts["idn"]][:, :])
                plan.step("tensor", f"{name}.tr", tr_fn)

                def red_fn(eng, ctx, g=g, sub=sub):
                    ap = ctx.buf[psum["tr"]][:, 0:128].rearrange("p (b k) -> p b k", k=16)
                    return eng.tensor_reduce(
                        ctx.buf[scratch["bm"]][:, 8 * g: 8 * g + 8], ap,
                        AX.X, ALU.max, apply_absolute_value=True)
                plan.step("vector", f"{name}.bm", red_fn)

            def rs_fn(eng, ctx):
                ctx.seq(eng, eng.tensor_scalar(ctx.buf[scratch["ss"]][:, 0:nb],
                                  ctx.buf[scratch["bm"]][:, 0:nb],
                                  1.0 / 6.0, 1e-12, ALU.mult, ALU.max))
                return eng.reciprocal(ctx.buf[scratch["rs"]][:, 0:nb],
                                      ctx.buf[scratch["ss"]][:, 0:nb])
            plan.step("vector", f"{name}.rs", rs_fn)

            def rsT_fn(eng, ctx, sub=sub):
                return eng.transpose(
                    ctx.buf[psum["rsT"]][0:nb, 0:128],
                    ctx.buf[scratch["rs"]][:, 0:nb],
                    ctx.buf[consts["idn"]][:, :])
            plan.step("tensor", f"{name}.rsT", rsT_fn)

            def rsc_fn(eng, ctx, sub=sub):
                return eng.tensor_copy(
                    ctx.buf[scratch["rsp"]][0:nb, sub * 128:(sub + 1) * 128],
                    ctx.buf[psum["rsT"]][0:nb, 0:128])
            plan.step("vector", f"{name}.rsc", rsc_fn)

        for sub in range(nsub):
            stage_list.append(lambda sub=sub: scale_sub(sub))

        def stage_exp(g, scratch, psum):
            def exp_fn(eng, ctx, g=g, scratch=scratch, psum=psum):
                return eng.matmul(ctx.buf[psum["exp"]][:, 0:sw],
                                  ctx.buf[consts["e64"]][0:nb, 128 * g:128 * (g + 1)],
                                  ctx.buf[scratch["rsp"]][0:nb, 0:sw],
                                  start=True, stop=True)
            plan.step("tensor", f"{name}.exp{g}", exp_fn)

        def stage_abs(g, scratch, psum):
            src_ap = lambda ctx, g=g: srcs[g][:, col0 + s0: col0 + s0 + sw]
            def abs_fn(eng, ctx, scratch=scratch, src_ap=src_ap):
                return eng.activation(ctx.buf[scratch["xabs"]][:, 0:sw], src_ap(ctx), AF.Abs)
            plan.step("scalar", f"{name}.abs{g}", abs_fn)

        def stage_y(g, scratch, psum):
            def y_fn(eng, ctx, scratch=scratch, psum=psum):
                return eng.tensor_tensor(ctx.buf[scratch["xabs"]][:, 0:sw],
                                         ctx.buf[scratch["xabs"]][:, 0:sw],
                                         ctx.buf[psum["exp"]][:, 0:sw], ALU.mult)
            plan.step("vector", f"{name}.y{g}", y_fn)

        def stage_sfld(g, scratch, psum):
            def sf_fn(eng, ctx, psum=psum):
                return eng.reciprocal(ctx.buf[psum["exp2"]][:, 0:sw],
                                      ctx.buf[psum["exp"]][:, 0:sw])
            plan.step("vector", f"{name}.sf{g}", sf_fn)

        def mkstep(j, dst, g, scratch):
            def f(eng, ctx, scratch=scratch):
                return eng.activation(ctx.buf[scratch[dst]][:, 0:sw],
                                      ctx.buf[scratch["xabs"]][:, 0:sw], AF.Sigmoid,
                                      bias=ctx.buf[consts["stepb"]][:, j:j + 1],
                                      scale=BIG)
            return plan.step("scalar", f"{name}.s{j}_{g}", f)

        def stage_s12(g, scratch, psum):
            mkstep(0, "c1", g, scratch)
            mkstep(1, "c2", g, scratch)

        def stage_gpa1(g, scratch, psum):
            def gpa1(eng, ctx, scratch=scratch):
                return eng.tensor_add(ctx.buf[scratch["accA"]][:, 0:sw],
                                      ctx.buf[scratch["c1"]][:, 0:sw],
                                      ctx.buf[scratch["c2"]][:, 0:sw])
            plan.step("gpsimd", f"{name}.gA1_{g}", gpa1)

        def stage_s34(g, scratch, psum):
            mkstep(2, "c1", g, scratch)
            mkstep(3, "c2", g, scratch)

        def stage_gpa2(g, scratch, psum):
            def gpa2(eng, ctx, scratch=scratch):
                return eng.tensor_add(ctx.buf[scratch["c2"]][:, 0:sw],
                                      ctx.buf[scratch["c1"]][:, 0:sw],
                                      ctx.buf[scratch["c2"]][:, 0:sw])
            plan.step("gpsimd", f"{name}.gA2_{g}", gpa2)

        def stage_s567(g, scratch, psum):
            mkstep(4, "c1", g, scratch)
            mkstep(6, "c7", g, scratch)
            mkstep(5, "xabs", g, scratch)

        def stage_gpa3(g, scratch, psum):
            def gpa3(eng, ctx, scratch=scratch):
                return eng.tensor_add(ctx.buf[scratch["c1"]][:, 0:sw],
                                      ctx.buf[scratch["c1"]][:, 0:sw],
                                      ctx.buf[scratch["xabs"]][:, 0:sw])
            plan.step("gpsimd", f"{name}.gA3_{g}", gpa3)

        def stage_comb(g, scratch, psum):
            def comb_fn(eng, ctx, scratch=scratch):
                ctx.seq(eng, eng.tensor_add(ctx.buf[scratch["accA"]][:, 0:sw],
                                            ctx.buf[scratch["accA"]][:, 0:sw],
                                            ctx.buf[scratch["c2"]][:, 0:sw]))
                ctx.seq(eng, eng.scalar_tensor_tensor(ctx.buf[scratch["c2"]][:, 0:sw],
                                         ctx.buf[scratch["c7"]][:, 0:sw], 2.0,
                                         ctx.buf[scratch["c1"]][:, 0:sw],
                                         ALU.mult, ALU.add))
                return eng.scalar_tensor_tensor(ctx.buf[scratch["qabs"]][:, 0:sw],
                                                ctx.buf[scratch["accA"]][:, 0:sw], 0.5,
                                                ctx.buf[scratch["c2"]][:, 0:sw],
                                                ALU.mult, ALU.add)
            plan.step("vector", f"{name}.comb{g}", comb_fn)

        def stage_sgn(g, scratch, psum):
            src_ap = lambda ctx, g=g: srcs[g][:, col0 + s0: col0 + s0 + sw]
            def sgn_fn(eng, ctx, scratch=scratch, src_ap=src_ap):
                return eng.activation(ctx.buf[scratch["c1"]][:, 0:sw], src_ap(ctx), AF.Sign)
            plan.step("scalar", f"{name}.sgn{g}", sgn_fn)

        def stage_q(g, scratch, psum):
            def q_fn(eng, ctx, g=g, scratch=scratch, psum=psum):
                ctx.seq(eng, eng.tensor_mul(ctx.buf[scratch["qabs"]][:, 0:sw],
                               ctx.buf[scratch["qabs"]][:, 0:sw],
                               ctx.buf[scratch["c1"]][:, 0:sw]))
                return eng.tensor_tensor(ctx.buf[scratch["xabs"]][:, 0:sw],
                                  ctx.buf[scratch["qabs"]][:, 0:sw],
                                  ctx.buf[psum["exp2"]][:, 0:sw], ALU.mult)
            plan.step("vector", f"{name}.q{g}", q_fn)

        def stage_split(g, scratch, psum):
            def sp_fn(eng, ctx, g=g, scratch=scratch):
                ctx.seq(eng, eng.tensor_copy(his[g][:, dcol0 + s0: dcol0 + s0 + sw],
                                ctx.buf[scratch["xabs"]][:, 0:sw]))
                return eng.tensor_tensor(los[g][:, dcol0 + s0: dcol0 + s0 + sw],
                                         ctx.buf[scratch["xabs"]][:, 0:sw],
                                         his[g][:, dcol0 + s0: dcol0 + s0 + sw],
                                         ALU.subtract)
            plan.step("gpsimd", f"{name}.sp{g}", sp_fn)

        stages = [stage_exp, stage_abs, stage_y, stage_sfld, stage_s12,
                  stage_gpa1, stage_s34, stage_gpa2, stage_s567, stage_gpa3,
                  stage_comb, stage_sgn, stage_q, stage_split]
        for g0 in range(0, npart_grp, 3):
            gs = [g for g in (g0, g0 + 1, g0 + 2) if g < npart_grp]
            assign = {}
            for g in gs:
                assign[g] = (scratch_sets[gser[0] % len(scratch_sets)],
                             psum_sets[gser[0] % len(psum_sets)])
                gser[0] += 1
            for stg in stages:
                def emit_stage(stg=stg, gs=gs, assign=assign):
                    for g in gs:
                        stg(g, assign[g][0], assign[g][1])
                stage_list.append(emit_stage)

    assert width <= 512
    seg_steps(0, width)
    if defer:
        return stage_list
    for f in stage_list:
        f()
    return None



def _np_fp4(x):
    GRID = np.array([0.0, 0.5, 1.0, 1.5, 2.0, 3.0, 4.0, 6.0], np.float32)
    sh = x.shape
    xb = x.reshape(sh[:-1] + (sh[-1] // 16, 16))
    sc = np.maximum(np.abs(xb).max(-1, keepdims=True) / 6.0, np.float32(1e-12)).astype(np.float32)
    xn = (xb / sc).astype(np.float32)
    idx = np.searchsorted(MID, np.abs(xn))
    q = GRID[idx] * np.sign(xn) * sc
    return q.reshape(sh).astype(np.float32)


def build_quant_test(wT):
    K, M = wT.shape
    ng = K // 128
    nc = bass.Bass("TRN2", target_bir_lowering=False)
    win = nc.dram_tensor("w", [K, M], F32, kind="ExternalInput").ap()
    idn_in = nc.dram_tensor("d_idn", [128, 128], F32, kind="ExternalInput").ap()
    e8_in = nc.dram_tensor("d_e64", [64, 1024], F32, kind="ExternalInput").ap()
    sb_in = nc.dram_tensor("d_stepb", [128, 7], F32, kind="ExternalInput").ap()
    qout = nc.dram_tensor("qo", [K, M], F32, kind="ExternalOutput").ap()

    plan = Plan()
    with contextlib.ExitStack() as st:
        sems = {}
        semnames = ["psync", "pv", "ps", "pt", "pg",
                    "dr_sync", "dr_vector", "dr_scalar", "dr_tensor", "dr_gpsimd"]
        semnames += [f"d_sync_{i}" for i in range(N_DSEM["sync"])]
        semnames += [f"d_gpsimd_{i}" for i in range(N_DSEM["gpsimd"])]
        for sname in semnames:
            sems[sname] = st.enter_context(nc.semaphore(sname))
        ctx = Ctx(nc, sems)

        def sb(nm, shape, dt=F32):
            ctx.buf[nm] = st.enter_context(nc.sbuf_tensor(nm, shape, dt))
            return nm

        def ps(nm, shape):
            ctx.buf[nm] = st.enter_context(nc.psum_tensor(nm, shape, F32))
            return nm

        wt = [sb(f"wt{g}", [128, M]) for g in range(ng)]
        hi = [sb(f"hi{g}", [128, M], BF16) for g in range(ng)]
        lo = [sb(f"lo{g}", [128, M], BF16) for g in range(ng)]
        wt_h = [ctx.buf[f"wt{g}"] for g in range(ng)]
        hi_h = [ctx.buf[f"hi{g}"] for g in range(ng)]
        lo_h = [ctx.buf[f"lo{g}"] for g in range(ng)]
        qf = [sb(f"qf{g}", [128, M]) for g in range(ng)]
        qsc2 = []
        for si in range(3):
            d = {}
            key = f"xabs.{si}"
            d["xabs"] = key; sb(key, [128, 512])
            for nm in ["c1", "c2", "accA", "c7", "qabs"]:
                key = f"{nm}.{si}"
                d[nm] = key; sb(key, [128, 512], BF16)
            qsc2.append(d)
        for nm, shp in [("bm", [128, 64]), ("rs", [128, 64]), ("ss", [128, 64]),
                        ("rsp", [64, 512]), ("ssp", [64, 512])]:
            for d in qsc2:
                d[nm] = nm
            sb(nm, shp)
        sb("idn", [128, 128])
        sb("e64", [64, 1024])
        sb("stepb", [128, 7])
        for pnm in ["p_tr", "p_rsT", "p_exp", "p_exp2", "p_a", "p_b", "p_c", "p_d"]:
            ps(pnm, [128, 512])

        def ld(eng, ctx):
            for g in range(ng):
                eng.dma_start(out=ctx.buf[f"wt{g}"][:, :], in_=win[g * 128:(g + 1) * 128, :]).then_inc(ctx.dma_sem(), 16)
            eng.dma_start(out=ctx.buf["idn"][:, :], in_=idn_in).then_inc(ctx.dma_sem(), 16)
            eng.dma_start(out=ctx.buf["e64"][:, :], in_=e8_in).then_inc(ctx.dma_sem(), 16)
            i = eng.dma_start(out=ctx.buf["stepb"][:, :], in_=sb_in)
            i.then_inc(ctx.dma_sem(), 16)
            return i
        plan.dma("sync", "load", ld, ng + 3, tag="in")

        def nopv(eng, ctx):
            return eng.tensor_copy(ctx.buf["xabs.0"][:, 0:1], ctx.buf["wt0"][:, 0:1])
        plan.step("vector", "gate", nopv, need_dma=["in"])

        psets = [{"tr": "p_tr", "rsT": "p_rsT", "exp": "p_exp", "exp2": "p_exp2"},
                 {"tr": "p_tr", "rsT": "p_rsT", "exp": "p_a", "exp2": "p_b"},
                 {"tr": "p_tr", "rsT": "p_rsT", "exp": "p_c", "exp2": "p_d"}]
        consts = {"idn": "idn", "e64": "e64", "stepb": "stepb"}
        for c0 in range(0, M, 512):
            emit_fm_quant(plan, f"q{c0}", wt_h, hi_h, lo_h, c0, min(512, M - c0), ng,
                          qsc2, psets, consts)

        for g in range(ng):
            def deq(eng, ctx, g=g):
                return eng.tensor_add(ctx.buf[f"qf{g}"][:, :], ctx.buf[f"hi{g}"][:, :], ctx.buf[f"lo{g}"][:, :])
            plan.step("vector", f"deq{g}", deq)

        def store(eng, ctx):
            for g in range(ng):
                i = eng.dma_start(out=qout[g * 128:(g + 1) * 128, :], in_=ctx.buf[f"qf{g}"][:, :])
                i.then_inc(ctx.dma_sem(), 16)
            return i
        plan.dma("gpsimd", "store", store, ng, tag="out")

        def fin(eng, ctx):
            for e in N_DSEM:
                for k in range(N_DSEM[e]):
                    tot = plan.dma_slot_count[e][k] * 16
                    if tot:
                        eng.wait_ge(ctx.sems[f"d_{e}_{k}"], tot)
            return eng.tensor_copy(ctx.buf["xabs.0"][:, 0:1], ctx.buf["xabs.0"][:, 0:1])
        plan.step("vector", "fin", fin)

        emit_program(nc, plan, ctx)
    return nc


def host_consts():
    idn = np.eye(128, dtype=np.float32)
    e8 = np.zeros((64, 1024), np.float32)
    for g in range(8):
        for c in range(128):
            e8[8 * g + c // 16, 128 * g + c] = 1.0
    stepb = np.tile(np.float32([-BIG * np.float32(t) for t in MID]), (128, 1)).astype(np.float32)
    return idn, e8, stepb


def _selftest_quant():
    rng = np.random.default_rng(0)
    M = 512
    w = (rng.standard_normal((M, 1024)) * 0.02).astype(np.float32)
    wT = np.ascontiguousarray(w.T)
    nc = build_quant_test(wT)
    idn, e8, stepb = host_consts()
    res = run_bass_kernel_spmd(nc, [{"w": wT, "d_idn": idn, "d_e64": e8, "d_stepb": stepb}], core_ids=[0])
    got = res.results[0]["qo"]
    ref = _np_fp4(w).T
    d = np.abs(got - ref)
    rel = np.abs(got - ref).max() / np.abs(ref).max()
    nbad = (d > 1e-5 * np.maximum(np.abs(ref), 1e-3)).sum()
    print(f"quant selftest: max_abs={d.max():.3e} rel={rel:.3e} nbad={nbad} / {ref.size}")
    bad = np.argwhere(d > 1e-3)
    for i, j in bad[:5]:
        print("  at", i, j, "got", got[i, j], "ref", ref[i, j], "w", wT[i, j])



FFN1_TILES = [(0, 256)] + [(256 + 512 * i, 512) for i in range(5)]
ATTN_TILES = [(0, 256)] + [(256 + 512 * i, 512) for i in range(5)]
CM_TILES = [(256 + 512 * i, 512) for i in range(5)]
FFN2_TILES = [(768 + 512 * i, 512) for i in range(4)]
NM_FF = IFF // 128
NOG_CM = ICM // 128


def build_full(phases=("wprep", "ffn1", "attn", "cm", "ffn2"), dump=None,
               deps=None, capture=None):
    nc = bass.Bass("TRN2", target_bir_lowering=False)
    dr = {}

    def din(name, shape, dt=F32):
        dr[name] = nc.dram_tensor("i_" + name, shape, dt, kind="ExternalInput").ap()
        return dr[name]

    def dint(name, shape, dt=F32):
        dr[name] = nc.dram_tensor("t_" + name, shape, dt, kind="Internal").ap()
        return dr[name]

    x_in = din("x", [D, TPRE])
    kmask_in = din("kmask", [128, TPRE])
    for w, sh in [("wk", [D, D]), ("wv", [D, D]), ("wr", [D, D]), ("wo", [D, D]),
                  ("wck", [D, ICM]), ("wcv", [ICM, D]), ("wcr", [D, D]),
                  ("wg_t", [D, IFF]), ("wu_t", [D, IFF]), ("wd_t", [IFF, D])]:
        din(w, sh)
    for p in ["ln0g", "ln0b", "ln1g", "ln1b", "tmk", "tmv", "tmr", "cmk", "cmr",
              "td", "tf"]:
        din(p, [128, NG])
    din("idn", [128, 128])
    din("e64", [64, 1024])
    din("stepb", [128, 7])
    din("onescol", [128, 1])
    din("browones", [1, 128])
    din("epsc", [128, 1])
    din("h0m", [128, 1])
    out = nc.dram_tensor("out", [D, TOWN], F32, kind="ExternalOutput").ap()

    sA = dint("sA", [D, TPRE])
    sB = dint("sB", [D, TPRE])
    spills = {}
    for wname, n0, M in [("wk", D // 128, D), ("wv", D // 128, D), ("wr", D // 128, D),
                         ("wo", D // 128, D), ("wck", D // 128, ICM),
                         ("wcv", ICM // 128, D), ("wcr", D // 128, D),
                         ("qg", D // 128, IFF), ("qu", D // 128, IFF),
                         ("qd", IFF // 128, D)]:
        spills[wname] = (
            nc.dram_tensor("sp_" + wname + "_h", [n0, 128, M], BF16, kind="Internal").ap(),
            nc.dram_tensor("sp_" + wname + "_l", [n0, 128, M], BF16, kind="Internal").ap(),
        )

    plan = Plan()
    tagn = [0]

    def tag():
        tagn[0] += 1
        return f"g{tagn[0]}"

    with contextlib.ExitStack() as st:
        sems = {}
        semnames = ["psync", "pv", "ps", "pt", "pg",
                    "dr_sync", "dr_vector", "dr_scalar", "dr_tensor", "dr_gpsimd"]
        semnames += [f"d_sync_{i}" for i in range(N_DSEM["sync"])]
        semnames += [f"d_gpsimd_{i}" for i in range(N_DSEM["gpsimd"])]
        for sname in semnames:
            sems[sname] = st.enter_context(nc.semaphore(sname))
        ctx = Ctx(nc, sems)

        nser = [0]

        def sb(stk, nm, shape, dt=F32):
            nser[0] += 1
            h = stk.enter_context(nc.sbuf_tensor(f"{nm}__{nser[0]}", shape, dt))
            ctx.buf[nm] = h
            ml = nc.lookup_mloc(h)
            npart = max(1, shape[0])
            ctx.addr_of[h.name] = ("sb", ml.addr, ml.addr + ml.size() // npart)
            return h

        def ps(stk, nm, shape=None):
            h = stk.enter_context(nc.psum_tensor(nm, shape or [128, 512], F32))
            ctx.buf[nm] = h
            ml = nc.lookup_mloc(h)
            sh = shape or [128, 512]
            ctx.addr_of[h.name] = ("ps", ml.addr, ml.addr + ml.size() // max(1, sh[0]))
            return h

        P = st
        idn = sb(P, "idn", [128, 128]); e64 = sb(P, "e64", [64, 1024])
        stepb = sb(P, "stepb", [128, 7]); onescol = sb(P, "onescol", [128, 1])
        brow = sb(P, "browones", [1, 128]); epsc = sb(P, "epsc", [128, 1])
        h0m = sb(P, "h0m", [128, 1])
        par = {p: sb(P, "p_" + p, [128, NG]) for p in
               ["ln0g", "ln0b", "ln1g", "ln1b", "tmk", "tmv", "tmr", "cmk", "cmr",
                "td", "tf"]}
        lam = sb(P, "lam", [128, NG]); eu = sb(P, "eu", [128, NG])
        st_a = sb(P, "st_a", [128, NG]); st_b = sb(P, "st_b", [128, NG])
        s_prev = sb(P, "s_prev", [128, NG]); s_cur = sb(P, "s_cur", [128, NG])
        ns_cur = sb(P, "ns_cur", [128, NG]); resc = sb(P, "resc", [128, NG])
        _rw = {nm: sb(P, "row_" + nm, [1, 512]) for nm in
               ["m", "A", "al", "t0", "t1", "sd"]}
        rows = {"m": _rw["m"], "A": _rw["A"], "al": _rw["al"], "sd": _rw["sd"],
                "sum": _rw["t0"], "sumsq": _rw["t1"], "msq": _rw["t1"],
                "var": _rw["t1"], "t2": _rw["t0"]}
        qsc2 = []
        for si in range(3):
            d = {}
            key = f"xabs.{si}"
            d["xabs"] = key; sb(P, key, [128, 512])
            for nm in ["c1", "c2", "accA", "c7", "qabs"]:
                key = f"{nm}.{si}"
                d[nm] = key; sb(P, key, [128, 512], BF16)
            qsc2.append(d)
        for nm, shp in [("bm", [128, 64]), ("rs", [128, 64]), ("ss", [128, 64]),
                        ("rsp", [64, 512]), ("ssp", [64, 512])]:
            for d in qsc2:
                d[nm] = nm
            sb(P, nm, shp)
        qsc = qsc2[0]
        sqs = sb(P, "sqs", [128, 512])
        consts = {"idn": "idn", "e64": "e64", "stepb": "stepb"}
        p_tr = ps(P, "p_tr"); p_rsT = ps(P, "p_rsT")
        p_exp = ps(P, "p_exp"); p_exp2 = ps(P, "p_exp2")
        p_a = ps(P, "p_a"); p_b = ps(P, "p_b")
        p_c = ps(P, "p_c"); p_d = ps(P, "p_d")
        qps = {"tr": "p_tr", "rsT": "p_rsT", "exp": "p_exp", "exp2": "p_exp2"}
        ogbank = [p_tr, p_rsT, p_exp, p_exp2, p_a, p_b, p_c, p_d]

        def ld_consts(eng, ctx):
            for nm, dst in [("idn", idn), ("e64", e64), ("stepb", stepb),
                            ("onescol", onescol), ("browones", brow), ("epsc", epsc),
                            ("h0m", h0m)]:
                eng.dma_start(out=dst[:, :], in_=dr[nm]).then_inc(ctx.dma_sem(), 16)
            last = None
            for p in par:
                last = eng.dma_start(out=par[p][:, :], in_=dr[p])
                last.then_inc(ctx.dma_sem(), 16)
            return last
        plan.dma("sync", "ldconsts", ld_consts, 7 + len(par), tag="consts")

        def derive(eng, ctx):
            ctx.seq(eng, eng.activation(lam[:, :], par["td"][:, :], AF.Exp))
            ctx.seq(eng, eng.activation(lam[:, :], lam[:, :], AF.Exp, scale=-1.0))
            return eng.activation(eu[:, :], par["tf"][:, :], AF.Exp)
        plan.step("scalar", "derive", derive, need_dma=["consts"])

        def init_state(eng, ctx):
            eng.memset(st_a[:, :], 0.0)
            eng.memset(st_b[:, :], 0.0)
            return eng.memset(s_prev[:, :], -1e30)
        plan.step("vector", "initstate", init_state)

        split_jobs = []
        if "wprep" in phases:
            def mk_split_job(wname, kg, cc0, cw):
                hi_d, lo_d = spills[wname]

                def emit(pidx):
                    t_in = tag()
                    def ldw(eng, ctx, wname=wname, kg=kg, cc0=cc0, cw=cw, pidx=pidx):
                        i = eng.dma_start(out=ctx.buf[f"wsf{pidx}"][:, 0:cw],
                                          in_=dr[wname][kg * 128:(kg + 1) * 128, cc0:cc0 + cw])
                        i.then_inc(ctx.dma_sem(), 16)
                        return i
                    plan.dma("gpsimd", f"ld_{wname}{kg}_{cc0}", ldw, 1, tag=t_in)

                    def split(eng, ctx, cw=cw, pidx=pidx):
                        ctx.seq(eng, eng.tensor_copy(ctx.buf[f"wsh{pidx}"][:, 0:cw],
                                                     ctx.buf[f"wsf{pidx}"][:, 0:cw]))
                        return eng.tensor_tensor(ctx.buf[f"wsl{pidx}"][:, 0:cw],
                                                 ctx.buf[f"wsf{pidx}"][:, 0:cw],
                                                 ctx.buf[f"wsh{pidx}"][:, 0:cw],
                                                 ALU.subtract)
                    plan.step("gpsimd", f"sp_{wname}{kg}_{cc0}", split, need_dma=[t_in])

                    def stw(eng, ctx, hi_d=hi_d, lo_d=lo_d, kg=kg, cc0=cc0, cw=cw, pidx=pidx):
                        eng.dma_start(out=hi_d[kg, :, cc0:cc0 + cw],
                                      in_=ctx.buf[f"wsh{pidx}"][:, 0:cw]).then_inc(ctx.dma_sem(), 16)
                        i = eng.dma_start(out=lo_d[kg, :, cc0:cc0 + cw],
                                          in_=ctx.buf[f"wsl{pidx}"][:, 0:cw])
                        i.then_inc(ctx.dma_sem(), 16)
                        return i
                    plan.dma("gpsimd", f"st_{wname}{kg}_{cc0}", stw, 2)
                return emit

            for wname, K, M in [("wk", D, D), ("wv", D, D), ("wr", D, D),
                                ("wo", D, D), ("wck", D, ICM),
                                ("wcv", ICM, D), ("wcr", D, D)]:
                for kg in range(K // 128):
                    for cc0 in range(0, M, 1024):
                        split_jobs.append(mk_split_job(wname, kg, cc0, min(1024, M - cc0)))

            with contextlib.ExitStack() as W:
                qsrc = [[sb(W, f"qw{c}_{g}", [128, 512]) for g in range(8)]
                        for c in range(2)]
                qhi = [[sb(W, f"qwh{c}_{g}", [128, 512], BF16) for g in range(8)]
                       for c in range(2)]
                qlo = [[sb(W, f"qwl{c}_{g}", [128, 512], BF16) for g in range(8)]
                       for c in range(2)]
                chn = [0]
                prev_store = [plan.last_store]

                def quant_one(wname, spname, K, M):
                    nslab = (K // 128 + 7) // 8
                    hi_d, lo_d = spills[spname]
                    full_slab = (nslab == 1)
                    for sl in range(nslab):
                        g0 = sl * 8
                        ngs = min(8, K // 128 - g0)
                        for c0 in range(0, M, 512):
                            cw = min(512, M - c0)
                            cc = chn[0] % 2
                            chn[0] += 1
                            t_in = tag()
                            def ldw(eng, ctx, wname=wname, g0=g0, ngs=ngs, c0=c0,
                                    cw=cw, cc=cc):
                                last = None
                                for gg in range(ngs):
                                    last = eng.dma_start(
                                        out=qsrc[cc][gg][:, 0:cw],
                                        in_=dr[wname][(g0 + gg) * 128:(g0 + gg + 1) * 128,
                                                      c0:c0 + cw])
                                    last.then_inc(ctx.dma_sem(), 16)
                                return last
                            plan.dma("sync", f"ldq_{wname}{sl}_{c0}", ldw, ngs,
                                     tag=t_in)
                            emit_fm_quant(plan, f"wq_{wname}{sl}_{c0}",
                                          qsrc[cc][:ngs], qhi[cc][:ngs], qlo[cc][:ngs],
                                          0, cw, ngs, qsc2,
                                          [qps, {"tr": "p_tr", "rsT": "p_rsT",
                                                 "exp": "p_a", "exp2": "p_b"},
                                           {"tr": "p_tr", "rsT": "p_rsT",
                                            "exp": "p_c", "exp2": "p_d"}],
                                          consts, dcol0=0)
                            t_out = tag()
                            def stw(eng, ctx, hi_d=hi_d, lo_d=lo_d, g0=g0,
                                    ngs=ngs, c0=c0, cw=cw, cc=cc):
                                last = None
                                for gg in range(ngs):
                                    eng.dma_start(out=hi_d[g0 + gg, :, c0:c0 + cw],
                                                  in_=qhi[cc][gg][:, 0:cw]).then_inc(ctx.dma_sem(), 16)
                                    last = eng.dma_start(out=lo_d[g0 + gg, :, c0:c0 + cw],
                                                         in_=qlo[cc][gg][:, 0:cw])
                                    last.then_inc(ctx.dma_sem(), 16)
                                return last
                            plan.dma("gpsimd", f"stq_{wname}{sl}_{c0}", stw,
                                     2 * ngs, tag=t_out)
                            prev_store[0] = t_out

                quant_one("wg_t", "qg", D, IFF)
                quant_one("wu_t", "qu", D, IFF)
                quant_one("wd_t", "qd", IFF, D)

        def ln_stats(name, xt, W, rows_extra=False, need=None,
                     bk=("p_a", "p_b", "p_c", "p_d", "p_exp")):
            def mm_sum(eng, ctx):
                last = None
                for g in range(NG):
                    last = eng.matmul(ctx.buf[bk[0]][0:1, 0:W], onescol[:, 0:1],
                                      xt[g][:, 0:W], start=(g == 0), stop=(g == NG - 1))
                return last
            plan.step("tensor", f"{name}.sum", mm_sum, need_dma=need or [])
            for g in range(NG):
                def sq_fn(eng, ctx, g=g):
                    return eng.activation(sqs[:, 0:W], xt[g][:, 0:W], AF.Square)
                plan.step("scalar", f"{name}.sq{g}", sq_fn)
                def mm_sq(eng, ctx, g=g):
                    return eng.matmul(ctx.buf["p_b"][0:1, 0:W], onescol[:, 0:1],
                                      sqs[:, 0:W], start=(g == 0),
                                      stop=(g == NG - 1))
                plan.step("tensor", f"{name}.msq{g}", mm_sq)

            def row_math(eng, ctx):
                r = rows
                ctx.seq(eng, eng.tensor_scalar_mul(r["m"][0:1, 0:W], ctx.buf[bk[0]][0:1, 0:W], 1.0 / D))
                ctx.seq(eng, eng.tensor_scalar_mul(r["msq"][0:1, 0:W], ctx.buf[bk[1]][0:1, 0:W], 1.0 / D))
                ctx.seq(eng, eng.tensor_mul(r["sd"][0:1, 0:W], r["m"][0:1, 0:W], r["m"][0:1, 0:W]))
                return eng.tensor_sub(r["var"][0:1, 0:W], r["msq"][0:1, 0:W], r["sd"][0:1, 0:W])
            plan.step("vector", f"{name}.rows", row_math)

            def row_sd(eng, ctx):
                return eng.activation(rows["sd"][0:1, 0:W], rows["var"][0:1, 0:W],
                                      AF.Sqrt, bias=epsc[0:1, 0:1])
            plan.step("scalar", f"{name}.sd", row_sd)

            def row_al(eng, ctx):
                return eng.reciprocal(rows["al"][0:1, 0:W], rows["sd"][0:1, 0:W])
            plan.step("vector", f"{name}.al", row_al)

            if rows_extra:
                def row_t2(eng, ctx):
                    ctx.seq(eng, eng.tensor_mul(rows["t2"][0:1, 0:W], rows["al"][0:1, 0:W], rows["al"][0:1, 0:W]))
                    return eng.tensor_mul(rows["t2"][0:1, 0:W], rows["t2"][0:1, 0:W],
                                          rows["var"][0:1, 0:W])
                plan.step("vector", f"{name}.t2", row_t2)
                def row_sd2(eng, ctx):
                    return eng.activation(rows["sd"][0:1, 0:W], rows["t2"][0:1, 0:W],
                                          AF.Sqrt, bias=epsc[0:1, 0:1])
                plan.step("scalar", f"{name}.sd2", row_sd2)
                def row_A(eng, ctx):
                    ctx.seq(eng, eng.reciprocal(rows["A"][0:1, 0:W], rows["sd"][0:1, 0:W]))
                    return eng.tensor_mul(rows["A"][0:1, 0:W], rows["A"][0:1, 0:W],
                                          rows["al"][0:1, 0:W])
                plan.step("vector", f"{name}.A", row_A)
            else:
                def row_A(eng, ctx):
                    return eng.tensor_copy(rows["A"][0:1, 0:W], rows["al"][0:1, 0:W])
                plan.step("vector", f"{name}.A", row_A)

            def bcast(eng, ctx):
                ctx.seq(eng, eng.matmul(ctx.buf[bk[2]][:, 0:W], brow[0:1, :],
                                        rows["m"][0:1, 0:W], start=True, stop=True))
                if rows_extra:
                    ctx.seq(eng, eng.matmul(ctx.buf[bk[4]][:, 0:W], brow[0:1, :],
                                            rows["al"][0:1, 0:W], start=True, stop=True))
                return eng.matmul(ctx.buf[bk[3]][:, 0:W], brow[0:1, :],
                                  rows["A"][0:1, 0:W], start=True, stop=True)
            plan.step("tensor", f"{name}.bc", bcast)

        def ln_apply(name, xt, dstf, W, gamma=None, beta=None, dst_off=0,
                     ybuf=None, bk=("p_a", "p_b", "p_c", "p_d", "p_exp")):
            for g in range(NG):
                def ap_fn(eng, ctx, g=g):
                    ctx.seq(eng, eng.tensor_tensor(sqs[:, 0:W], xt[g][:, 0:W],
                                                   ctx.buf[bk[2]][:, 0:W], ALU.subtract))
                    if ybuf is not None:
                        ctx.seq(eng, eng.tensor_tensor(ybuf[g][:, 0:W], sqs[:, 0:W],
                                                       ctx.buf[bk[4]][:, 0:W], ALU.mult))
                    if gamma is None:
                        return eng.tensor_tensor(dstf[g][:, dst_off:dst_off + W],
                                                 sqs[:, 0:W],
                                                 ctx.buf[bk[3]][:, 0:W], ALU.mult)
                    ctx.seq(eng, eng.tensor_tensor(sqs[:, 0:W],
                                                   sqs[:, 0:W],
                                                   ctx.buf[bk[3]][:, 0:W], ALU.mult))
                    return eng.tensor_scalar(dstf[g][:, dst_off:dst_off + W],
                                             sqs[:, 0:W],
                                             gamma[:, g:g + 1], beta[:, g:g + 1],
                                             ALU.mult, ALU.add)
                plan.step("vector", f"{name}.ap{g}", ap_fn)

        def ffn_phase(name, tiles, src_stream, dst_stream, out_coloff=None,
                      side_jobs=None):
            with contextlib.ExitStack() as S:
                if side_jobs:
                    for pidx in range(2):
                        sb(S, f"wsf{pidx}", [128, 1024])
                        sb(S, f"wsh{pidx}", [128, 1024], BF16)
                        sb(S, f"wsl{pidx}", [128, 1024], BF16)
                xt = [sb(S, f"{name}_x{g}", [128, 512]) for g in range(NG)]
                xc = [sb(S, f"{name}_c{g}", [128, 512]) for g in range(NG)]
                xqh = [sb(S, f"{name}_qh{g}", [128, 512], BF16) for g in range(NG)]
                xql = [sb(S, f"{name}_ql{g}", [128, 512], BF16) for g in range(NG)]
                hqh = [sb(S, f"{name}_hh{m}", [128, 512], BF16) for m in range(NM_FF)]
                hql = [sb(S, f"{name}_hl{m}", [128, 512], BF16) for m in range(NM_FF)]
                wga = [sb(S, f"wga{i}", [128, 1024], BF16) for i in range(2)]
                wgb = [sb(S, f"wgb{i}", [128, 1024], BF16) for i in range(2)]
                wua = [sb(S, f"wua{i}", [128, 1024], BF16) for i in range(2)]
                wub = [sb(S, f"wub{i}", [128, 1024], BF16) for i in range(2)]
                wda = [sb(S, f"wda{i}", [128, 2816], BF16) for i in range(2)]
                wdb = [sb(S, f"wdb{i}", [128, 2816], BF16) for i in range(2)]
                x2t = [sb(S, f"x2t{i}", [128, 512]) for i in range(2)]
                sgsr = [sb(S, f"sgsr{i}", [128, 512]) for i in range(2)]
                hscr = sgsr
                prev_x2store = [plan.last_store]
                jobs_left = list(side_jobs or [])
                per_tile = (len(jobs_left) + len(tiles) - 1) // max(1, len(tiles))
                jpidx = [0]

                for (c0, W) in tiles:
                    for _ in range(min(per_tile, len(jobs_left))):
                        jobs_left.pop(0)(jpidx[0] % 2)
                        jpidx[0] += 1
                    t_x = tag()
                    def ld_x(eng, ctx, c0=c0, W=W):
                        last = None
                        for g in range(NG):
                            last = eng.dma_start(out=xt[g][:, 0:W],
                                                 in_=src_stream[g * 128:(g + 1) * 128, c0:c0 + W])
                            last.then_inc(ctx.dma_sem(), 16)
                        return last
                    plan.dma("sync", f"{name}.ldx{c0}", ld_x, NG, tag=t_x,
                             need_dma=[prev_x2store[0]] if prev_x2store[0] else [])

                    ln_stats(f"{name}.ln{c0}", xt, W, rows_extra=False, need=[t_x])
                    ln_apply(f"{name}.lna{c0}", xt, xc, W)
                    emit_fm_quant(plan, f"{name}.xq{c0}", xc, xqh, xql, 0, W, NG,
                                  qsc2, [qps, {"tr": "p_tr", "rsT": "p_rsT",
                                               "exp": "p_a", "exp2": "p_b"},
                                         {"tr": "p_tr", "rsT": "p_rsT",
                                          "exp": "p_c", "exp2": "p_d"}],
                                  consts, dcol0=0)

                    def emit_mm_gu(m):
                        p = m % 2
                        gbank, ubank = ("p_a", "p_b") if p == 0 else ("p_c", "p_d")
                        t_w = tag()
                        def ld_w(eng, ctx, m=m, p=p):
                            last = None
                            for nmw, dsts in (("qg", (wga[p], wgb[p])), ("qu", (wua[p], wub[p]))):
                                hi_d, lo_d = spills[nmw]
                                src_h = hi_d[:, :, m * 128:(m + 1) * 128].transpose([1, 0, 2])
                                src_l = lo_d[:, :, m * 128:(m + 1) * 128].transpose([1, 0, 2])
                                eng.dma_start(out=dsts[0][:, :].rearrange("p (k c) -> p k c", c=128),
                                              in_=src_h).then_inc(ctx.dma_sem(), 16)
                                last = eng.dma_start(out=dsts[1][:, :].rearrange("p (k c) -> p k c", c=128),
                                                     in_=src_l)
                                last.then_inc(ctx.dma_sem(), 16)
                            return last
                        plan.dma("sync", f"{name}.ldw{c0}_{m}", ld_w, 4, tag=t_w)

                        def mm_gu(eng, ctx, W=W, p=p, gbank=gbank, ubank=ubank):
                            last = None
                            for (pbank, wa, wb) in ((gbank, wga[p], wgb[p]),
                                                    (ubank, wua[p], wub[p])):
                                for kg in range(NG):
                                    sl = slice(kg * 128, (kg + 1) * 128)
                                    eng.matmul(ctx.buf[pbank][:, 0:W], wa[:, sl], xqh[kg][:, 0:W],
                                               start=(kg == 0), stop=False)
                                    eng.matmul(ctx.buf[pbank][:, 0:W], wa[:, sl], xql[kg][:, 0:W],
                                               start=False, stop=False)
                                    last = eng.matmul(ctx.buf[pbank][:, 0:W], wb[:, sl], xqh[kg][:, 0:W],
                                                      start=False, stop=(kg == NG - 1))
                            return last
                        plan.step("tensor", f"{name}.gu{c0}_{m}", mm_gu, need_dma=[t_w])

                    def emit_post(m):
                        p = m % 2
                        ps3 = m % 3
                        gbank, ubank = ("p_a", "p_b") if p == 0 else ("p_c", "p_d")
                        def silu_fn(eng, ctx, W=W, p=p, gbank=gbank):
                            return eng.activation(sgsr[p][:, 0:W], ctx.buf[gbank][:, 0:W], AF.Sigmoid)
                        plan.step("scalar", f"{name}.silu{c0}_{m}", silu_fn)

                        def h_fn(eng, ctx, W=W, p=p, gbank=gbank, ubank=ubank):
                            ctx.seq(eng, eng.tensor_tensor(sgsr[p][:, 0:W], sgsr[p][:, 0:W],
                                                           ctx.buf[gbank][:, 0:W], ALU.mult))
                            return eng.tensor_tensor(sgsr[p][:, 0:W], sgsr[p][:, 0:W],
                                                     ctx.buf[ubank][:, 0:W], ALU.mult)
                        plan.step("vector", f"{name}.h{c0}_{m}", h_fn)

                        return emit_fm_quant(plan, f"{name}.hq{c0}_{m}", [hscr[p]],
                                             [hqh[m]], [hql[m]], 0, W, 1, qsc2[ps3],
                                             qps, consts, dcol0=0, defer=True)

                    pending = []
                    for m in range(NM_FF):
                        emit_mm_gu(m)
                        for qch in pending:
                            k = 9
                            for f in qch[:k]:
                                f()
                            del qch[:k]
                        pending = [q for q in pending if q]
                        if m > 0:
                            pending.append(emit_post(m - 1))
                    for qch in pending:
                        for f in qch:
                            f()
                    for f in emit_post(NM_FF - 1):
                        f()

                    for og in range(NG):
                        p = og % 2
                        dbank = "p_a" if p == 0 else "p_b"
                        t_w = tag()
                        def ld_wd(eng, ctx, og=og, p=p):
                            hi_d, lo_d = spills["qd"]
                            src_h = hi_d[:, :, og * 128:(og + 1) * 128].transpose([1, 0, 2])
                            src_l = lo_d[:, :, og * 128:(og + 1) * 128].transpose([1, 0, 2])
                            eng.dma_start(out=wda[p][:, :].rearrange("p (k c) -> p k c", c=128),
                                          in_=src_h).then_inc(ctx.dma_sem(), 16)
                            last = eng.dma_start(out=wdb[p][:, :].rearrange("p (k c) -> p k c", c=128),
                                                 in_=src_l)
                            last.then_inc(ctx.dma_sem(), 16)
                            return last
                        plan.dma("sync", f"{name}.ldwd{c0}_{og}", ld_wd, 2, tag=t_w)

                        def mm_dn(eng, ctx, W=W, p=p, dbank=dbank):
                            last = None
                            for m in range(NM_FF):
                                sl = slice(m * 128, (m + 1) * 128)
                                eng.matmul(ctx.buf[dbank][:, 0:W], wda[p][:, sl], hqh[m][:, 0:W],
                                           start=(m == 0), stop=False)
                                eng.matmul(ctx.buf[dbank][:, 0:W], wda[p][:, sl], hql[m][:, 0:W],
                                           start=False, stop=False)
                                last = eng.matmul(ctx.buf[dbank][:, 0:W], wdb[p][:, sl], hqh[m][:, 0:W],
                                                  start=False, stop=(m == NM_FF - 1))
                            return last
                        plan.step("tensor", f"{name}.dn{c0}_{og}", mm_dn, need_dma=[t_w])

                        def resid(eng, ctx, og=og, W=W, p=p, dbank=dbank):
                            return eng.scalar_tensor_tensor(x2t[p][:, 0:W], ctx.buf[dbank][:, 0:W],
                                                            0.5, xt[og][:, 0:W],
                                                            ALU.mult, ALU.add)
                        plan.step("vector", f"{name}.res{c0}_{og}", resid)

                        t_st = tag()
                        def st_x2(eng, ctx, og=og, c0=c0, W=W, p=p):
                            if out_coloff is not None:
                                dst = out[og * 128:(og + 1) * 128, c0 - out_coloff:c0 - out_coloff + W]
                            else:
                                dst = dst_stream[og * 128:(og + 1) * 128, c0:c0 + W]
                            i = eng.dma_start(out=dst, in_=x2t[p][:, 0:W])
                            i.then_inc(ctx.dma_sem(), 16)
                            return i
                        plan.dma("gpsimd", f"{name}.stx{c0}_{og}", st_x2, 1, tag=t_st)
                        prev_x2store[0] = t_st

        if "ffn1" in phases:
            ffn_phase("f1", FFN1_TILES, x_in, sA, side_jobs=split_jobs)
        elif split_jobs:
            with contextlib.ExitStack() as W:
                for pidx in range(2):
                    sb(W, f"wsf{pidx}", [128, 1024])
                    sb(W, f"wsh{pidx}", [128, 1024], BF16)
                    sb(W, f"wsl{pidx}", [128, 1024], BF16)
                for j, job in enumerate(split_jobs):
                    job(j % 2)

        def attn_phase():
            with contextlib.ExitStack() as S:
                x2t = [sb(S, f"at_x{g}", [128, 512]) for g in range(NG)]
                yt = x2t
                xab = [sb(S, f"at_a{g}", [128, 513]) for g in range(NG)]
                ks = [sb(S, f"at_k{g}", [128, 512]) for g in range(NG)]
                vs = [sb(S, f"at_v{g}", [128, 512]) for g in range(NG)]
                rs_ = [sb(S, f"at_r{g}", [128, 512]) for g in range(NG)]
                sab = [sb(S, f"at_sa{i}", [128, 513]) for i in range(2)]
                sbb = [sb(S, f"at_sb{i}", [128, 513]) for i in range(2)]
                dsc = [sb(S, f"dsc{i}", [128, 512]) for i in range(2)]
                sc1 = [sb(S, f"sc1_{i}", [128, 512]) for i in range(2)]
                sc2 = [sb(S, f"sc2_{i}", [128, 512]) for i in range(2)]
                rwh = [sb(S, f"at_rh{g}", [128, 512], BF16) for g in range(NG)]
                rwl = [sb(S, f"at_rl{g}", [128, 512], BF16) for g in range(NG)]
                kmt = sb(S, "kmt", [128, 512])
                wsa = [sb(S, f"wsa{i}", [128, 1024], BF16) for i in range(2)]
                wsb_ = [sb(S, f"wsb{i}", [128, 1024], BF16) for i in range(2)]
                mxf2 = [sb(S, f"mxf{i}", [128, 512]) for i in range(2)]
                mxh2 = [sb(S, f"mxh{i}", [128, 512], BF16) for i in range(2)]
                mxl2 = [sb(S, f"mxl{i}", [128, 512], BF16) for i in range(2)]

                def zero_halo(eng, ctx):
                    last = None
                    for g in range(NG):
                        last = eng.memset(xab[g][:, 0:1], 0.0)
                    return last
                plan.step("vector", "at.zh", zero_halo, need_dma=[plan.last_store] if plan.last_store else [])
                prev_x3store = [plan.last_store]
                prev_W = [None]

                for ti, (c0, W) in enumerate(ATTN_TILES):
                    t_x = tag()
                    def ld_x(eng, ctx, c0=c0, W=W):
                        for g in range(NG):
                            eng.dma_start(out=x2t[g][:, 0:W],
                                          in_=sA[g * 128:(g + 1) * 128, c0:c0 + W]).then_inc(ctx.dma_sem(), 16)
                        i = eng.dma_start(out=kmt[:, 0:W], in_=kmask_in[:, c0:c0 + W])
                        i.then_inc(ctx.dma_sem(), 16)
                        return i
                    plan.dma("sync", f"at.ldx{c0}", ld_x, NG + 1, tag=t_x,
                             need_dma=[prev_x3store[0]] if prev_x3store[0] else [])

                    if ti > 0:
                        def halo(eng, ctx, pw=prev_W[0], c0=c0):
                            last = None
                            for g in range(NG):
                                last = eng.tensor_copy(xab[g][:, 0:1], xab[g][:, pw:pw + 1])
                                if c0 == LB:
                                    ctx.seq(eng, last)
                                    last = eng.tensor_scalar_mul(xab[g][:, 0:1], xab[g][:, 0:1],
                                                                 ctx.buf["h0m"][:, 0:1])
                            return last
                        plan.step("vector", f"at.halo{c0}", halo)
                    prev_W[0] = W

                    ln_stats(f"at.ln{c0}", x2t, W, rows_extra=True, need=[t_x])
                    ln_apply(f"at.lna{c0}", x2t, xab, W, gamma=par["ln0g"],
                             beta=par["ln0b"], dst_off=1, ybuf=yt)

                    mats = [("wk", "tmk", ks, "k"), ("wv", "tmv", vs, "v")]
                    if ti > 0:
                        mats.append(("wr", "tmr", rs_, "r"))
                    for (wname, tmname, dsts, mt) in mats:
                        for kg in range(NG):
                            p = kg % 2
                            t_w = tag()
                            def ld_w(eng, ctx, wname=wname, kg=kg, p=p):
                                hi_d, lo_d = spills[wname]
                                eng.dma_start(out=wsa[p][:, :], in_=hi_d[kg, :, :]).then_inc(ctx.dma_sem(), 16)
                                i = eng.dma_start(out=wsb_[p][:, :], in_=lo_d[kg, :, :])
                                i.then_inc(ctx.dma_sem(), 16)
                                return i
                            plan.dma("sync", f"at.lw{mt}{c0}_{kg}", ld_w, 2, tag=t_w)

                            def mix(eng, ctx, kg=kg, tmname=tmname, W=W, p=p):
                                ctx.seq(eng, eng.tensor_tensor(dsc[p][:, 0:W],
                                                               xab[kg][:, 1:W + 1],
                                                               xab[kg][:, 0:W], ALU.subtract))
                                ctx.seq(eng, eng.scalar_tensor_tensor(mxf2[p][:, 0:W], dsc[p][:, 0:W],
                                                                      par[tmname][:, kg:kg + 1],
                                                                      xab[kg][:, 0:W],
                                                                      ALU.mult, ALU.add))
                                ctx.seq(eng, eng.tensor_copy(mxh2[p][:, 0:W], mxf2[p][:, 0:W]))
                                return eng.tensor_tensor(mxl2[p][:, 0:W], mxf2[p][:, 0:W],
                                                         mxh2[p][:, 0:W], ALU.subtract)
                            plan.step("vector", f"at.mx{mt}{c0}_{kg}", mix)

                            def mm(eng, ctx, kg=kg, W=W, p=p):
                                last = None
                                for og in range(NG):
                                    sl = slice(og * 128, (og + 1) * 128)
                                    eng.matmul(ogbank[og][:, 0:W], wsa[p][:, sl], mxh2[p][:, 0:W],
                                               start=(kg == 0), stop=False)
                                    eng.matmul(ogbank[og][:, 0:W], wsa[p][:, sl], mxl2[p][:, 0:W],
                                               start=False, stop=False)
                                    last = eng.matmul(ogbank[og][:, 0:W], wsb_[p][:, sl], mxh2[p][:, 0:W],
                                                      start=False, stop=(kg == NG - 1))
                                return last
                            plan.step("tensor", f"at.mm{mt}{c0}_{kg}", mm, need_dma=[t_w])

                        if mt == "k":
                            for og in range(NG):
                                def ev(eng, ctx, og=og, W=W):
                                    return eng.tensor_tensor(ks[og][:, 0:W],
                                                             ogbank[og][:, 0:W],
                                                             kmt[:, 0:W], ALU.add)
                                plan.step("vector", f"at.evk{c0}_{og}", ev)
                        elif mt == "v":
                            for og in range(NG):
                                def ev(eng, ctx, og=og, W=W):
                                    return eng.tensor_copy(vs[og][:, 0:W], ogbank[og][:, 0:W])
                                plan.step("vector", f"at.evv{c0}_{og}", ev)
                        else:
                            for og in range(NG):
                                def ev(eng, ctx, og=og, W=W):
                                    return eng.activation(rs_[og][:, 0:W], ogbank[og][:, 0:W],
                                                          AF.Sigmoid)
                                plan.step("scalar", f"at.evr{c0}_{og}", ev)

                    def smax(eng, ctx, W=W):
                        last = None
                        for g in range(NG):
                            last = eng.tensor_reduce(s_cur[:, g:g + 1],
                                                     ks[g][:, 0:W], AX.X, ALU.max)
                        return last
                    plan.step("vector", f"at.smax{c0}", smax)

                    def sresc(eng, ctx):
                        ctx.seq(eng, eng.tensor_scalar_max(s_cur[:, :], s_cur[:, :], -60.0))
                        ctx.seq(eng, eng.tensor_sub(resc[:, :], s_prev[:, :], s_cur[:, :]))
                        return eng.tensor_scalar_mul(ns_cur[:, :], s_cur[:, :], -1.0)
                    plan.step("vector", f"at.sresc{c0}", sresc)

                    def sexp(eng, ctx):
                        return eng.activation(resc[:, :], resc[:, :], AF.Exp)
                    plan.step("scalar", f"at.sexp{c0}", sexp)

                    def stfix(eng, ctx):
                        ctx.seq(eng, eng.tensor_mul(st_a[:, :], st_a[:, :], resc[:, :]))
                        ctx.seq(eng, eng.tensor_mul(st_b[:, :], st_b[:, :], resc[:, :]))
                        return eng.tensor_copy(s_prev[:, :], s_cur[:, :])
                    plan.step("vector", f"at.stfix{c0}", stfix)

                    for g in range(NG):
                        def kap(eng, ctx, g=g, W=W):
                            return eng.activation(ks[g][:, 0:W], ks[g][:, 0:W], AF.Exp,
                                                  bias=ns_cur[:, g:g + 1])
                        plan.step("scalar", f"at.kap{c0}_{g}", kap)
                        def kv(eng, ctx, g=g, W=W):
                            return eng.tensor_mul(vs[g][:, 0:W], ks[g][:, 0:W], vs[g][:, 0:W])
                        plan.step("vector", f"at.kv{c0}_{g}", kv)

                    for g in range(NG):
                        p = g % 2
                        def scan(eng, ctx, g=g, W=W, p=p):
                            lam_bc = lam[:, g:g + 1].to_broadcast([128, W])
                            ctx.seq(eng, eng.tensor_copy(sab[p][:, 0:1], st_a[:, g:g + 1]))
                            ctx.seq(eng, eng.tensor_copy(sbb[p][:, 0:1], st_b[:, g:g + 1]))
                            ctx.seq(eng, eng.tensor_tensor_scan(sab[p][:, 1:W + 1], lam_bc,
                                                                vs[g][:, 0:W], st_a[:, g:g + 1],
                                                                ALU.mult, ALU.add))
                            ctx.seq(eng, eng.tensor_tensor_scan(sbb[p][:, 1:W + 1], lam_bc,
                                                                ks[g][:, 0:W], st_b[:, g:g + 1],
                                                                ALU.mult, ALU.add))
                            ctx.seq(eng, eng.tensor_copy(st_a[:, g:g + 1], sab[p][:, W:W + 1]))
                            return eng.tensor_copy(st_b[:, g:g + 1], sbb[p][:, W:W + 1])
                        plan.step("vector", f"at.scan{c0}_{g}", scan)

                        if ti == 0:
                            continue

                        def wkv(eng, ctx, g=g, W=W, p=p):
                            s1 = sc1[p][:, 0:W]
                            s2 = sc2[p][:, 0:W]
                            ctx.seq(eng, eng.tensor_scalar_mul(s1, ks[g][:, 0:W],
                                                               eu[:, g:g + 1]))
                            ctx.seq(eng, eng.tensor_add(s1, s1, sbb[p][:, 0:W]))
                            ctx.seq(eng, eng.tensor_scalar_max(s1, s1, 1e-30))
                            ctx.seq(eng, eng.reciprocal(s2, s1))
                            ctx.seq(eng, eng.tensor_scalar_mul(s1, vs[g][:, 0:W],
                                                               eu[:, g:g + 1]))
                            ctx.seq(eng, eng.tensor_add(s1, s1, sab[p][:, 0:W]))
                            ctx.seq(eng, eng.tensor_mul(s1, s1, s2))
                            ctx.seq(eng, eng.tensor_mul(s1, s1, rs_[g][:, 0:W]))
                            ctx.seq(eng, eng.tensor_copy(rwh[g][:, 0:W], s1))
                            return eng.tensor_tensor(rwl[g][:, 0:W], s1,
                                                     rwh[g][:, 0:W], ALU.subtract)
                        plan.step("vector", f"at.wkv{c0}_{g}", wkv)

                    if ti == 0:
                        continue

                    for kg in range(NG):
                        p = kg % 2
                        t_w = tag()
                        def ld_w(eng, ctx, kg=kg, p=p):
                            hi_d, lo_d = spills["wo"]
                            eng.dma_start(out=wsa[p][:, :], in_=hi_d[kg, :, :]).then_inc(ctx.dma_sem(), 16)
                            i = eng.dma_start(out=wsb_[p][:, :], in_=lo_d[kg, :, :])
                            i.then_inc(ctx.dma_sem(), 16)
                            return i
                        plan.dma("sync", f"at.lwo{c0}_{kg}", ld_w, 2, tag=t_w)

                        def mm(eng, ctx, kg=kg, W=W, p=p):
                            last = None
                            for og in range(NG):
                                sl = slice(og * 128, (og + 1) * 128)
                                eng.matmul(ogbank[og][:, 0:W], wsa[p][:, sl], rwh[kg][:, 0:W],
                                           start=(kg == 0), stop=False)
                                eng.matmul(ogbank[og][:, 0:W], wsa[p][:, sl], rwl[kg][:, 0:W],
                                           start=False, stop=False)
                                last = eng.matmul(ogbank[og][:, 0:W], wsb_[p][:, sl], rwh[kg][:, 0:W],
                                                  start=False, stop=(kg == NG - 1))
                            return last
                        plan.step("tensor", f"at.mmo{c0}_{kg}", mm, need_dma=[t_w])

                    for og in range(NG):
                        def x3(eng, ctx, og=og, W=W):
                            return eng.tensor_add(yt[og][:, 0:W], ogbank[og][:, 0:W],
                                                  yt[og][:, 0:W])
                        plan.step("vector", f"at.x3{c0}_{og}", x3)

                        t_st = tag()
                        def st_x3(eng, ctx, og=og, c0=c0, W=W):
                            i = eng.dma_start(out=sB[og * 128:(og + 1) * 128, c0:c0 + W],
                                              in_=yt[og][:, 0:W])
                            i.then_inc(ctx.dma_sem(), 16)
                            return i
                        plan.dma("gpsimd", f"at.stx3{c0}_{og}", st_x3, 1, tag=t_st)
                        prev_x3store[0] = t_st

        if "attn" in phases:
            attn_phase()

        def cm_phase():
            with contextlib.ExitStack() as S:
                x3t = [sb(S, f"cm_x{g}", [128, 512]) for g in range(NG)]
                x2one = [sb(S, f"cm_x2o{i}", [128, 512]) for i in range(2)]
                xcb = [sb(S, f"cm_c{g}", [128, 513]) for g in range(NG)]
                rrs = [sb(S, f"cm_rr{g}", [128, 512]) for g in range(NG)]
                rkh = [sb(S, f"cm_kh{o}", [128, 512], BF16) for o in range(NOG_CM)]
                rkl = [sb(S, f"cm_kl{o}", [128, 512], BF16) for o in range(NOG_CM)]
                dsc = [sb(S, f"dsc{i}", [128, 512]) for i in range(2)]
                mxf2 = [sb(S, f"mxf{i}", [128, 512]) for i in range(2)]
                mxh2 = [sb(S, f"mxh{i}", [128, 512], BF16) for i in range(2)]
                mxl2 = [sb(S, f"mxl{i}", [128, 512], BF16) for i in range(2)]
                sc1 = [sb(S, f"sc1_{i}", [128, 512]) for i in range(2)]
                wsa = [sb(S, f"wsa{i}", [128, 1024], BF16) for i in range(2)]
                wsb_ = [sb(S, f"wsb{i}", [128, 1024], BF16) for i in range(2)]

                def zero_halo(eng, ctx):
                    last = None
                    for g in range(NG):
                        last = eng.memset(xcb[g][:, 0:1], 0.0)
                    return last
                plan.step("vector", "cm.zh", zero_halo, need_dma=[plan.last_store] if plan.last_store else [])
                prev_store = [plan.last_store]
                prev_W = [None]

                for ti, (c0, W) in enumerate(CM_TILES):
                    t_x = tag()
                    def ld_x(eng, ctx, c0=c0, W=W):
                        last = None
                        for g in range(NG):
                            last = eng.dma_start(out=x3t[g][:, 0:W],
                                                 in_=sB[g * 128:(g + 1) * 128, c0:c0 + W])
                            last.then_inc(ctx.dma_sem(), 16)
                        return last
                    plan.dma("sync", f"cm.ldx{c0}", ld_x, NG,
                             tag=t_x, need_dma=[prev_store[0]] if prev_store[0] else [])

                    if ti > 0:
                        def halo(eng, ctx, pw=prev_W[0], c0=c0):
                            last = None
                            for g in range(NG):
                                last = eng.tensor_copy(xcb[g][:, 0:1], xcb[g][:, pw:pw + 1])
                                if c0 == LB:
                                    ctx.seq(eng, last)
                                    last = eng.tensor_scalar_mul(xcb[g][:, 0:1], xcb[g][:, 0:1],
                                                                 ctx.buf["h0m"][:, 0:1])
                            return last
                        plan.step("vector", f"cm.halo{c0}", halo)
                    prev_W[0] = W

                    ln_stats(f"cm.ln{c0}", x3t, W, rows_extra=False, need=[t_x])
                    ln_apply(f"cm.lna{c0}", x3t, xcb, W, gamma=par["ln1g"],
                             beta=par["ln1b"], dst_off=1)

                    if ti == 0:
                        continue

                    for kg in range(NG):
                        p = kg % 2
                        t_w = tag()
                        def ld_w(eng, ctx, kg=kg, p=p):
                            hi_d, lo_d = spills["wcr"]
                            eng.dma_start(out=wsa[p][:, 0:D], in_=hi_d[kg, :, :]).then_inc(ctx.dma_sem(), 16)
                            i = eng.dma_start(out=wsb_[p][:, 0:D], in_=lo_d[kg, :, :])
                            i.then_inc(ctx.dma_sem(), 16)
                            return i
                        plan.dma("sync", f"cm.lwr{c0}_{kg}", ld_w, 2, tag=t_w)

                        def mix(eng, ctx, kg=kg, W=W, p=p):
                            ctx.seq(eng, eng.tensor_tensor(dsc[p][:, 0:W], xcb[kg][:, 1:W + 1],
                                                           xcb[kg][:, 0:W], ALU.subtract))
                            ctx.seq(eng, eng.scalar_tensor_tensor(mxf2[p][:, 0:W], dsc[p][:, 0:W],
                                                                  par["cmr"][:, kg:kg + 1],
                                                                  xcb[kg][:, 0:W],
                                                                  ALU.mult, ALU.add))
                            ctx.seq(eng, eng.tensor_copy(mxh2[p][:, 0:W], mxf2[p][:, 0:W]))
                            return eng.tensor_tensor(mxl2[p][:, 0:W], mxf2[p][:, 0:W],
                                                     mxh2[p][:, 0:W], ALU.subtract)
                        plan.step("vector", f"cm.mxr{c0}_{kg}", mix)

                        def mm(eng, ctx, kg=kg, W=W, p=p):
                            last = None
                            for og in range(NG):
                                sl = slice(og * 128, (og + 1) * 128)
                                eng.matmul(ogbank[og][:, 0:W], wsa[p][:, sl], mxh2[p][:, 0:W],
                                           start=(kg == 0), stop=False)
                                eng.matmul(ogbank[og][:, 0:W], wsa[p][:, sl], mxl2[p][:, 0:W],
                                           start=False, stop=False)
                                last = eng.matmul(ogbank[og][:, 0:W], wsb_[p][:, sl], mxh2[p][:, 0:W],
                                                  start=False, stop=(kg == NG - 1))
                            return last
                        plan.step("tensor", f"cm.mmr{c0}_{kg}", mm, need_dma=[t_w])
                    for og in range(NG):
                        def ev(eng, ctx, og=og, W=W):
                            return eng.activation(rrs[og][:, 0:W], ogbank[og][:, 0:W],
                                                  AF.Sigmoid)
                        plan.step("scalar", f"cm.evr{c0}_{og}", ev)

                    for kg in range(NG):
                        p = kg % 2
                        def mix(eng, ctx, kg=kg, W=W, p=p):
                            ctx.seq(eng, eng.tensor_tensor(dsc[p][:, 0:W], xcb[kg][:, 1:W + 1],
                                                           xcb[kg][:, 0:W], ALU.subtract))
                            ctx.seq(eng, eng.scalar_tensor_tensor(mxf2[p][:, 0:W], dsc[p][:, 0:W],
                                                                  par["cmk"][:, kg:kg + 1],
                                                                  xcb[kg][:, 0:W],
                                                                  ALU.mult, ALU.add))
                            ctx.seq(eng, eng.tensor_copy(rkh[NOG_CM - 8 + kg][:, 0:W], mxf2[p][:, 0:W]))
                            return eng.tensor_tensor(rkl[NOG_CM - 8 + kg][:, 0:W], mxf2[p][:, 0:W],
                                                     rkh[NOG_CM - 8 + kg][:, 0:W], ALU.subtract)
                        plan.step("vector", f"cm.mxk{c0}_{kg}", mix)

                    for rnd in range(4):
                        for kg in range(NG):
                            p = kg % 2
                            t_w = tag()
                            def ld_w(eng, ctx, kg=kg, rnd=rnd, p=p):
                                hi_d, lo_d = spills["wck"]
                                eng.dma_start(out=wsa[p][:, 0:1024],
                                              in_=hi_d[kg, :, rnd * 1024:(rnd + 1) * 1024]).then_inc(ctx.dma_sem(), 16)
                                i = eng.dma_start(out=wsb_[p][:, 0:1024],
                                                  in_=lo_d[kg, :, rnd * 1024:(rnd + 1) * 1024])
                                i.then_inc(ctx.dma_sem(), 16)
                                return i
                            plan.dma("sync", f"cm.lwk{c0}_{rnd}_{kg}", ld_w, 2, tag=t_w)

                            def mm(eng, ctx, kg=kg, W=W, p=p):
                                last = None
                                mh = rkh[NOG_CM - 8 + kg]
                                ml = rkl[NOG_CM - 8 + kg]
                                for oo in range(8):
                                    sl = slice(oo * 128, (oo + 1) * 128)
                                    eng.matmul(ogbank[oo][:, 0:W], wsa[p][:, sl], mh[:, 0:W],
                                               start=(kg == 0), stop=False)
                                    eng.matmul(ogbank[oo][:, 0:W], wsa[p][:, sl], ml[:, 0:W],
                                               start=False, stop=False)
                                    last = eng.matmul(ogbank[oo][:, 0:W], wsb_[p][:, sl], mh[:, 0:W],
                                                      start=False, stop=(kg == NG - 1))
                                return last
                            plan.step("tensor", f"cm.mmk{c0}_{rnd}_{kg}", mm, need_dma=[t_w])

                        for oo in range(8):
                            og32 = rnd * 8 + oo
                            p = oo % 2
                            def rk2(eng, ctx, oo=oo, og32=og32, W=W, p=p):
                                return eng.activation(sc1[p][:, 0:W], ogbank[oo][:, 0:W], AF.Relu)
                            plan.step("scalar", f"cm.rk{c0}_{og32}", rk2)

                            def sq_split(eng, ctx, og32=og32, W=W, p=p):
                                ctx.seq(eng, eng.tensor_mul(sc1[p][:, 0:W], sc1[p][:, 0:W], sc1[p][:, 0:W]))
                                ctx.seq(eng, eng.tensor_copy(rkh[og32][:, 0:W], sc1[p][:, 0:W]))
                                return eng.tensor_tensor(rkl[og32][:, 0:W], sc1[p][:, 0:W],
                                                         rkh[og32][:, 0:W], ALU.subtract)
                            plan.step("vector", f"cm.rs{c0}_{og32}", sq_split)

                    for kg32 in range(NOG_CM):
                        p = kg32 % 2
                        t_w = tag()
                        def ld_w(eng, ctx, kg32=kg32, p=p):
                            hi_d, lo_d = spills["wcv"]
                            eng.dma_start(out=wsa[p][:, 0:D], in_=hi_d[kg32, :, :]).then_inc(ctx.dma_sem(), 16)
                            i = eng.dma_start(out=wsb_[p][:, 0:D], in_=lo_d[kg32, :, :])
                            i.then_inc(ctx.dma_sem(), 16)
                            return i
                        plan.dma("sync", f"cm.lwv{c0}_{kg32}", ld_w, 2, tag=t_w)

                        def mm(eng, ctx, kg32=kg32, W=W, p=p):
                            last = None
                            for og in range(NG):
                                sl = slice(og * 128, (og + 1) * 128)
                                eng.matmul(ogbank[og][:, 0:W], wsa[p][:, sl], rkh[kg32][:, 0:W],
                                           start=(kg32 == 0), stop=False)
                                eng.matmul(ogbank[og][:, 0:W], wsa[p][:, sl], rkl[kg32][:, 0:W],
                                           start=False, stop=False)
                                last = eng.matmul(ogbank[og][:, 0:W], wsb_[p][:, sl], rkh[kg32][:, 0:W],
                                                  start=False, stop=(kg32 == NOG_CM - 1))
                            return last
                        plan.step("tensor", f"cm.mmv{c0}_{kg32}", mm, need_dma=[t_w])

                    for og in range(NG):
                        t_x2 = tag()
                        def ldx2(eng, ctx, og=og, c0=c0, W=W):
                            i = eng.dma_start(out=x2one[og % 2][:, 0:W],
                                              in_=sA[og * 128:(og + 1) * 128, c0:c0 + W])
                            i.then_inc(ctx.dma_sem(), 16)
                            return i
                        plan.dma("sync", f"cm.ldx2{c0}_{og}", ldx2, 1, tag=t_x2)

                        def x5(eng, ctx, og=og, W=W):
                            ctx.seq(eng, eng.tensor_mul(sc1[og % 2][:, 0:W], ogbank[og][:, 0:W],
                                                        rrs[og][:, 0:W]))
                            ctx.seq(eng, eng.tensor_add(x3t[og][:, 0:W], sc1[og % 2][:, 0:W],
                                                        x3t[og][:, 0:W]))
                            return eng.tensor_add(x3t[og][:, 0:W], x3t[og][:, 0:W],
                                                  x2one[og % 2][:, 0:W])
                        plan.step("vector", f"cm.x5{c0}_{og}", x5, need_dma=[t_x2])

                        t_st = tag()
                        def st5(eng, ctx, og=og, c0=c0, W=W):
                            i = eng.dma_start(out=sA[og * 128:(og + 1) * 128, c0:c0 + W],
                                              in_=x3t[og][:, 0:W])
                            i.then_inc(ctx.dma_sem(), 16)
                            return i
                        plan.dma("gpsimd", f"cm.st5{c0}_{og}", st5, 1, tag=t_st)
                        prev_store[0] = t_st

        if "cm" in phases:
            cm_phase()

        if "ffn2" in phases:
            ffn_phase("f2", FFN2_TILES, sA, None, out_coloff=LB)

        if dump is not None:
            src, cols = dump
            srcmap = {"sA": sA, "sB": sB}[src]
            with contextlib.ExitStack() as S:
                dsb = sb(S, "dumpb", [128, 512])
                prevs = [plan.last_store]
                for g in range(NG):
                    for c0 in range(cols[0], cols[1], 512):
                        cw = min(512, cols[1] - c0)
                        t_l = tag()
                        def dl(eng, ctx, g=g, c0=c0, cw=cw):
                            i = eng.dma_start(out=dsb[:, 0:cw],
                                              in_=srcmap[g * 128:(g + 1) * 128, c0:c0 + cw])
                            i.then_inc(ctx.dma_sem(), 16)
                            return i
                        plan.dma("sync", f"dmp_l{g}_{c0}", dl, 1, tag=t_l,
                                 need_dma=[prevs[0]] if prevs[0] else [])
                        t_s = tag()
                        def ds(eng, ctx, g=g, c0=c0, cw=cw):
                            i = eng.dma_start(out=out[g * 128:(g + 1) * 128,
                                                      c0 - cols[0]:c0 - cols[0] + cw],
                                              in_=dsb[:, 0:cw])
                            i.then_inc(ctx.dma_sem(), 16)
                            return i
                        plan.dma("gpsimd", f"dmp_s{g}_{c0}", ds, 1, tag=t_s,
                                 need_dma=[t_l])
                        prevs[0] = t_s

        def fin(eng, ctx):
            for e in N_DSEM:
                for k in range(N_DSEM[e]):
                    tot = plan.dma_slot_count[e][k] * 16
                    if tot:
                        eng.wait_ge(ctx.sems[f"d_{e}_{k}"], tot)
            return eng.memset(ctx.buf["lam"][0:1, 0:1], 0.0)
        plan.step("vector", "fin", fin)

        emit_program(nc, plan, ctx, deps=deps, capture=capture)
    return nc, plan, ctx


def build_auto(phases=("wprep", "ffn1", "attn", "cm", "ffn2"), dump=None):
    cap = {}
    _nc1, plan1, ctx1 = build_full(phases, dump, capture=cap)
    deps = compute_deps(plan1, cap, ctx1.addr_of)
    nc, _p, _c = build_full(phases, dump, deps=deps)
    return nc



def host_inputs(inputs, core):
    x = np.asarray(inputs["L0_attn_x"]).reshape(16384, D)
    t0 = core * TOWN
    lo = t0 - LB
    if lo < 0:
        sl = np.zeros((TPRE, D), np.float32)
        sl[LB:] = x[0:TOWN]
    else:
        sl = x[lo:t0 + TOWN]
    xfm = np.ascontiguousarray(sl.T)

    km = np.zeros((128, TPRE), np.float32)
    if core == 0:
        km[:, 0:LB] = NEG

    idn, e64c, stepb = host_consts()
    fm = lambda v: np.ascontiguousarray(np.asarray(v).reshape(NG, 128).T)
    m = {
        "i_x": xfm, "i_kmask": km,
        "i_wk": np.asarray(inputs["L0_attn_Wk"]),
        "i_wv": np.asarray(inputs["L0_attn_Wv"]),
        "i_wr": np.asarray(inputs["L0_attn_Wr"]),
        "i_wo": np.asarray(inputs["L0_attn_Wo"]),
        "i_wck": np.asarray(inputs["L0_attn_Wck"]),
        "i_wcv": np.asarray(inputs["L0_attn_Wcv"]),
        "i_wcr": np.asarray(inputs["L0_attn_Wcr"]),
        "i_wg_t": np.ascontiguousarray(np.asarray(inputs["L0_ffn_w_gate"]).T),
        "i_wu_t": np.ascontiguousarray(np.asarray(inputs["L0_ffn_w_up"]).T),
        "i_wd_t": np.ascontiguousarray(np.asarray(inputs["L0_ffn_w_down"]).T),
        "i_ln0g": fm(inputs["L0_attn_ln0_g"]), "i_ln0b": fm(inputs["L0_attn_ln0_b"]),
        "i_ln1g": fm(inputs["L0_attn_ln1_g"]), "i_ln1b": fm(inputs["L0_attn_ln1_b"]),
        "i_tmk": fm(inputs["L0_attn_tm_k"]), "i_tmv": fm(inputs["L0_attn_tm_v"]),
        "i_tmr": fm(inputs["L0_attn_tm_r"]),
        "i_cmk": fm(inputs["L0_attn_cm_k"]), "i_cmr": fm(inputs["L0_attn_cm_r"]),
        "i_td": fm(inputs["L0_attn_time_decay"]), "i_tf": fm(inputs["L0_attn_time_first"]),
        "i_idn": idn, "i_e64": e64c, "i_stepb": stepb,
        "i_onescol": np.ones((128, 1), np.float32),
        "i_browones": np.ones((1, 128), np.float32),
        "i_epsc": np.full((128, 1), 1e-5, np.float32),
        "i_h0m": np.full((128, 1), 0.0 if core == 0 else 1.0, np.float32),
    }
    return m


def kernel(**inputs):
    nc = build_auto()
    in_maps = [host_inputs(inputs, c) for c in range(NCORES)]
    res = run_bass_kernel_spmd(nc, in_maps, core_ids=list(range(NCORES)))
    out = np.empty((16384, D), np.float32)
    for c in range(NCORES):
        out[c * TOWN:(c + 1) * TOWN] = res.results[c]["out"].T
    return out.reshape(1, 16384, D)



if __name__ == "__main__":
    _selftest_quant()
